# version 06
# speedup vs baseline: 1.3108x; 1.0029x over previous
import contextlib
import numpy as np
import ml_dtypes
import concourse.bass as bass
import concourse.mybir as mybir
from concourse.bass_utils import run_bass_kernel_spmd

F32 = mybir.dt.float32
BF16 = mybir.dt.bfloat16
AF = mybir.ActivationFunctionType
ALU = mybir.AluOpType
AX = mybir.AxisListType

NSEQ = 4
S = 2048
D = 1024
NH = 16
HD = 64
NLAT = 64
CHUNK = 32
LHEADS = 4
IFF = 2816
NEXP = 16
MIDR = 256
MIDS = 512
RNE_C = 12582912.0

ENGS = ("t", "v", "s", "g", "y")


def host_consts():
    bf = ml_dtypes.bfloat16
    c = {}
    c["c_ident_bf"] = np.eye(128, dtype=bf)
    c["c_ident_f32"] = np.eye(128, dtype=np.float32)
    c["c_ones_row_bf"] = np.ones((1, 128), dtype=bf)
    c["c_ones_row_f32"] = np.ones((1, 128), dtype=np.float32)
    c["c_ones_col_bf"] = np.ones((128, 1), dtype=bf)
    hm = np.zeros((128, 2), dtype=bf)
    hm[0:64, 0] = 1
    hm[64:128, 1] = 1
    c["c_headmask"] = hm
    s2 = np.zeros((2, 128), dtype=np.float32)
    s2[0, 0:64] = 1
    s2[1, 64:128] = 1
    c["c_sel2"] = s2
    s16 = np.zeros((16, 16 * 128), dtype=bf)
    for e in range(16):
        s16[e, e * 128:(e + 1) * 128] = 1
    c["c_sel16"] = s16
    i_idx = np.arange(128)
    c["c_mask_k_lt_q"] = (i_idx[:, None] < i_idx[None, :]).astype(bf)
    c["c_mask_k_ge_q"] = (i_idx[:, None] >= i_idx[None, :]).astype(bf)
    c["c_mask_k_le_q"] = (i_idx[:, None] <= i_idx[None, :]).astype(bf)
    t_idx = np.arange(S)
    c["c_mask_lat"] = (np.arange(NLAT)[:, None] < (t_idx // CHUNK)[None, :]).astype(bf)
    return c


class RSem:
    __slots__ = ("name", "base", "q", "cur", "handle", "rec")

    def __init__(self, rec, name):
        self.rec = rec
        self.name = name
        self.base = 0
        self.q = 0
        self.cur = 0
        self.handle = None

    def _bump(self, n):
        if self.rec.seg == "prep":
            self.base += n
        else:
            self.cur += n

    def now(self, dit=0):
        if self.rec.seg == "prep":
            return ("abs", self, self.base)
        if self.rec.seg == "epi":
            return ("epi", self, self.cur)
        return ("lin", self, dit, self.cur)


class Rec:
    def __init__(self, nc, nseq=NSEQ):
        self.nc = nc
        self.nseq = nseq
        self.seg = "prep"
        self.cmds = {seg: {e: [] for e in ENGS} for seg in ("prep", "body", "epi")}
        self.sems = []
        self.ctx = contextlib.ExitStack()

    def sem(self, name):
        s = RSem(self, name)
        self.sems.append(s)
        return s

    def begin_body(self):
        self.seg = "body"
        for s in self.sems:
            s.cur = 0

    def end_body(self):
        for s in self.sems:
            s.q = s.cur
            s.cur = 0
        self.seg = "epi"

    def _resolve(self, spec, i):
        kind = spec[0]
        if kind == "abs":
            return spec[2]
        if kind == "epi":
            _, sem, const = spec
            return sem.base + self.nseq * sem.q + const
        _, sem, dit, const = spec
        flat = sem.base + const + dit * sem.q
        if sem.q == 0:
            return flat
        assert flat >= 0, (sem.name, dit, const, sem.base, sem.q)
        return i * sem.q + flat

    def on(self, e, fn, waits=(), incs=()):
        seg = self.seg
        wspecs = list(waits)
        ispecs = [(s, n) for (s, n) in incs]
        for s, n in ispecs:
            s._bump(n)
        rec = self

        def cmd(eng, i, regctx, fn=fn, wspecs=wspecs, ispecs=ispecs):
            for spec in wspecs:
                if spec[0] == "lin" and spec[1].q != 0 and regctx is not None:
                    sem, dit, const = spec[1], spec[2], spec[3]
                    flat = sem.base + const + dit * sem.q
                    assert flat >= 0, (sem.name, dit, const, sem.base, sem.q)
                    regctx.wait(eng, sem, flat)
                else:
                    eng.wait_ge(spec[1].handle, rec._resolve(spec, i))
            ins = fn(eng, i) if fn is not None else None
            if ispecs:
                if ins is None:
                    ins = eng.nop()
                ins = ins.then_inc(ispecs[0][0].handle, ispecs[0][1])
                for s, n in ispecs[1:]:
                    eng.nop().then_inc(s.handle, n)
            return ins

        self.cmds[seg][e].append(cmd)

    def build(self, block):
        for s in self.sems:
            s.handle = self.ctx.enter_context(self.nc.semaphore(s.name))
        for e in ENGS:
            if not (self.cmds["prep"][e] or self.cmds["body"][e] or self.cmds["epi"][e]):
                continue
            prep = self.cmds["prep"][e]
            body = self.cmds["body"][e]
            epi = self.cmds["epi"][e]
            dec = getattr(block, {"t": "tensor", "v": "vector", "s": "scalar",
                                  "g": "gpsimd", "y": "sync"}[e])

            def run(eng, prep=prep, body=body, epi=epi):
                for cmd in prep:
                    cmd(eng, 0, None)
                if body:
                    with eng.Fori(0, self.nseq) as i:
                        regctx = _RegCtx(eng, i)
                        for cmd in body:
                            cmd(eng, i, regctx)
                for cmd in epi:
                    cmd(eng, 0, None)

            dec(run)


class _RegCtx:

    def __init__(self, eng, i):
        self.eng = eng
        self.i = i
        self.ireg = eng.to_reg(i)
        self.scratch = eng.alloc_register("waitval")
        self.last = None

    def wait(self, eng, sem, flat):
        key = (id(sem), sem.q, flat)
        if self.last != key:
            eng.reg_mul(self.scratch, self.ireg, sem.q)
            eng.reg_add(self.scratch, self.scratch, flat)
            self.last = key
        eng.wait_ge(sem.handle, self.scratch)


def emit_phase_l1attn(rc, B, xsrc_fn, xdst_fn):
    sV, sS, sT, sG = B["sV"], B["sS"], B["sT"], B["sG"]
    banks = B["banks"]
    pfE, pfF = banks.f32[4], banks.f32[5]
    xnf = B["xnf"]
    kbuf = B["kbuf"]
    vt = B["vt"]
    qgb = B["qgb"]
    ctxb = B["ctxb"]
    slab = B["slab"]
    wsrc = B["wp_l1qkvo"]

    rc.on("v", lambda v, i: v.memset(vt[:, :, :, 64:65], 1.0), incs=[(sV, 1)])

    for t in range(16):
        emit_ln_tile(rc, B, xsrc_fn, t, t % 2, B["xnt"][:, t % 2, :])
        emit_tile_to_feature(rc, B, B["xnt"][:, t % 2, :],
                             xnf[:, 0:8, t * 128:(t + 1) * 128], t % 2)
        B[("xldfree", t % 2)] = B[("xldfree", t % 2)] + [sS.now()]
        B[("lnrdy_l1", t)] = sS.now()

    wnames = ["L1_attn_Wq", "L1_attn_Wk", "L1_attn_Wv", "L1_attn_Wo"]

    def slab_load(slot, widx, waitspecs):
        rc.on("g", lambda g, i, slot=slot, widx=widx: g.dma_start(
            out=slab[:, slot, :, :],
            in_=B["dr"][wnames[widx]].rearrange("(c p) m -> p c m", p=128)),
            waits=waitspecs, incs=[(B["dSL"][slot], 16)])

    slab_load(0, 1, [])
    slab_load(1, 2, [])
    wk_loaded = B["dSL0"].now()

    wv_loaded = B["dSL1"].now()
    for tt in range(16):
        for og in range(2):
            bank = banks.f32[(tt * 2 + og) % 2]
            rc.on("t", lambda t_, i, og=og, bank=bank: t_.matmul(
                out=bank[:], lhsT=B["ones_row_bf"][0:1, :],
                rhs=B["bias_row"][0:1, BROW["l1_bv"] + og * 512:BROW["l1_bv"] + og * 512 + 512],
                start=True, stop=False),
                waits=[wv_loaded, B[("lnrdy_l1", tt)]] + banks.bwait(bank), incs=[])
            for kc in range(8):
                rc.on("t", lambda t_, i, tt=tt, og=og, kc=kc, bank=bank: t_.matmul(
                    out=bank[:], lhsT=xnf[:, kc, tt * 128:(tt + 1) * 128],
                    rhs=slab[:, 1, kc, og * 512:(og + 1) * 512],
                    start=False, stop=(kc == 7)),
                    incs=([(sT, 1)] if kc == 7 else []))
            rc.on("s", lambda s, i, tt=tt, og=og, bank=bank: s.copy(
                out=vt[:, tt, og * 8:(og + 1) * 8, 0:64],
                in_=bank[:].rearrange("p (h d) -> p h d", d=64)),
                waits=[sT.now()], incs=[(sS, 1)])
            banks.bfree(bank, [sS.now()])
    v_done = sS.now()

    for oc in range(8):
        for kc in range(8):
            for tg in range(4):
                bank = banks.f32[tg]
                rc.on("t", lambda t_, i, oc=oc, tg=tg, kc=kc, bank=bank: t_.matmul(
                    out=bank[:], lhsT=slab[:, 0, kc, oc * 128:(oc + 1) * 128],
                    rhs=xnf[:, kc, tg * 512:(tg + 1) * 512],
                    start=(kc == 0), stop=(kc == 7)),
                    waits=([wk_loaded, sS.now()] + banks.bwait(bank)
                           if kc == 0 else []),
                    incs=([(sT, 1)] if kc == 7 and tg == 3 else []))
        for tg in range(4):
            bank = banks.f32[tg]
            rc.on("v", lambda v, i, oc=oc, tg=tg, bank=bank: v.tensor_scalar(
                out=kbuf[:, oc, tg * 512:(tg + 1) * 512], in0=bank[:],
                scalar1=B["bias_pp"][:, BPP["l1_bk"] + oc:BPP["l1_bk"] + oc + 1],
                scalar2=None, op0=ALU.add),
                waits=([sT.now()] if tg == 0 else []), incs=[(sV, 1)])
            banks.bfree(bank, [sV.now()])
    kq_done = sV.now()

    for oc in range(8):
        rc.on("s", lambda s, i, oc=oc: s.activation(
            out=B["sqscr"][:], in_=kbuf[:, oc, :], func=AF.Square),
            waits=[kq_done, sT.now(), sS.now()], incs=[(sS, 1)])
        for tg in range(4):
            bank = banks.f32[2 + (oc * 4 + tg) % 2]
            rc.on("t", lambda t_, i, oc=oc, tg=tg, bank=bank: t_.matmul(
                out=bank[0:2, :], lhsT=B["headmask"][:],
                rhs=B["sqscr"][:, tg * 512:(tg + 1) * 512], start=True, stop=True),
                waits=[sS.now()] + banks.bwait(bank), incs=[(sT, 1)])
            rc.on("v", lambda v, i, oc=oc, tg=tg, bank=bank: v.tensor_reduce(
                out=B["nrm"][:, oc * 4 + tg:oc * 4 + tg + 1], in_=bank[0:2, :],
                axis=AX.X, op=ALU.max), waits=[sT.now()], incs=[(sV, 1)])
            banks.bfree(bank, [sV.now()])
    rc.on("v", lambda v, i: v.tensor_reduce(
        out=B["kn2a"][0:2, :], in_=B["nrm"][:, 0:32], axis=AX.X, op=ALU.max),
        waits=[sV.now()], incs=[(sV, 1)])
    rc.on("t", lambda t_, i: t_.transpose(
        out=pfF[0:1, 0:2], in_=B["kn2a"][0:2, 0:1], identity=B["ident_f32"][0:2, 0:2]),
        waits=[sV.now()] + banks.bwait(pfF), incs=[(sT, 1)])
    rc.on("v", lambda v, i: v.tensor_reduce(
        out=B["kn2"][0:1, :], in_=pfF[0:1, 0:2], axis=AX.X, op=ALU.max),
        waits=[sT.now()], incs=[(sV, 1)])
    banks.bfree(pfF, [sV.now()])

    slab_load(0, 0, [kq_done, sT.now()])
    wq_loaded = B["dSL0"].now()
    slab_load(1, 3, [v_done, sT.now()])
    wo_loaded = B["dSL1"].now()

    for qg in range(4):
        for oc in range(8):
            bank = banks.f32[oc % 2]
            for kc in range(8):
                rc.on("t", lambda t_, i, oc=oc, kc=kc, qg=qg, bank=bank: t_.matmul(
                    out=bank[:], lhsT=slab[:, 0, kc, oc * 128:(oc + 1) * 128],
                    rhs=xnf[:, kc, qg * 512:(qg + 1) * 512],
                    start=(kc == 0), stop=(kc == 7)),
                    waits=([wq_loaded, sV.now()] + banks.bwait(bank) if kc == 0 else []),
                    incs=([(sT, 1)] if kc == 7 else []))
            rc.on("v", lambda v, i, oc=oc, bank=bank: v.tensor_scalar(
                out=qgb[:, oc, :], in0=bank[:],
                scalar1=B["bias_pp"][:, BPP["l1_bq"] + oc:BPP["l1_bq"] + oc + 1],
                scalar2=None, op0=ALU.add), waits=[sT.now()], incs=[(sV, 1)])
            banks.bfree(bank, [sV.now()])
        q_done = sV.now()
        for oc in range(8):
            rc.on("s", lambda s, i, oc=oc: s.activation(
                out=B["sqscr"][:, 0:512], in_=qgb[:, oc, :], func=AF.Square),
                waits=[q_done, sT.now(), sS.now()], incs=[(sS, 1)])
            bank = banks.f32[2 + oc % 2]
            rc.on("t", lambda t_, i, oc=oc, bank=bank: t_.matmul(
                out=bank[0:2, :], lhsT=B["headmask"][:], rhs=B["sqscr"][:, 0:512],
                start=True, stop=True), waits=[sS.now()] + banks.bwait(bank),
                incs=[(sT, 1)])
            rc.on("v", lambda v, i, oc=oc, bank=bank: v.tensor_reduce(
                out=B["nrm"][:, 32 + oc:33 + oc], in_=bank[0:2, :], axis=AX.X,
                op=ALU.max), waits=[sT.now()], incs=[(sV, 1)])
            banks.bfree(bank, [sV.now()])
        rc.on("v", lambda v, i: v.tensor_reduce(
            out=B["kn2a"][0:2, :], in_=B["nrm"][:, 32:40], axis=AX.X, op=ALU.max),
            waits=[sV.now()], incs=[(sV, 1)])
        rc.on("t", lambda t_, i: t_.transpose(
            out=pfF[0:1, 2:4], in_=B["kn2a"][0:2, 0:1], identity=B["ident_f32"][0:2, 0:2]),
            waits=[sV.now()] + banks.bwait(pfF), incs=[(sT, 1)])
        rc.on("v", lambda v, i: v.tensor_reduce(
            out=B["qn2"][0:1, :], in_=pfF[0:1, 2:4], axis=AX.X, op=ALU.max),
            waits=[sT.now()], incs=[(sV, 1)])
        banks.bfree(pfF, [sV.now()])
        rc.on("v", lambda v, i: v.tensor_tensor(
            out=B["pp1"][0:1, :], in0=B["qn2"][0:1, :], in1=B["kn2"][0:1, :],
            op=ALU.mult), waits=[sV.now()], incs=[(sV, 1)])
        rc.on("s", lambda s, i: s.activation(
            out=B["pp1"][0:1, :], in_=B["pp1"][0:1, :], func=AF.Ln,
            bias=B["tiny"][0:1, :]), waits=[sV.now()], incs=[(sS, 1)])
        rc.on("s", lambda s, i: s.activation(
            out=B["pp2"][0:1, :], in_=B["pp1"][0:1, :], func=AF.Exp, scale=0.5),
            waits=[sS.now()], incs=[(sS, 1)])
        rc.on("v", lambda v, i: v.tensor_scalar(
            out=B["pp2f"][0:1, :], in0=B["pp2"][0:1, :], scalar1=-0.125,
            scalar2=None, op0=ALU.mult), waits=[sS.now()], incs=[(sV, 1)])
        rc.on("t", lambda t_, i: t_.matmul(
            out=pfF[:, 0:1], lhsT=B["ones_row_f32"][0:1, :], rhs=B["pp2f"][0:1, 0:1],
            start=True, stop=True),
            waits=[sV.now()] + banks.bwait(pfF), incs=[(sT, 1)])
        rc.on("s", lambda s, i: s.copy(out=B["negM"][:], in_=pfF[:, 0:1]),
            waits=[sT.now()], incs=[(sS, 1)])
        banks.bfree(pfF, [sS.now()])
        negm_ready = sS.now()

        nkt = 4 * qg + 4
        norm_q = []

        def l1_norm_tail(item):
            h2, c2, r2, ctxbank2 = item
            rc.on("t", lambda t_, i: t_.matmul(
                out=pfF[0:64, :], lhsT=B["ones_row_f32"][0:1, 0:64],
                rhs=B["linv"][0:1, :], start=True, stop=True),
                waits=[sV.now()] + banks.bwait(pfF), incs=[(sT, 1)])
            rc.on("s", lambda s, i: s.copy(out=B["bcs"][0:64, :], in_=pfF[0:64, :]),
                waits=[sT.now(), sS.now()], incs=[(sS, 1)])
            banks.bfree(pfF, [sS.now()])
            rc.on("v", lambda v, i, c=c2, r0=r2, ctxbank=ctxbank2: v.tensor_tensor(
                out=ctxb[r0:r0 + 64, c, :], in0=ctxbank[0:64, 0:512],
                in1=B["bcs"][0:64, :], op=ALU.mult),
                waits=[sS.now()], incs=[(sV, 1)])
            banks.bfree(ctxbank2, [sV.now()])

        for h in range(16):
            c = h // 2
            r0 = (h % 2) * 64
            ctxbank = banks.f32[2 + h % 2]
            pv_q = []
            for kt in range(nkt):
                joff = max(0, kt * 128 - qg * 512)
                sbank = banks.f32[kt % 2]
                rc.on("t", lambda t_, i, c=c, r0=r0, kt=kt, joff=joff, qg=qg,
                      sbank=sbank: t_.matmul(
                    out=sbank[:, joff:512],
                    lhsT=kbuf[r0:r0 + 64, c, kt * 128:(kt + 1) * 128],
                    rhs=qgb[r0:r0 + 64, c, joff:512], start=True, stop=True),
                    waits=[negm_ready] + banks.bwait(sbank), incs=[(sT, 1)])
                rc.on("s", lambda s, i, kt=kt, joff=joff, sbank=sbank: s.activation(
                    out=B["ptile"][:, kt % 2, joff:512], in_=sbank[:, joff:512],
                    func=AF.Exp, bias=B["negM"][:], scale=0.125),
                    waits=[sT.now(), sS.now()], incs=[(sS, 1)])
                banks.bfree(sbank, [sS.now()])
                if kt >= 4 * qg:
                    rc.on("v", lambda v, i, kt=kt, joff=joff: v.tensor_tensor(
                        out=B["ptile"][:, kt % 2, joff:joff + 128],
                        in0=B["ptile"][:, kt % 2, joff:joff + 128],
                        in1=B["mask_le"][:], op=ALU.mult),
                        waits=[sS.now()], incs=[(sV, 1)])
                    pw = sV.now()
                else:
                    pw = sS.now()
                pv_q.append((kt, joff, pw))
                if kt == 1 and norm_q:
                    l1_norm_tail(norm_q.pop(0))
                if len(pv_q) > 1:
                    k2, j2, pw2 = pv_q.pop(0)
                    rc.on("t", lambda t_, i, h=h, kt=k2, joff=j2, nkt=nkt,
                          ctxbank=ctxbank: t_.matmul(
                        out=ctxbank[0:65, joff:512], lhsT=vt[:, kt, h, 0:65],
                        rhs=B["ptile"][:, kt % 2, joff:512],
                        start=(kt == 0), stop=(kt == nkt - 1)),
                        waits=[pw2] + (banks.bwait(ctxbank) if k2 == 0 else []),
                        incs=[(sT, 1)])
            k2, j2, pw2 = pv_q.pop(0)
            rc.on("t", lambda t_, i, h=h, kt=k2, joff=j2, nkt=nkt,
                  ctxbank=ctxbank: t_.matmul(
                out=ctxbank[0:65, joff:512], lhsT=vt[:, kt, h, 0:65],
                rhs=B["ptile"][:, kt % 2, joff:512],
                start=(kt == 0), stop=(kt == nkt - 1)),
                waits=[pw2] + (banks.bwait(ctxbank) if k2 == 0 else []),
                incs=[(sT, 1)])
            rc.on("v", lambda v, i, ctxbank=ctxbank: v.tensor_scalar(
                out=B["lrow"][0:1, :], in0=ctxbank[64:65, 0:512], scalar1=1e-30,
                scalar2=None, op0=ALU.max), waits=[sT.now(), sV.now()], incs=[(sV, 1)])
            rc.on("v", lambda v, i: v.reciprocal(
                out=B["linv"][0:1, :], in_=B["lrow"][0:1, :]),
                waits=[sV.now()], incs=[(sV, 1)])
            norm_q.append((h, c, r0, ctxbank))
        l1_norm_tail(norm_q.pop(0))

        ctx_done = sV.now()
        for tl in range(4):
            tt = qg * 4 + tl
            xslot = tl % 2
            rc.on("y", lambda y, i, tt=tt, xslot=xslot: y.dma_start(
                out=B["xld2"][:, xslot, :], in_=xsrc_fn(i)[tt * 128:(tt + 1) * 128, :]),
                waits=B.pop(("xld2free", xslot), []), incs=[(B["dX2"][xslot], 16)])
            for og in range(2):
                bank = banks.f32[og % 2]
                rc.on("t", lambda t_, i, og=og, bank=bank: t_.matmul(
                    out=bank[:], lhsT=B["ones_row_bf"][0:1, :],
                    rhs=B["bias_row"][0:1, BROW["l1_bo"] + og * 512:BROW["l1_bo"] + og * 512 + 512],
                    start=True, stop=False),
                    waits=[ctx_done, wo_loaded] + banks.bwait(bank), incs=[])
                for kc in range(8):
                    rc.on("t", lambda t_, i, tl=tl, og=og, kc=kc, bank=bank: t_.matmul(
                        out=bank[:], lhsT=ctxb[:, kc, tl * 128:(tl + 1) * 128],
                        rhs=slab[:, 1, kc, og * 512:(og + 1) * 512],
                        start=False, stop=(kc == 7)),
                        incs=([(sT, 1)] if kc == 7 else []))
                rc.on("v", lambda v, i, og=og, xslot=xslot, bank=bank: v.tensor_tensor(
                    out=B["yst"][:, xslot, og * 512:(og + 1) * 512], in0=bank[:],
                    in1=B["xld2"][:, xslot, og * 512:(og + 1) * 512], op=ALU.add),
                    waits=[sT.now(), B["dX2"][xslot].now()] + B.pop(("ystfree", xslot), []),
                    incs=[(sV, 1)])
                banks.bfree(bank, [sV.now()])
            rc.on("y", lambda y, i, tt=tt, xslot=xslot: y.dma_start(
                out=xdst_fn(i)[tt * 128:(tt + 1) * 128, :], in_=B["yst"][:, xslot, :]),
                waits=[sV.now()], incs=[(B["dST"][xslot], 16)])
            B[("xld2free", xslot)] = [sV.now()]
            B[("ystfree", xslot)] = [B["dST"][xslot].now()]


class Arena:

    def __init__(self, nc, ctx, cols_f32):
        self.t = ctx.enter_context(nc.sbuf_tensor("arena", [128, cols_f32], F32))
        self.cols = cols_f32
        self.cur = 0
        self.mark = 0
        self.hwm = 0

    def persist_done(self):
        self.mark = self.cur

    def phase_reset(self):
        self.cur = self.mark

    @property
    def cur_bf(self):
        return self.cur

    @cur_bf.setter
    def cur_bf(self, v):
        self.cur = v

    @property
    def cur_f32(self):
        return self.cur

    @cur_f32.setter
    def cur_f32(self, v):
        pass

    def view(self, shape, dt=BF16, parts=128):
        n = int(np.prod(shape))
        nf32 = (n + 1) // 2 if dt == BF16 else n
        off = self.cur
        self.cur = off + nf32
        self.hwm = max(self.hwm, self.cur)
        assert self.cur <= self.cols, f"arena overflow {self.cur} > {self.cols}"
        ap = self.t[0:parts, off:off + nf32]
        if dt == BF16:
            ap = ap.bitcast(BF16)
            if n % 2:
                ap = ap[:, 0:n]
        if len(shape) > 1:
            letters = "abcdef"[: len(shape)]
            kw = {letters[j]: shape[j] for j in range(1, len(shape))}
            ap = ap.rearrange("p (" + " ".join(letters) + ") -> p " + " ".join(letters), **kw)
        return ap


class PBanks:

    def __init__(self, nc, ctx, rc):
        self.f32 = [ctx.enter_context(nc.psum_tensor(f"pb{j}", [128, 512], F32))
                    for j in range(6)]
        self.bf = [ctx.enter_context(nc.psum_tensor(f"pt{j}", [128, 1024], BF16))
                   for j in range(2)]
        self.guard = {}

    def bwait(self, bank):
        return list(self.guard.get(id(bank), []))

    def bfree(self, bank, specs):
        self.guard[id(bank)] = list(specs)


def _bias_pp_host(I):
    cols = []
    def chunks(v):
        return [v[c * 128:(c + 1) * 128] for c in range(len(v) // 128)]
    cols += chunks(I["L1_attn_bq"])
    cols += chunks(I["L1_attn_bk"])
    cols += chunks(I["L0_attn_qkv_b"][0:1024])
    cols += chunks(I["L0_attn_qkv_b"][1024:2048])
    cols += chunks(I["L0_attn_rq_b"])
    cols += chunks(I["L0_attn_rkv_b"][0:256])
    cols += chunks(I["L0_attn_rout_b"])
    return np.stack(cols, axis=1).astype(np.float32)

BPP = dict(l1_bq=0, l1_bk=8, l0_q=16, l0_k=24, l0_rq=32, l0_rk=34, l0_rout=36)

def _bias_row_host(I):
    v = np.concatenate([
        I["L1_attn_bv"],
        I["L1_attn_bo"],
        I["L0_attn_qkv_b"][2048:3072],
        I["L0_attn_out_b"],
        I["L0_attn_rkv_b"][256:512],
    ]).astype(np.float32)
    return v[None, :]

BROW = dict(l1_bv=0, l1_bo=1024, l0_v=2048, l0_out=3072, l0_rv=4096)
NROW = 4352

PHASE_IO = {
    "l0attn": ("x", "xres"),
    "l0ffn": ("xres", "xres"),
    "l1attn": ("xres", "xres"),
    "l1ffn": ("xres", "y"),
}


def build_program(phases, dev_io=False, nseq=NSEQ):
    nc = bass.Bass("TRN2", target_bir_lowering=False, debug=False)
    dr = {}
    dr["x"] = nc.dram_tensor("x", [NSEQ, S, D], F32, kind="ExternalInput").ap()
    dr["y"] = nc.dram_tensor("y", [NSEQ, S, D], F32, kind="ExternalOutput").ap()
    dr["xres"] = nc.dram_tensor("xres", [NSEQ, S, D], F32, kind="Internal").ap()
    wnames = [
        ("L0_attn_qkv_w", [D, 3 * D]), ("L0_attn_qkv_b", [3 * D]),
        ("L0_attn_rq_w", [D, 256]), ("L0_attn_rq_b", [256]),
        ("L0_attn_rkv_w", [D, 512]), ("L0_attn_rkv_b", [512]),
        ("L0_attn_rout_w", [256, D]), ("L0_attn_rout_b", [D]),
        ("L0_attn_out_w", [D, D]), ("L0_attn_out_b", [D]),
        ("L0_attn_gl_w", [64, 1]), ("L0_attn_gr_w", [64, 1]),
        ("L0_ffn_w_gate", [IFF, D]), ("L0_ffn_w_up", [IFF, D]),
        ("L0_ffn_w_down", [D, IFF]),
        ("L1_attn_Wq", [D, D]), ("L1_attn_bq", [D]),
        ("L1_attn_Wk", [D, D]), ("L1_attn_bk", [D]),
        ("L1_attn_Wv", [D, D]), ("L1_attn_bv", [D]),
        ("L1_attn_Wo", [D, D]), ("L1_attn_bo", [D]),
        ("L1_ffn_router_w", [D, NEXP]),
        ("L1_ffn_wg_r", [NEXP, D, MIDR]), ("L1_ffn_wu_r", [NEXP, D, MIDR]),
        ("L1_ffn_wd_r", [NEXP, MIDR, D]),
        ("L1_ffn_wg_s", [2, D, MIDS]), ("L1_ffn_wu_s", [2, D, MIDS]),
        ("L1_ffn_wd_s", [2, MIDS, D]),
    ]
    for nm, sh in wnames:
        dr[nm] = nc.dram_tensor(nm, sh, F32, kind="ExternalInput").ap()
    cvals = host_consts()
    for nm, arr in cvals.items():
        dt = BF16 if arr.dtype == ml_dtypes.bfloat16 else F32
        dr[nm] = nc.dram_tensor(nm, list(arr.shape), dt, kind="ExternalInput").ap()
    dr["c_bias_pp"] = nc.dram_tensor("c_bias_pp", [128, 44], F32, kind="ExternalInput").ap()
    dr["c_bias_row"] = nc.dram_tensor("c_bias_row", [1, NROW], F32, kind="ExternalInput").ap()
    scr = {}
    scr["wp_l1qkvo"] = nc.dram_tensor("wp_l1qkvo", [4, D, D], BF16, kind="Internal").ap()
    scr["wp_l0qkv"] = nc.dram_tensor("wp_l0qkv", [D, 3 * D], BF16, kind="Internal").ap()
    scr["wp_l0out"] = nc.dram_tensor("wp_l0out", [D, D], BF16, kind="Internal").ap()
    scr["wp_l0rq"] = nc.dram_tensor("wp_l0rq", [D, 256], BF16, kind="Internal").ap()
    scr["wp_l0rkv"] = nc.dram_tensor("wp_l0rkv", [D, 512], BF16, kind="Internal").ap()
    scr["wp_l0rout"] = nc.dram_tensor("wp_l0rout", [256, D], BF16, kind="Internal").ap()
    scr["wp_l0g"] = nc.dram_tensor("wp_l0g", [D, IFF], BF16, kind="Internal").ap()
    scr["wp_l0u"] = nc.dram_tensor("wp_l0u", [D, IFF], BF16, kind="Internal").ap()
    scr["wp_l0d"] = nc.dram_tensor("wp_l0d", [IFF, D], BF16, kind="Internal").ap()
    scr["wp_l1rg"] = nc.dram_tensor("wp_l1rg", [NEXP, D, MIDR], BF16, kind="Internal").ap()
    scr["wp_l1ru"] = nc.dram_tensor("wp_l1ru", [NEXP, D, MIDR], BF16, kind="Internal").ap()
    scr["wp_l1rd"] = nc.dram_tensor("wp_l1rd", [NEXP, MIDR, D], BF16, kind="Internal").ap()
    scr["wp_l1sg"] = nc.dram_tensor("wp_l1sg", [2, D, MIDS], BF16, kind="Internal").ap()
    scr["wp_l1su"] = nc.dram_tensor("wp_l1su", [2, D, MIDS], BF16, kind="Internal").ap()
    scr["wp_l1sd"] = nc.dram_tensor("wp_l1sd", [2, MIDS, D], BF16, kind="Internal").ap()
    scr["wp_router"] = nc.dram_tensor("wp_router", [D, NEXP], BF16, kind="Internal").ap()

    rc = Rec(nc, nseq)
    with rc.ctx:
        arena = Arena(nc, rc.ctx, cols_f32=53180)
        banks = PBanks(nc, rc.ctx, rc)
        B = {"banks": banks, "dr": dr, "scr": scr, "arena": arena}
        for snm in ("sT", "sV", "sS", "sG", "sJ"):
            B[snm] = rc.sem(snm)
        for snm in ("dXL0", "dXL1", "dX20", "dX21", "dST0", "dST1",
                    "dSL0", "dSL1", "dCB0", "dCB1", "dCS0", "dCS1",
                    "dPS0", "dPS1", "dCN", "dCNG",
                    "dGS", "dGS1", "dUS", "dUS1", "dDS", "dES0", "dES1", "dUS20", "dUS21",
                    "dDS20", "dDS21", "dSG", "dSU", "dSDW", "dRT", "dGA", "dGB"):
            B[snm] = rc.sem(snm)
        B["dXL"] = [B["dXL0"], B["dXL1"]]
        B["dX2"] = [B["dX20"], B["dX21"]]
        B["dST"] = [B["dST0"], B["dST1"]]
        B["dSL"] = [B["dSL0"], B["dSL1"]]
        B["dCB"] = [B["dCB0"], B["dCB1"]]
        B["dCS"] = [B["dCS0"], B["dCS1"]]
        B["dPS"] = [B["dPS0"], B["dPS1"]]
        sY, sGD = B["dCN"], B["dCNG"]

        def pview(shape, dt=BF16, parts=128):
            return arena.view(shape, dt, parts)

        B["ident_bf"] = pview([128], BF16)
        B["ident_f32"] = pview([128], F32)
        B["ones_row_bf"] = pview([128], BF16, parts=1)
        B["ones_row_f32"] = pview([128], F32, parts=1)
        B["headmask"] = pview([2], BF16)
        B["sel2"] = pview([128], F32, parts=2)
        B["mask_lt"] = pview([128], BF16)
        B["mask_ge"] = pview([128], BF16)
        B["mask_le"] = pview([128], BF16)
        B["bias_pp"] = pview([44], F32)
        B["bias_row"] = pview([NROW], BF16, parts=1)
        B["gldiag"] = pview([2], BF16)
        B["grdiag"] = pview([2], BF16)
        B["eps"] = pview([1], F32)
        B["tiny"] = pview([1], F32, parts=1)
        for nm in ("lnms", "lnvs", "lnm", "lnmsq", "lnvar", "lnt", "lnr"):
            B[nm] = pview([2], F32)
        B["dump"] = pview([1024], BF16)
        B["xld"] = pview([2, 1024], F32)
        B["xld2"] = pview([2, 1024], F32)
        B["yst"] = pview([2, 1024], F32)
        B["nrm"] = pview([56], F32, parts=2)
        B["kn2a"] = pview([1], F32, parts=2)
        B["kn2"] = pview([1], F32, parts=1)
        B["qn2"] = pview([1], F32, parts=1)
        B["pp1"] = pview([1], F32, parts=1)
        B["pp2"] = pview([1], F32, parts=1)
        B["pp2f"] = pview([1], F32, parts=1)
        B["negM"] = pview([1], F32)
        B["negM2"] = pview([1], F32)
        for k, v in scr.items():
            B[k] = v
        for k in list(dr.keys()):
            B.setdefault(k, dr[k])
        arena.persist_done()

        def cload(dst_ap, key, cast=False):
            if cast:
                rc.on("g", lambda g, i, dst_ap=dst_ap, key=key:
                      g.dma_start(out=dst_ap, in_=dr[key]), incs=[(sGD, 16)])
            else:
                rc.on("y", lambda y, i, dst_ap=dst_ap, key=key:
                      y.dma_start(out=dst_ap, in_=dr[key]), incs=[(sY, 16)])

        cload(B["ident_bf"], "c_ident_bf")
        cload(B["ident_f32"], "c_ident_f32")
        cload(B["ones_row_bf"], "c_ones_row_bf")
        cload(B["ones_row_f32"], "c_ones_row_f32")
        cload(B["headmask"], "c_headmask")
        cload(B["sel2"], "c_sel2")
        cload(B["mask_lt"], "c_mask_k_lt_q")
        cload(B["mask_ge"], "c_mask_k_ge_q")
        cload(B["mask_le"], "c_mask_k_le_q")
        cload(B["bias_pp"], "c_bias_pp")
        cload(B["bias_row"], "c_bias_row", cast=True)
        rc.on("v", lambda v, i: v.memset(B["eps"], 1e-5), incs=[(B["sV"], 1)])
        rc.on("v", lambda v, i: v.memset(B["tiny"], 1e-20), incs=[(B["sV"], 1)])

        need = set(phases)
        alloc_phase_bufs(rc, B, need)
        jobs = []
        if "l1ffn" in need:
            for e in range(NEXP):
                jobs += [(dr["L1_ffn_wg_r"][e], scr["wp_l1rg"][e], D, MIDR),
                         (dr["L1_ffn_wu_r"][e], scr["wp_l1ru"][e], D, MIDR),
                         (dr["L1_ffn_wd_r"][e], scr["wp_l1rd"][e], MIDR, D)]
            for e in range(2):
                jobs += [(dr["L1_ffn_wg_s"][e], scr["wp_l1sg"][e], D, MIDS),
                         (dr["L1_ffn_wu_s"][e], scr["wp_l1su"][e], D, MIDS),
                         (dr["L1_ffn_wd_s"][e], scr["wp_l1sd"][e], MIDS, D)]
            jobs.append((dr["L1_ffn_router_w"], scr["wp_router"], D, NEXP))
        emit_prep_cast(rc, B, jobs)
        if "l0attn" in need:
            rc.on("v", lambda v, i: v.memset(B["gldiag"], 0.0), incs=[(B["sV"], 1)])
            rc.on("v", lambda v, i: v.memset(B["grdiag"], 0.0), incs=[(B["sV"], 1)])
            for (dst, key) in [(B["gldiag"], "L0_attn_gl_w"), (B["grdiag"], "L0_attn_gr_w")]:
                rc.on("g", lambda g, i, dst=dst, key=key: g.dma_start(
                    out=dst[0:64, 0:1], in_=dr[key]),
                    waits=[B["sV"].now()], incs=[(sGD, 16)])
                rc.on("g", lambda g, i, dst=dst, key=key: g.dma_start(
                    out=dst[64:128, 1:2], in_=dr[key]), incs=[(sGD, 16)])
        if "l0ffn" in need:
            emit_prep_quant_weights(rc, B)

        rc.on("y", None, waits=[B[s].now() for s in
              ("dCN", "dCS0", "dCS1", "dPS0", "dPS1", "dXL0", "dXL1")])
        rc.on("g", None, waits=[B["dCNG"].now()])
        phase_join(rc, B)
        rc.begin_body()
        for ph in phases:
            skey, dkey = PHASE_IO[ph]
            if dev_io:
                skey, dkey = "x", "y"
            xsrc_fn = (lambda i, t=dr[skey]: t[i])
            xdst_fn = (lambda i, t=dr[dkey]: t[i])
            if ph == "l1attn":
                emit_phase_l1attn(rc, B, xsrc_fn, xdst_fn)
            elif ph == "l1ffn":
                emit_phase_l1ffn(rc, B, xsrc_fn, xdst_fn)
            elif ph == "l0attn":
                emit_phase_l0attn(rc, B, xsrc_fn, xdst_fn)
            elif ph == "l0ffn":
                emit_phase_l0ffn(rc, B, xsrc_fn, xdst_fn)
            phase_join(rc, B)
        rc.end_body()
        for e in ENGS:
            rc.on(e, None, waits=[B["sJ"].now()])
        rc.on("y", None, waits=[B["dST0"].now(), B["dST1"].now(), B["dCN"].now()])
        block = rc.ctx.enter_context(nc.Block())
        rc.build(block)
    return nc


_DMA_SEMS_Y = ("dXL0", "dXL1", "dX20", "dX21", "dST0", "dST1", "dSL0", "dSL1",
               "dCS0", "dCS1", "dPS0", "dPS1", "dCN", "dGS", "dGS1", "dUS", "dUS1", "dDS",
               "dES0", "dES1", "dUS20", "dUS21", "dDS20", "dDS21",
               "dSG", "dSU", "dSDW", "dRT")


def phase_join(rc, B):
    sJ = B["sJ"]
    own = {"t": "sT", "v": "sV", "s": "sS", "g": "sG"}
    rc.on("y", None, waits=[B[s].now() for s in _DMA_SEMS_Y], incs=[(sJ, 1)])
    rc.on("g", None, waits=[B["dCNG"].now(), B["sG"].now()], incs=[(sJ, 1)])
    for e in ("t", "v", "s"):
        rc.on(e, None, waits=[B[own[e]].now()], incs=[(sJ, 1)])
    target = sJ.now()
    for e in ENGS:
        rc.on(e, None, waits=[target])


def alloc_phase_bufs(rc, B, need):
    arena = B["arena"]
    arena.phase_reset()
    B["xnf"] = arena.view([8, 2048], BF16)
    B["xnt"] = arena.view([2, 1024], BF16)
    mark_bf, mark_f = arena.cur_bf, arena.cur_f32
    peak = (mark_bf, mark_f)

    def reset():
        arena.cur_bf, arena.cur_f32 = mark_bf, mark_f

    def done(nm):
        nonlocal peak
        peak = (max(peak[0], arena.cur_bf), max(peak[1], arena.cur_f32))

    if "l1attn" in need or "l0attn" in need:
        reset()
        B["slab"] = arena.view([2, 8, 1024], BF16)
        B["lrow"] = arena.view([512], F32, parts=1)
        B["linv"] = arena.view([512], F32, parts=1)
        B["bcs"] = arena.view([512], F32, parts=64)
        B["sqscr"] = arena.view([2048], BF16)
        B["qgb"] = arena.view([8, 512], BF16)
        B["ptile"] = arena.view([2, 512], BF16)
        B["kbuf"] = arena.view([8, 2048], BF16)
        kmark = (arena.cur_bf, arena.cur_f32)
        B["vt"] = arena.view([16, 16, 65], BF16)
        B["ctxb"] = arena.view([8, 512], BF16)
        done("l1attn")
        if "l0attn" in need:
            arena.cur_bf, arena.cur_f32 = kmark
            B["vt6"] = arena.view([5, 16, 65], BF16)
            B["routslab"] = arena.view([2, 1024], BF16)
            B["mask_lat"] = arena.view([2048], BF16, parts=64)
            B["rqb"] = arena.view([2, 2048], BF16)
            B["localb"] = arena.view([8, 512], BF16)
            B["remoteb"] = arena.view([8, 512], BF16)
            B["pooledb"] = arena.view([8, 64], BF16)
            B["rkf"] = arena.view([2, 64], BF16)
            B["rvt"] = arena.view([4, 65], BF16, parts=64)
            B["lpt"] = arena.view([512], BF16, parts=64)
            B["gsig"] = arena.view([512], F32, parts=2)
            B["scr1"] = arena.view([512], BF16)
            B["latc"] = arena.view([2, 512], BF16)
            done("l0attn")
    if "l0ffn" in need:
        reset()
        for qn in ("q", "q2"):
            B[qn] = {
                "mx": arena.view([64], F32), "scl": arena.view([64], F32),
                "rcp": arena.view([64], F32), "abs": arena.view([1024], F32),
                "sgn": arena.view([1024], BF16), "axn": arena.view([1024], F32),
                "w": arena.view([1024], BF16), "tA": arena.view([1024], BF16),
                "tB": arena.view([1024], BF16),
            }
        B["xnq"] = arena.view([2, 1024], F32)
        B["xqt"] = arena.view([2, 1024], BF16)
        B["hq_t"] = arena.view([3, 512], BF16)
        fmark = (arena.cur_bf, arena.cur_f32)
        B["hqf"] = arena.view([22, 512], BF16)
        B["gslab"] = arena.view([2, 8, 512], BF16)
        B["uslab"] = arena.view([2, 8, 512], BF16)
        B["dslab"] = arena.view([22, 512], BF16)
        B["hstage"] = arena.view([2, 512], F32)
        done("l0ffn")
        arena.cur_bf, arena.cur_f32 = fmark
        B["pstage"] = arena.view([2, 2816], BF16)
        B["wdq"] = arena.view([2, 2816], BF16)
        B["castbuf"] = arena.view([2, 3072], BF16)
        done("prep")
    if "l1ffn" in need:
        reset()
        B["sel16"] = arena.view([16 * 128], BF16, parts=16)
        B["hs"] = arena.view([8, 512], BF16)
        B["hr"] = arena.view([32, 512], BF16)
        B["combT"] = arena.view([2048], BF16, parts=16)
        B["lstage"] = arena.view([512], F32, parts=16)
        B["tklog"] = arena.view([2, 16], F32)
        B["tkexp"] = arena.view([2, 16], F32)
        B["tkmx"] = arena.view([8], F32)
        B["tke1"] = arena.view([16], F32)
        B["tkpm"] = arena.view([16], F32)
        B["tkt"] = arena.view([16], F32)
        B["comb"] = arena.view([2, 16], F32)
        B["sil"] = arena.view([512], F32)
        B["rslab"] = arena.view([8, 16], BF16)
        B["eslab"] = arena.view([2, 8, 256], BF16)
        B["uslab2"] = arena.view([2, 8, 256], BF16)
        B["dslab2"] = arena.view([2, 2, 1024], BF16)
        B["sgslab"] = arena.view([2, 8, 512], BF16)
        B["suslab"] = arena.view([2, 8, 512], BF16)
        B["sdslab"] = arena.view([2, 4, 1024], BF16)
        done("l1ffn")
    if "castbuf" not in B:
        reset()
        B["castbuf"] = arena.view([2, 3072], BF16)


def emit_prep_quant_weights(rc, B):
    sT, sS, sV, sG = B["sT"], B["sS"], B["sV"], B["sG"]
    dXL, dPS = B["dXL"], B["dPS"]
    banks = B["banks"]
    dr, scr = B["dr"], B["scr"]
    wld = B["castbuf"]
    jobs = [(dr["L0_ffn_w_gate"], scr["wp_l0g"], IFF, D),
            (dr["L0_ffn_w_up"], scr["wp_l0u"], IFF, D)]
    for (src, dst, R, C) in jobs:
        for ri in range(R // 128):
            slot = ri % 2
            rc.on("y", lambda y, i, src=src, ri=ri, slot=slot: y.dma_start(
                out=B["xld"][:, slot, :], in_=src[ri * 128:(ri + 1) * 128, :]),
                waits=B.pop(("pqfree", slot), []), incs=[(dXL[slot], 16)])
            emit_quant_tile(rc, B, B["xld"][:, slot, :], B["xqt"][:, slot, :], 1024,
                            src_wait=[dXL[slot].now()], dst_wait=[sT.now()])
            pb = banks.bf[slot]
            for c in range(8):
                rc.on("t", lambda t_, i, c=c, slot=slot, pb=pb: t_.transpose(
                    out=pb[:, c * 128:(c + 1) * 128],
                    in_=B["xqt"][:, slot, c * 128:(c + 1) * 128],
                    identity=B["ident_bf"]),
                    waits=([sV.now(), sG.now(), B["dCN"].now()] + banks.bwait(pb)
                           if c == 0 else []),
                    incs=[(sT, 1)])
            rc.on("s", lambda s, i, slot=slot, pb=pb: s.copy(
                out=B["pstage"][:, slot, 0:1024], in_=pb[:]),
                waits=[sT.now(), dPS[slot].now()], incs=[(sS, 1)])
            banks.bfree(pb, [sS.now()])
            for c in range(8):
                rc.on("y", lambda y, i, dst=dst, ri=ri, c=c, slot=slot: y.dma_start(
                    out=dst[c * 128:(c + 1) * 128, ri * 128:(ri + 1) * 128],
                    in_=B["pstage"][:, slot, c * 128:(c + 1) * 128]),
                    waits=([sS.now()] if c == 0 else []), incs=[(dPS[slot], 16)])
            B[("pqfree", slot)] = [sV.now(), sS.now(), sT.now()]
    src, dst = dr["L0_ffn_w_down"], scr["wp_l0d"]
    for ri in range(8):
        slot = ri % 2
        for p0, pw in ((0, 1024), (1024, 1024), (2048, 768)):
            rc.on("y", lambda y, i, src=src, ri=ri, slot=slot, p0=p0, pw=pw:
                  y.dma_start(out=B["xld"][:, slot, 0:pw],
                              in_=src[ri * 128:(ri + 1) * 128, p0:p0 + pw]),
                  waits=B.pop(("pqdfree", slot), []), incs=[(dXL[slot], 16)])
            emit_quant_tile(rc, B, B["xld"][:, slot, 0:pw],
                            B["wdq"][:, slot, p0:p0 + pw], pw,
                            src_wait=[dXL[slot].now()], dst_wait=[sT.now()])
            B[("pqdfree", slot)] = [B["sV"].now(), sS.now()]
        for cg in range(6):
            w = 512 if cg < 5 else 256
            pb = banks.bf[cg % 2]
            for c in range(w // 128):
                rc.on("t", lambda t_, i, cg=cg, c=c, slot=slot, pb=pb: t_.transpose(
                    out=pb[:, c * 128:(c + 1) * 128],
                    in_=B["wdq"][:, slot, cg * 512 + c * 128:cg * 512 + (c + 1) * 128],
                    identity=B["ident_bf"]),
                    waits=([sV.now(), sG.now(), B["dCN"].now()] + banks.bwait(pb)
                           if c == 0 else []),
                    incs=[(sT, 1)])
            rc.on("s", lambda s, i, cg=cg, slot=slot, pb=pb, w=w: s.copy(
                out=B["pstage"][:, slot, cg * 512:cg * 512 + w], in_=pb[:, 0:w]),
                waits=[sT.now(), dPS[slot].now()], incs=[(sS, 1)])
            banks.bfree(pb, [sS.now()])
        for c in range(22):
            rc.on("y", lambda y, i, dst=dst, ri=ri, c=c, slot=slot: y.dma_start(
                out=dst[c * 128:(c + 1) * 128, ri * 128:(ri + 1) * 128],
                in_=B["pstage"][:, slot, c * 128:(c + 1) * 128]),
                waits=([sS.now()] if c == 0 else []), incs=[(dPS[slot], 16)])
        B[("pqdfree", slot)] = [sV.now(), sS.now(), sT.now()]


def emit_prep_cast(rc, B, jobs):
    stage = B["castbuf"]
    for (src, dst, R, C) in jobs:
        for ri, r0 in enumerate(range(0, R, 128)):
            slot = ri % 2
            rc.on("g", lambda g, i, src=src, r0=r0, C=C, slot=slot:
                  g.dma_start(out=stage[:, slot, 0:C], in_=src[r0:r0 + 128, :]),
                  waits=[B["dCS"][slot].now()], incs=[(B["dCB"][slot], 16)])
            rc.on("y", lambda y, i, dst=dst, r0=r0, C=C, slot=slot:
                  y.dma_start(out=dst[r0:r0 + 128, :], in_=stage[:, slot, 0:C]),
                  waits=[B["dCB"][slot].now()], incs=[(B["dCS"][slot], 16)])


def emit_quant_tile(rc, B, src_ap, dst_ap, F, src_wait=(), dst_wait=(), qsel=0):
    nb = F // 16
    q = B["q"] if qsel == 0 else B["q2"]
    sV, sS, sG = B["sV"], B["sS"], B["sG"]
    src3 = src_ap.rearrange("p (a b) -> p a b", b=16)

    rc.on("v", lambda v, i, a=src3, nb=nb: v.tensor_reduce(
        out=q["mx"][:, 0:nb], in_=a, axis=AX.X, op=ALU.max,
        apply_absolute_value=True), waits=list(src_wait), incs=[(sV, 1)])
    rc.on("v", lambda v, i, nb=nb: v.tensor_scalar(
        out=q["scl"][:, 0:nb], in0=q["mx"][:, 0:nb], scalar1=1.0 / 6.0,
        scalar2=1e-12, op0=ALU.mult, op1=ALU.max),
        waits=[sV.now()], incs=[(sV, 1)])
    rc.on("v", lambda v, i, nb=nb: v.reciprocal(
        out=q["rcp"][:, 0:nb], in_=q["scl"][:, 0:nb]),
        waits=[sV.now()], incs=[(sV, 1)])
    rc.on("s", lambda s, i, a=src_ap, F=F: s.activation(
        out=q["abs"][:, 0:F], in_=a, func=AF.Abs),
        waits=list(src_wait) + [sG.now(), sV.now()], incs=[(sS, 1)])
    rc.on("s", lambda s, i, a=src_ap, F=F: s.activation(
        out=q["sgn"][:, 0:F], in_=a, func=AF.Sign), incs=[(sS, 1)])
    rc.on("g", lambda g, i, F=F, nb=nb: g.tensor_tensor(
        out=q["axn"][:, 0:F].rearrange("p (a b) -> p a b", b=16),
        in0=q["abs"][:, 0:F].rearrange("p (a b) -> p a b", b=16),
        in1=q["rcp"][:, 0:nb].unsqueeze(2).broadcast_to([128, nb, 16]),
        op=ALU.mult), waits=[sV.now(), sS.now()], incs=[(sG, 1)])
    rc.on("g", lambda g, i, F=F, nb=nb: g.tensor_tensor(
        out=q["w"][:, 0:F].rearrange("p (a b) -> p a b", b=16),
        in0=q["sgn"][:, 0:F].rearrange("p (a b) -> p a b", b=16),
        in1=q["scl"][:, 0:nb].unsqueeze(2).broadcast_to([128, nb, 16]),
        op=ALU.mult), incs=[(sG, 1)])
    A, Bv = q["tA"], q["tB"]
    rc.on("v", lambda v, i, F=F: v.tensor_scalar(
        out=Bv[:, 0:F], in0=q["axn"][:, 0:F], scalar1=5.0, scalar2=2.0,
        op0=ALU.is_gt, op1=ALU.mult), waits=[sG.now()], incs=[(sV, 1)])
    rc.on("v", lambda v, i, F=F: v.tensor_single_scalar(
        out=A[:, 0:F], in_=q["axn"][:, 0:F], scalar=0.25, op=ALU.is_gt),
        incs=[(sV, 1)])
    for thr, tgt, oth in ((2.5, "B", "A"), (0.75, "A", "B"), (3.5, "B", "A"),
                          (1.25, "A", "B"), (1.75, "A", "B")):
        dst = Bv if tgt == "B" else A
        rc.on("v", lambda v, i, F=F, thr=thr, dst=dst: v.scalar_tensor_tensor(
            out=dst[:, 0:F], in0=q["axn"][:, 0:F], scalar=thr, in1=dst[:, 0:F],
            op0=ALU.is_gt, op1=ALU.add), waits=[sV.now()], incs=[(sV, 1)])
    rc.on("v", lambda v, i, F=F: v.scalar_tensor_tensor(
        out=A[:, 0:F], in0=A[:, 0:F], scalar=0.5, in1=Bv[:, 0:F],
        op0=ALU.mult, op1=ALU.add), waits=[sV.now()], incs=[(sV, 1)])
    rc.on("g", lambda g, i, F=F, dst=dst_ap: g.tensor_tensor(
        out=dst, in0=A[:, 0:F], in1=q["w"][:, 0:F], op=ALU.mult),
        waits=[sV.now()] + list(dst_wait), incs=[(sG, 1)])


def emit_ln_tile(rc, B, xsrc_fn, t, slot, out_ap, out_f32=False):
    sV, sS = B["sV"], B["sS"]
    dXL = B["dXL"][slot]
    xld = B["xld"]
    rc.on("y", lambda y, i, t=t, slot=slot: y.dma_start(
        out=xld[:, slot, :], in_=xsrc_fn(i)[t * 128:(t + 1) * 128, :]),
        waits=B.pop(("xldfree", slot), []), incs=[(dXL, 16)])
    sl = slice(slot, slot + 1)
    rc.on("v", lambda v, i, slot=slot, sl=sl: v.tensor_reduce(
        out=B["lnms"][:, sl], in_=xld[:, slot, :], axis=AX.X, op=ALU.add),
        waits=[dXL.now()], incs=[(sV, 1)])
    rc.on("s", lambda s, i, slot=slot, sl=sl: s.activation(
        out=B["dump"][:], in_=xld[:, slot, :], func=AF.Square,
        accum_out=B["lnvs"][:, sl]), waits=[dXL.now(), sS.now()], incs=[(sS, 1)])
    rc.on("v", lambda v, i, sl=sl: v.tensor_scalar(
        out=B["lnm"][:, sl], in0=B["lnms"][:, sl], scalar1=1.0 / D, scalar2=None,
        op0=ALU.mult), waits=[sV.now()], incs=[(sV, 1)])
    rc.on("s", lambda s, i, sl=sl: s.activation(
        out=B["lnmsq"][:, sl], in_=B["lnm"][:, sl], func=AF.Square),
        waits=[sV.now()], incs=[(sS, 1)])
    rc.on("v", lambda v, i, sl=sl: v.tensor_scalar(
        out=B["lnvar"][:, sl], in0=B["lnvs"][:, sl], scalar1=1.0 / D,
        scalar2=B["lnmsq"][:, sl], op0=ALU.mult, op1=ALU.subtract),
        waits=[sS.now()], incs=[(sV, 1)])
    rc.on("s", lambda s, i, sl=sl: s.activation(
        out=B["lnt"][:, sl], in_=B["lnvar"][:, sl], func=AF.Ln, bias=B["eps"][:]),
        waits=[sV.now()], incs=[(sS, 1)])
    rc.on("s", lambda s, i, sl=sl: s.activation(
        out=B["lnr"][:, sl], in_=B["lnt"][:, sl], func=AF.Exp, scale=-0.5),
        waits=[sS.now()], incs=[(sS, 1)])
    rc.on("v", lambda v, i, slot=slot, sl=sl, out_ap=out_ap: v.tensor_scalar(
        out=out_ap, in0=xld[:, slot, :], scalar1=B["lnm"][:, sl],
        scalar2=B["lnr"][:, sl], op0=ALU.subtract, op1=ALU.mult),
        waits=[sS.now()], incs=[(sV, 1)])
    B[("xldfree", slot)] = [sV.now(), sS.now()]


def emit_tile_to_feature(rc, B, src_ap, dst3_ap, tbank_idx):
    sV, sT = B["sV"], B["sT"]
    banks = B["banks"]
    pb = banks.bf[tbank_idx]
    for c in range(8):
        rc.on("t", lambda t, i, c=c, src=src_ap, pb=pb: t.transpose(
            out=pb[:, c * 128:(c + 1) * 128], in_=src[:, c * 128:(c + 1) * 128],
            identity=B["ident_bf"][:]),
            waits=([sV.now(), B["sG"].now()] + banks.bwait(pb) if c == 0 else []),
            incs=([(sT, 1)] if c == 7 else []))
    rc.on("s", lambda s, i, pb=pb, dst=dst3_ap: s.copy(
        out=dst, in_=pb[:].rearrange("p (c w) -> p c w", w=128)),
        waits=[sT.now()], incs=[(B["sS"], 1)])
    banks.bfree(pb, [B["sS"].now()])


def emit_phase_l1ffn(rc, B, xsrc_fn, xdst_fn):
    sV, sS, sT, sG = B["sV"], B["sS"], B["sT"], B["sG"]
    banks = B["banks"]
    xnf = B["xnf"]
    scr = B["scr"]

    rc.on("y", lambda y, i: y.dma_start(
        out=B["sel16"], in_=B["dr"]["c_sel16"]),
        waits=[sT.now()], incs=[(B["dCN"], 16)])
    rc.on("y", lambda y, i: y.dma_start(
        out=B["rslab"][:], in_=scr["wp_router"].rearrange("(c p) m -> p c m", p=128)),
        incs=[(B["dRT"], 16)])
    for e in range(2):
        rc.on("y", lambda y, i, e=e: y.dma_start(
            out=B["sgslab"][:, e, :, :],
            in_=scr["wp_l1sg"][e].rearrange("(c p) m -> p c m", p=128)),
            waits=([sT.now()] if e == 0 else []), incs=[(B["dSG"], 16)])
        rc.on("y", lambda y, i, e=e: y.dma_start(
            out=B["suslab"][:, e, :, :],
            in_=scr["wp_l1su"][e].rearrange("(c p) m -> p c m", p=128)),
            waits=([sT.now()] if e == 0 else []), incs=[(B["dSU"], 16)])
        rc.on("y", lambda y, i, e=e: y.dma_start(
            out=B["sdslab"][:, e, :, :],
            in_=scr["wp_l1sd"][e].rearrange("(c p) m -> p c m", p=128)),
            waits=([sT.now()] if e == 0 else []), incs=[(B["dSDW"], 16)])

    lnrdy = []
    for t in range(16):
        emit_ln_tile(rc, B, xsrc_fn, t, t % 2, B["xnt"][:, t % 2, :])
        emit_tile_to_feature(rc, B, B["xnt"][:, t % 2, :],
                             xnf[:, 0:8, t * 128:(t + 1) * 128, ], t % 2)
        B[("xldfree", t % 2)] = B[("xldfree", t % 2)] + [sS.now()]
        lnrdy.append(sS.now())
    ln_done = sS.now()

    for tg in range(4):
        bank = banks.f32[5]
        rc.on("t", lambda t_, i, tg=tg, bank=bank: t_.matmul(
            out=bank[0:16, :], lhsT=B["rslab"][:, 0, :],
            rhs=xnf[:, 0, tg * 512:(tg + 1) * 512], start=True, stop=False),
            waits=[lnrdy[tg * 4 + 3], B["dRT"].now()] + banks.bwait(bank), incs=[])
        for kc in range(1, 8):
            rc.on("t", lambda t_, i, tg=tg, kc=kc, bank=bank: t_.matmul(
                out=bank[0:16, :], lhsT=B["rslab"][:, kc, :],
                rhs=xnf[:, kc, tg * 512:(tg + 1) * 512],
                start=False, stop=(kc == 7)), incs=([(sT, 1)] if kc == 7 else []))
        rc.on("v", lambda v, i, bank=bank: v.tensor_copy(
            out=B["lstage"][0:16, :], in_=bank[0:16, :]),
            waits=[sT.now()], incs=[(sV, 1)])
        banks.bfree(bank, [sV.now()])
        for ts_ in range(4):
            tile = tg * 4 + ts_
            slot = tile % 2
            tb = banks.f32[4]
            rc.on("t", lambda t_, i, ts_=ts_, tb=tb: t_.transpose(
                out=tb[0:128, 0:16], in_=B["lstage"][0:16, ts_ * 128:(ts_ + 1) * 128],
                identity=B["ident_f32"][0:16, 0:16]),
                waits=[sV.now()] + banks.bwait(tb), incs=[(sT, 1)])
            rc.on("s", lambda s, i, slot=slot, tb=tb: s.copy(
                out=B["tklog"][:, slot, :], in_=tb[0:128, 0:16]),
                waits=[sT.now(), sG.now(), sV.now()], incs=[(sS, 1)])
            banks.bfree(tb, [sS.now()])
            tl = B["tklog"][:, slot, :]
            rc.on("v", lambda v, i, tl=tl: v.tensor_reduce(
                out=B["tkmx"][:, 0:1], in_=tl, axis=AX.X, op=ALU.max, negate=True),
                waits=[sS.now(), sG.now()], incs=[(sV, 1)])
            rc.on("s", lambda s, i, tl=tl, slot=slot: s.activation(
                out=B["tkexp"][:, slot, :], in_=tl, func=AF.Exp,
                bias=B["tkmx"][:, 0:1]), waits=[sV.now()], incs=[(sS, 1)])
            rc.on("v", lambda v, i, slot=slot: v.tensor_reduce(
                out=B["tkmx"][:, 1:2], in_=B["tkexp"][:, slot, :], axis=AX.X,
                op=ALU.max), waits=[sS.now()], incs=[(sV, 1)])
            rc.on("v", lambda v, i, slot=slot: v.tensor_single_scalar(
                out=B["tke1"][:], in_=B["tkexp"][:, slot, :],
                scalar=B["tkmx"][:, 1:2], op=ALU.is_ge),
                waits=[sV.now()], incs=[(sV, 1)])
            rc.on("v", lambda v, i: v.tensor_scalar(
                out=B["tkt"][:], in0=B["tke1"][:], scalar1=-1.0, scalar2=1.0,
                op0=ALU.mult, op1=ALU.add), waits=[sV.now()], incs=[(sV, 1)])
            rc.on("g", lambda g, i, slot=slot: g.tensor_tensor(
                out=B["tkpm"][:], in0=B["tkt"][:], in1=B["tkexp"][:, slot, :],
                op=ALU.mult), waits=[sV.now()], incs=[(sG, 1)])
            rc.on("v", lambda v, i: v.tensor_reduce(
                out=B["tkmx"][:, 2:3], in_=B["tkpm"][:], axis=AX.X, op=ALU.max),
                waits=[sG.now()], incs=[(sV, 1)])
            rc.on("g", lambda g, i: g.tensor_tensor(
                out=B["tkmx"][:, 3:4], in0=B["tkmx"][:, 1:2], in1=B["tkmx"][:, 2:3],
                op=ALU.add), waits=[sV.now()], incs=[(sG, 1)])
            rc.on("v", lambda v, i: v.tensor_scalar(
                out=B["tkmx"][:, 4:5], in0=B["tkmx"][:, 3:4], scalar1=1e-9,
                scalar2=None, op0=ALU.add), waits=[sG.now()], incs=[(sV, 1)])
            rc.on("v", lambda v, i: v.reciprocal(
                out=B["tkmx"][:, 5:6], in_=B["tkmx"][:, 4:5]),
                waits=[sV.now()], incs=[(sV, 1)])
            rc.on("g", lambda g, i: g.tensor_tensor(
                out=B["tkmx"][:, 6:7], in0=B["tkmx"][:, 1:2], in1=B["tkmx"][:, 5:6],
                op=ALU.mult), waits=[sV.now()], incs=[(sG, 1)])
            rc.on("g", lambda g, i: g.tensor_tensor(
                out=B["tkmx"][:, 7:8], in0=B["tkmx"][:, 2:3], in1=B["tkmx"][:, 5:6],
                op=ALU.mult), incs=[(sG, 1)])
            rc.on("v", lambda v, i: v.tensor_single_scalar(
                out=B["tkt"][:], in_=B["tkpm"][:], scalar=B["tkmx"][:, 2:3],
                op=ALU.is_ge), waits=[sG.now()], incs=[(sV, 1)])
            rc.on("v", lambda v, i: v.tensor_single_scalar(
                out=B["tke1"][:], in_=B["tke1"][:], scalar=B["tkmx"][:, 6:7],
                op=ALU.mult), waits=[sV.now(), sG.now()], incs=[(sV, 1)])
            rc.on("v", lambda v, i, slot=slot: v.scalar_tensor_tensor(
                out=B["comb"][:, slot, :], in0=B["tkt"][:], scalar=B["tkmx"][:, 7:8],
                in1=B["tke1"][:], op0=ALU.mult, op1=ALU.add),
                waits=[sV.now(), sG.now()], incs=[(sV, 1)])
            tb2 = banks.f32[4]
            rc.on("t", lambda t_, i, slot=slot, tb2=tb2: t_.transpose(
                out=tb2[0:16, 0:128], in_=B["comb"][:, slot, :],
                identity=B["ident_f32"][:]),
                waits=[sV.now()] + banks.bwait(tb2), incs=[(sT, 1)])
            rc.on("s", lambda s, i, tile=tile, tb2=tb2: s.copy(
                out=B["combT"][0:16, tile * 128:(tile + 1) * 128], in_=tb2[0:16, 0:128]),
                waits=[sT.now(), sS.now()], incs=[(sS, 1)])
            banks.bfree(tb2, [sS.now()])
    comb_done = sS.now()

    for tg in range(4):
        for e in range(2):
            for mc in range(4):
                gb = banks.f32[0]
                ub = banks.f32[1]
                for kc in range(8):
                    rc.on("t", lambda t_, i, e=e, mc=mc, kc=kc, tg=tg, gb=gb: t_.matmul(
                        out=gb[:], lhsT=B["sgslab"][:, e, kc, mc * 128:(mc + 1) * 128],
                        rhs=xnf[:, kc, tg * 512:(tg + 1) * 512],
                        start=(kc == 0), stop=(kc == 7)),
                        waits=([lnrdy[tg * 4 + 3], B["dSG"].now()] + banks.bwait(gb)
                               if kc == 0 else []),
                        incs=([(sT, 1)] if kc == 7 else []))
                for kc in range(8):
                    rc.on("t", lambda t_, i, e=e, mc=mc, kc=kc, tg=tg, ub=ub: t_.matmul(
                        out=ub[:], lhsT=B["suslab"][:, e, kc, mc * 128:(mc + 1) * 128],
                        rhs=xnf[:, kc, tg * 512:(tg + 1) * 512],
                        start=(kc == 0), stop=(kc == 7)),
                        waits=([B["dSU"].now()] + banks.bwait(ub) if kc == 0 else []),
                        incs=([(sT, 1)] if kc == 7 else []))
                rc.on("s", lambda s, i, gb=gb: s.activation(
                    out=B["sil"][:], in_=gb[:], func=AF.Silu),
                    waits=[sT.now(), sV.now()], incs=[(sS, 1)])
                banks.bfree(gb, [sS.now()])
                rc.on("v", lambda v, i, e=e, mc=mc, ub=ub: v.tensor_tensor(
                    out=B["hs"][:, e * 4 + mc, :], in0=B["sil"][:], in1=ub[:],
                    op=ALU.mult), waits=[sS.now()], incs=[(sV, 1)])
                banks.bfree(ub, [sV.now()])
        for e in range(NEXP):
            slot = e % 2
            rc.on("y", lambda y, i, e=e, slot=slot: y.dma_start(
                out=B["eslab"][:, slot, :, :],
                in_=scr["wp_l1rg"][e].rearrange("(c p) m -> p c m", p=128)),
                waits=B.pop(("esfree", slot), []), incs=[(B["dES0" if slot == 0 else "dES1"], 16)])
            rc.on("y", lambda y, i, e=e, slot=slot: y.dma_start(
                out=B["uslab2"][:, slot, :, :],
                in_=scr["wp_l1ru"][e].rearrange("(c p) m -> p c m", p=128)),
                waits=B.pop(("usfree", slot), []), incs=[(B["dUS20" if slot == 0 else "dUS21"], 16)])
            bc = banks.f32[2]
            rc.on("t", lambda t_, i, e=e, tg=tg, bc=bc: t_.matmul(
                out=bc[:], lhsT=B["sel16"][:, e * 128:(e + 1) * 128],
                rhs=B["combT"][0:16, tg * 512:(tg + 1) * 512], start=True, stop=True),
                waits=[comb_done, B["dCN"].now()] + banks.bwait(bc), incs=[(sT, 1)])
            bc_ready = sT.now()
            for mc in range(2):
                gb = banks.f32[3]
                ub = banks.f32[0]
                for kc in range(8):
                    rc.on("t", lambda t_, i, e=e, mc=mc, kc=kc, tg=tg, slot=slot,
                          gb=gb: t_.matmul(
                        out=gb[:], lhsT=B["eslab"][:, slot, kc, mc * 128:(mc + 1) * 128],
                        rhs=xnf[:, kc, tg * 512:(tg + 1) * 512],
                        start=(kc == 0), stop=(kc == 7)),
                        waits=([B["dES0" if slot == 0 else "dES1"].now()]
                               + banks.bwait(gb) if kc == 0 else []),
                        incs=([(sT, 1)] if kc == 7 else []))
                for kc in range(8):
                    rc.on("t", lambda t_, i, e=e, mc=mc, kc=kc, tg=tg, slot=slot,
                          ub=ub: t_.matmul(
                        out=ub[:], lhsT=B["uslab2"][:, slot, kc, mc * 128:(mc + 1) * 128],
                        rhs=xnf[:, kc, tg * 512:(tg + 1) * 512],
                        start=(kc == 0), stop=(kc == 7)),
                        waits=([B["dUS20" if slot == 0 else "dUS21"].now()]
                               + banks.bwait(ub) if kc == 0 else []),
                        incs=([(sT, 1)] if kc == 7 else []))
                rc.on("s", lambda s, i, gb=gb: s.activation(
                    out=B["sil"][:], in_=gb[:], func=AF.Silu),
                    waits=[sT.now(), sV.now()], incs=[(sS, 1)])
                banks.bfree(gb, [sS.now()])
                rc.on("v", lambda v, i, ub=ub: v.tensor_tensor(
                    out=B["sil"][:], in0=B["sil"][:], in1=ub[:], op=ALU.mult),
                    waits=[sS.now()], incs=[(sV, 1)])
                banks.bfree(ub, [sV.now()])
                rc.on("v", lambda v, i, e=e, mc=mc, bc=bc: v.tensor_tensor(
                    out=B["hr"][:, e * 2 + mc, :], in0=B["sil"][:], in1=bc[:],
                    op=ALU.mult), waits=[sV.now()], incs=[(sV, 1)])
                if mc == 1:
                    banks.bfree(bc, [sV.now()])
            B[("esfree", slot)] = [sT.now()]
            B[("usfree", slot)] = [sT.now()]
        h_done = sV.now()

        dwn = ([(banks.f32[j], banks.f32[j][:]) for j in range(6)]
               + [(bb, bb[:].bitcast(F32)) for bb in banks.bf])
        for bko, _ in dwn:
            for w in banks.bwait(bko):
                rc.on("t", None, waits=[w])
        nsrcmc = 2 * 4 + NEXP * 2
        si = 0
        for src_i in range(2 + NEXP):
            routed = src_i >= 2
            e = src_i - 2 if routed else src_i
            if routed:
                slot = e % 2
                rc.on("y", lambda y, i, e=e, slot=slot: y.dma_start(
                    out=B["dslab2"][:, slot, :, :],
                    in_=scr["wp_l1rd"][e].rearrange("(c p) m -> p c m", p=128)),
                    waits=B.pop(("dsfree", slot), []),
                    incs=[(B["dDS20" if slot == 0 else "dDS21"], 16)])
            nmc = 2 if routed else 4
            for mc in range(nmc):
                hsrc = B["hr"][:, e * 2 + mc, :] if routed else B["hs"][:, e * 4 + mc, :]
                for tt in range(4):
                    for og in range(2):
                        bk = dwn[tt * 2 + og][1]
                        if routed:
                            rhs_fn = (lambda i, slot=slot, mc=mc, og=og:
                                      B["dslab2"][:, slot, mc, og * 512:(og + 1) * 512])
                            w0 = ([B["dDS20" if slot == 0 else "dDS21"].now(), h_done]
                                  if (tt == 0 and og == 0 and mc == 0) else [])
                        else:
                            rhs_fn = (lambda i, e=e, mc=mc, og=og:
                                      B["sdslab"][:, e, mc, og * 512:(og + 1) * 512])
                            w0 = ([B["dSDW"].now(), h_done]
                                  if (tt == 0 and og == 0 and mc == 0) else [])
                        rc.on("t", lambda t_, i, hsrc=hsrc, rhs_fn=rhs_fn, tt=tt,
                              bk=bk, si=si, nsrcmc=nsrcmc: t_.matmul(
                            out=bk, lhsT=hsrc[:, tt * 128:(tt + 1) * 128],
                            rhs=rhs_fn(i), start=(si == 0), stop=(si == nsrcmc - 1)),
                            waits=w0, incs=([(sT, 1)] if (tt == 3 and og == 1 and
                                                          mc == nmc - 1) else []))
                si += 1
            if routed:
                B[("dsfree", slot)] = [sT.now()]
        down_done = sT.now()
        for tt in range(4):
            ttg = tg * 4 + tt
            xslot = tt % 2
            rc.on("y", lambda y, i, ttg=ttg, xslot=xslot: y.dma_start(
                out=B["xld2"][:, xslot, :], in_=xsrc_fn(i)[ttg * 128:(ttg + 1) * 128, :]),
                waits=B.pop(("xld2free", xslot), []), incs=[(B["dX2"][xslot], 16)])
            for og in range(2):
                bko, bk = dwn[tt * 2 + og]
                rc.on("v", lambda v, i, og=og, xslot=xslot, bk=bk: v.tensor_tensor(
                    out=B["yst"][:, xslot, og * 512:(og + 1) * 512], in0=bk,
                    in1=B["xld2"][:, xslot, og * 512:(og + 1) * 512], op=ALU.add),
                    waits=[down_done, B["dX2"][xslot].now()] + B.pop(("ystfree", xslot), []),
                    incs=[(sV, 1)])
                banks.bfree(bko, [sV.now()])
            rc.on("y", lambda y, i, ttg=ttg, xslot=xslot: y.dma_start(
                out=xdst_fn(i)[ttg * 128:(ttg + 1) * 128, :], in_=B["yst"][:, xslot, :]),
                waits=[sV.now()], incs=[(B["dST"][xslot], 16)])
            B[("xld2free", xslot)] = [sV.now()]
            B[("ystfree", xslot)] = [B["dST"][xslot].now()]


def _emit_local_pv(rc, B, banks, pfE, vt6, qg, h, item, sT):
    kt, c0, c1, pw, ki = item
    for rcol in range(c0, c1, 128):
        tl_r = rcol // 128
        first_kt = max(qg * 4 + tl_r - 1, 0)
        rc.on("t", lambda t_, i, h=h, kt=kt, rcol=rcol, qg=qg,
              st=(kt == first_kt),
              sp=(kt == qg * 4 + tl_r): t_.matmul(
            out=pfE[0:65, rcol:rcol + 128], lhsT=vt6[:, kt % 5, h, 0:65],
            rhs=B["ptile"][:, kt % 2, rcol:rcol + 128],
            start=st, stop=sp),
            waits=[pw] + (banks.bwait(pfE) if ki == 0 and rcol == c0 else []),
            incs=[(sT, 1)])


def emit_phase_l0attn(rc, B, xsrc_fn, xdst_fn):
    sV, sS, sT, sG = B["sV"], B["sS"], B["sT"], B["sG"]
    banks = B["banks"]
    pfE, pfF = banks.f32[4], banks.f32[5]
    xnf, kbuf, qgb = B["xnf"], B["kbuf"], B["qgb"]
    vt6, rqb, rkf, rvt = B["vt6"], B["rqb"], B["rkf"], B["rvt"]
    localb, remoteb, latc = B["localb"], B["remoteb"], B["latc"]
    slab = B["slab"]
    scr = B["scr"]
    NRQ, NRK, NK, NQ = 0, 8, 10, 42

    rc.on("v", lambda v, i: v.memset(vt6[:, :, :, 64:65], 1.0), incs=[(sV, 1)])
    rc.on("v", lambda v, i: v.memset(rvt[:, :, 64:65], 1.0),
          waits=[sV.now()], incs=[(sV, 1)])
    rc.on("y", lambda y, i: y.dma_start(out=B["mask_lat"], in_=B["dr"]["c_mask_lat"]),
          waits=[sV.now()], incs=[(B["dCN"], 16)])

    rkvslab = slab[:, 0, :, 0:512]
    rqslab = slab[:, 0, :, 512:768]
    rc.on("g", lambda g, i: g.dma_start(
        out=rqslab, in_=B["dr"]["L0_attn_rq_w"].rearrange("(c p) m -> p c m", p=128)),
        waits=[sT.now()], incs=[(B["dGA"], 16)])
    rc.on("g", lambda g, i: g.dma_start(
        out=rkvslab, in_=B["dr"]["L0_attn_rkv_w"].rearrange("(c p) m -> p c m", p=128)),
        waits=[sT.now()], incs=[(B["dGB"], 16)])
    rc.on("g", lambda g, i: g.dma_start(
        out=B["routslab"][:], in_=B["dr"]["L0_attn_rout_w"]
        .rearrange("(c p) m -> p c m", p=128)),
        waits=[sT.now()], incs=[(B["dCB0"], 16)])
    rc.on("g", lambda g, i: g.dma_start(
        out=slab[:, 1, :, :], in_=B["dr"]["L0_attn_qkv_w"][:, 2048:3072]
        .rearrange("(c p) m -> p c m", p=128)),
        waits=[sT.now()], incs=[(B["dSL1"], 16)])

    for t in range(16):
        emit_ln_tile(rc, B, xsrc_fn, t, t % 2, B["xnt"][:, t % 2, :])
        emit_tile_to_feature(rc, B, B["xnt"][:, t % 2, :],
                             xnf[:, 0:8, t * 128:(t + 1) * 128], t % 2)
        B[("xldfree", t % 2)] = B[("xldfree", t % 2)] + [sS.now()]
    ln_done = sS.now()

    def _pool_reduce(v, i, c):
        with v.bass.allow_low_precision(reason="32-chunk mean pooled to bf16"):
            return v.tensor_reduce(
                out=B["pooledb"][:, c, :], in_=xnf[:, c, :]
                .rearrange("p (a b) -> p a b", b=32), axis=AX.X, op=ALU.add)

    for c in range(8):
        rc.on("v", lambda v, i, c=c: _pool_reduce(v, i, c),
              waits=([ln_done] if c == 0 else []), incs=[(sV, 1)])
        rc.on("v", lambda v, i, c=c: v.tensor_scalar(
            out=B["pooledb"][:, c, :], in0=B["pooledb"][:, c, :],
            scalar1=1.0 / 32.0, scalar2=None, op0=ALU.mult),
            waits=[sV.now()], incs=[(sV, 1)])
    pool_done = sV.now()

    for oc in range(2):
        bank = banks.f32[oc % 2]
        for kc in range(8):
            rc.on("t", lambda t_, i, oc=oc, kc=kc, bank=bank: t_.matmul(
                out=bank[:, 0:64], lhsT=rkvslab[:, kc, oc * 128:(oc + 1) * 128],
                rhs=B["pooledb"][:, kc, :], start=(kc == 0), stop=(kc == 7)),
                waits=([pool_done, B["dGB"].now()] + banks.bwait(bank)
                       if kc == 0 else []),
                incs=([(sT, 1)] if kc == 7 else []))
        rc.on("v", lambda v, i, oc=oc, bank=bank: v.tensor_scalar(
            out=rkf[:, oc, :], in0=bank[:, 0:64],
            scalar1=B["bias_pp"][:, BPP["l0_rk"] + oc:BPP["l0_rk"] + oc + 1],
            scalar2=None, op0=ALU.add), waits=[sT.now()], incs=[(sV, 1)])
        banks.bfree(bank, [sV.now()])
    bank = banks.f32[2]
    rc.on("t", lambda t_, i, bank=bank: t_.matmul(
        out=bank[0:64, 0:256], lhsT=B["ones_row_bf"][0:1, 0:64],
        rhs=B["bias_row"][0:1, BROW["l0_rv"]:BROW["l0_rv"] + 256],
        start=True, stop=False),
        waits=[pool_done, B["dGB"].now()] + banks.bwait(bank), incs=[])
    for kc in range(8):
        rc.on("t", lambda t_, i, kc=kc, bank=bank: t_.matmul(
            out=bank[0:64, 0:256], lhsT=B["pooledb"][:, kc, :],
            rhs=rkvslab[:, kc, 256:512], start=False, stop=(kc == 7)),
            incs=([(sT, 1)] if kc == 7 else []))
    rc.on("s", lambda s, i, bank=bank: s.copy(
        out=rvt[0:64, 0:4, 0:64],
        in_=bank[0:64, 0:256].rearrange("p (h d) -> p h d", d=64)),
        waits=[sT.now()], incs=[(sS, 1)])
    banks.bfree(bank, [sS.now()])

    for oc in range(2):
        for kc in range(8):
            for tg in range(4):
                bank = banks.f32[tg]
                rc.on("t", lambda t_, i, oc=oc, tg=tg, kc=kc, bank=bank: t_.matmul(
                    out=bank[:], lhsT=rqslab[:, kc, oc * 128:(oc + 1) * 128],
                    rhs=xnf[:, kc, tg * 512:(tg + 1) * 512],
                    start=(kc == 0), stop=(kc == 7)),
                    waits=([ln_done, B["dGA"].now()] + banks.bwait(bank)
                           if kc == 0 else []),
                    incs=([(sT, 1)] if kc == 7 and tg == 3 else []))
        for tg in range(4):
            bank = banks.f32[tg]
            rc.on("v", lambda v, i, oc=oc, tg=tg, bank=bank: v.tensor_scalar(
                out=rqb[:, oc, tg * 512:(tg + 1) * 512], in0=bank[:],
                scalar1=B["bias_pp"][:, BPP["l0_rq"] + oc:BPP["l0_rq"] + oc + 1],
                scalar2=None, op0=ALU.add),
                waits=([sT.now()] if tg == 0 else []), incs=[(sV, 1)])
            banks.bfree(bank, [sV.now()])
    rq_done = sV.now()

    def norm_cols(src_fn, n, base, width=512, wait0=()):
        for j in range(n):
            rc.on("s", lambda s, i, j=j: s.activation(
                out=B["sqscr"][:, 0:width], in_=src_fn(j), func=AF.Square),
                waits=list(wait0) + [sT.now(), sS.now()], incs=[(sS, 1)])
            bank = banks.f32[2 + j % 2]
            rc.on("t", lambda t_, i, bank=bank: t_.matmul(
                out=bank[0:2, 0:width], lhsT=B["headmask"][:],
                rhs=B["sqscr"][:, 0:width], start=True, stop=True),
                waits=[sS.now()] + banks.bwait(bank), incs=[(sT, 1)])
            rc.on("v", lambda v, i, j=j, bank=bank: v.tensor_reduce(
                out=B["nrm"][:, base + j:base + j + 1], in_=bank[0:2, 0:width],
                axis=AX.X, op=ALU.max), waits=[sT.now()], incs=[(sV, 1)])
            banks.bfree(bank, [sV.now()])

    def reduce_norm(base, n, dst):
        rc.on("v", lambda v, i: v.tensor_reduce(
            out=B["kn2a"][0:2, :], in_=B["nrm"][:, base:base + n], axis=AX.X,
            op=ALU.max), waits=[sV.now()], incs=[(sV, 1)])
        rc.on("t", lambda t_, i: t_.transpose(
            out=pfF[0:1, 0:2], in_=B["kn2a"][0:2, 0:1],
            identity=B["ident_f32"][0:2, 0:2]),
            waits=[sV.now()] + banks.bwait(pfF), incs=[(sT, 1)])
        rc.on("v", lambda v, i, dst=dst: v.tensor_reduce(
            out=dst, in_=pfF[0:1, 0:2], axis=AX.X, op=ALU.max),
            waits=[sT.now()], incs=[(sV, 1)])
        banks.bfree(pfF, [sV.now()])

    def make_neg_bound(qsrc, ksrc, dst):
        rc.on("v", lambda v, i, qsrc=qsrc, ksrc=ksrc: v.tensor_tensor(
            out=B["pp1"][0:1, :], in0=qsrc, in1=ksrc, op=ALU.mult),
            waits=[sV.now()], incs=[(sV, 1)])
        rc.on("s", lambda s, i: s.activation(
            out=B["pp1"][0:1, :], in_=B["pp1"][0:1, :], func=AF.Ln,
            bias=B["tiny"][0:1, :]), waits=[sV.now()], incs=[(sS, 1)])
        rc.on("s", lambda s, i: s.activation(
            out=B["pp2"][0:1, :], in_=B["pp1"][0:1, :], func=AF.Exp, scale=0.5),
            waits=[sS.now()], incs=[(sS, 1)])
        rc.on("v", lambda v, i: v.tensor_scalar(
            out=B["pp2f"][0:1, :], in0=B["pp2"][0:1, :], scalar1=-0.125,
            scalar2=None, op0=ALU.mult), waits=[sS.now()], incs=[(sV, 1)])
        rc.on("t", lambda t_, i: t_.matmul(
            out=pfF[:, 0:1], lhsT=B["ones_row_f32"][0:1, :], rhs=B["pp2f"][0:1, 0:1],
            start=True, stop=True),
            waits=[sV.now()] + banks.bwait(pfF), incs=[(sT, 1)])
        rc.on("s", lambda s, i, dst=dst: s.copy(out=dst, in_=pfF[:, 0:1]),
            waits=[sT.now(), sS.now()], incs=[(sS, 1)])
        banks.bfree(pfF, [sS.now()])

    norm_cols(lambda j: rqb[:, j // 4, (j % 4) * 512:(j % 4) * 512 + 512], 8, NRQ,
              wait0=[rq_done])
    reduce_norm(NRQ, 8, B["qn2"][0:1, :])
    norm_cols(lambda j: rkf[:, j, :], 2, NRK, width=64)
    reduce_norm(NRK, 2, B["kn2"][0:1, :])
    make_neg_bound(B["qn2"][0:1, :], B["kn2"][0:1, :], B["negM2"][:])

    rc.on("g", lambda g, i: g.dma_start(
        out=slab[:, 0, :, :], in_=B["dr"]["L0_attn_qkv_w"][:, 1024:2048]
        .rearrange("(c p) m -> p c m", p=128)),
        waits=[sT.now()], incs=[(B["dSL0"], 16)])
    for oc in range(8):
        for kc in range(8):
            for tg in range(4):
                bank = banks.f32[tg]
                rc.on("t", lambda t_, i, oc=oc, tg=tg, kc=kc, bank=bank: t_.matmul(
                    out=bank[:], lhsT=slab[:, 0, kc, oc * 128:(oc + 1) * 128],
                    rhs=xnf[:, kc, tg * 512:(tg + 1) * 512],
                    start=(kc == 0), stop=(kc == 7)),
                    waits=([ln_done, B["dSL0"].now()] + banks.bwait(bank)
                           if kc == 0 else []),
                    incs=([(sT, 1)] if kc == 7 and tg == 3 else []))
        for tg in range(4):
            bank = banks.f32[tg]
            rc.on("v", lambda v, i, oc=oc, tg=tg, bank=bank: v.tensor_scalar(
                out=kbuf[:, oc, tg * 512:(tg + 1) * 512], in0=bank[:],
                scalar1=B["bias_pp"][:, BPP["l0_k"] + oc:BPP["l0_k"] + oc + 1],
                scalar2=None, op0=ALU.add),
                waits=([sT.now()] if tg == 0 else []), incs=[(sV, 1)])
            banks.bfree(bank, [sV.now()])
    k_done = sV.now()
    norm_cols(lambda j: kbuf[:, j // 4, (j % 4) * 512:(j % 4) * 512 + 512], 32, NK,
              wait0=[k_done])
    reduce_norm(NK, 32, B["kn2"][0:1, :])

    for qg in range(4):
        rc.on("g", lambda g, i: g.dma_start(
            out=slab[:, 0, :, :], in_=B["dr"]["L0_attn_qkv_w"][:, 0:1024]
            .rearrange("(c p) m -> p c m", p=128)),
            waits=[sT.now()], incs=[(B["dSL0"], 16)])
        for oc in range(8):
            bank = banks.f32[oc % 2]
            for kc in range(8):
                rc.on("t", lambda t_, i, oc=oc, kc=kc, qg=qg, bank=bank: t_.matmul(
                    out=bank[:], lhsT=slab[:, 0, kc, oc * 128:(oc + 1) * 128],
                    rhs=xnf[:, kc, qg * 512:(qg + 1) * 512],
                    start=(kc == 0), stop=(kc == 7)),
                    waits=([B["dSL0"].now(), sV.now()] + banks.bwait(bank)
                           if kc == 0 else []),
                    incs=([(sT, 1)] if kc == 7 else []))
            rc.on("v", lambda v, i, oc=oc, bank=bank: v.tensor_scalar(
                out=qgb[:, oc, :], in0=bank[:],
                scalar1=B["bias_pp"][:, BPP["l0_q"] + oc:BPP["l0_q"] + oc + 1],
                scalar2=None, op0=ALU.add), waits=[sT.now()], incs=[(sV, 1)])
            banks.bfree(bank, [sV.now()])
        q_done = sV.now()
        norm_cols(lambda j: qgb[:, j, :], 8, NQ, wait0=[q_done])
        reduce_norm(NQ, 8, B["qn2"][0:1, :])
        make_neg_bound(B["qn2"][0:1, :], B["kn2"][0:1, :], B["negM"][:])
        negm_ready = sS.now()

        for kt in range(qg * 4, qg * 4 + 4):
            for og in range(2):
                bank = banks.f32[(kt * 2 + og) % 2]
                rc.on("t", lambda t_, i, og=og, bank=bank: t_.matmul(
                    out=bank[:], lhsT=B["ones_row_bf"][0:1, :],
                    rhs=B["bias_row"][0:1, BROW["l0_v"] + og * 512:BROW["l0_v"] + og * 512 + 512],
                    start=True, stop=False),
                    waits=[B["dSL1"].now(), sS.now()] + banks.bwait(bank), incs=[])
                for kc in range(8):
                    rc.on("t", lambda t_, i, kt=kt, og=og, kc=kc, bank=bank: t_.matmul(
                        out=bank[:], lhsT=xnf[:, kc, kt * 128:(kt + 1) * 128],
                        rhs=slab[:, 1, kc, og * 512:(og + 1) * 512],
                        start=False, stop=(kc == 7)),
                        incs=([(sT, 1)] if kc == 7 else []))
                rc.on("s", lambda s, i, kt=kt, og=og, bank=bank: s.copy(
                    out=vt6[:, kt % 5, og * 8:(og + 1) * 8, 0:64],
                    in_=bank[:].rearrange("p (h d) -> p h d", d=64)),
                    waits=[sT.now()], incs=[(sS, 1)])
                banks.bfree(bank, [sS.now()])
        v_ready = sS.now()

        for h in range(16):
            c = h // 2
            r0 = (h % 2) * 64
            kts = [kt for kt in range(qg * 4 - 1, qg * 4 + 4) if kt >= 0]
            pv_q = []
            for ki, kt in enumerate(kts):
                j = kt - qg * 4
                c0 = max(0, j * 128)
                c1 = min(512, (j + 2) * 128)
                sbank = banks.f32[2 + kt % 2]
                rc.on("t", lambda t_, i, c=c, r0=r0, kt=kt, c0=c0, c1=c1,
                      sbank=sbank: t_.matmul(
                    out=sbank[:, c0:c1],
                    lhsT=kbuf[r0:r0 + 64, c, kt * 128:(kt + 1) * 128],
                    rhs=qgb[r0:r0 + 64, c, c0:c1], start=True, stop=True),
                    waits=[negm_ready, v_ready] + banks.bwait(sbank), incs=[(sT, 1)])
                rc.on("s", lambda s, i, kt=kt, c0=c0, c1=c1, sbank=sbank: s.activation(
                    out=B["ptile"][:, kt % 2, c0:c1], in_=sbank[:, c0:c1],
                    func=AF.Exp, bias=B["negM"][:], scale=0.125),
                    waits=[sT.now(), sS.now()], incs=[(sS, 1)])
                banks.bfree(sbank, [sS.now()])
                mspecs = []
                if j >= 0:
                    mspecs.append((j * 128, "mask_lt"))
                if (j + 2) * 128 <= 512 + 128 and (j + 1) * 128 < 512:
                    mspecs.append(((j + 1) * 128, "mask_ge"))
                for (mc0, mkey) in mspecs:
                    rc.on("v", lambda v, i, kt=kt, mc0=mc0, mkey=mkey: v.tensor_tensor(
                        out=B["ptile"][:, kt % 2, mc0:mc0 + 128],
                        in0=B["ptile"][:, kt % 2, mc0:mc0 + 128],
                        in1=B[mkey][:], op=ALU.mult),
                        waits=[sS.now()], incs=[(sV, 1)])
                pw = sV.now() if mspecs else sS.now()
                pv_q.append((kt, c0, c1, pw, ki))
                if len(pv_q) > 1:
                    _emit_local_pv(rc, B, banks, pfE, vt6, qg, h, pv_q.pop(0), sT)
            _emit_local_pv(rc, B, banks, pfE, vt6, qg, h, pv_q.pop(0), sT)
            rc.on("v", lambda v, i: v.tensor_scalar(
                out=B["lrow"][0:1, :], in0=pfE[64:65, 0:512], scalar1=1e-30,
                scalar2=None, op0=ALU.max), waits=[sT.now()], incs=[(sV, 1)])
            rc.on("v", lambda v, i: v.reciprocal(
                out=B["linv"][0:1, :], in_=B["lrow"][0:1, :]),
                waits=[sV.now()], incs=[(sV, 1)])
            rc.on("t", lambda t_, i: t_.matmul(
                out=pfF[0:64, :], lhsT=B["ones_row_f32"][0:1, 0:64],
                rhs=B["linv"][0:1, :], start=True, stop=True),
                waits=[sV.now()] + banks.bwait(pfF), incs=[(sT, 1)])
            rc.on("s", lambda s, i: s.copy(out=B["bcs"][0:64, :], in_=pfF[0:64, :]),
                waits=[sT.now(), sS.now()], incs=[(sS, 1)])
            banks.bfree(pfF, [sS.now()])
            rc.on("v", lambda v, i, c=c, r0=r0: v.tensor_tensor(
                out=localb[r0:r0 + 64, c, :], in0=pfE[0:64, 0:512],
                in1=B["bcs"][0:64, :], op=ALU.mult),
                waits=[sS.now()], incs=[(sV, 1)])
            banks.bfree(pfE, [sV.now()])

        for lh in range(4):
            lc = lh // 2
            lr = (lh % 2) * 64
            sbank = banks.f32[2 + lh % 2]
            rc.on("t", lambda t_, i, lc=lc, lr=lr, qg=qg, sbank=sbank: t_.matmul(
                out=sbank[0:64, :], lhsT=rkf[lr:lr + 64, lc, :],
                rhs=rqb[lr:lr + 64, lc, qg * 512:(qg + 1) * 512],
                start=True, stop=True),
                waits=[negm_ready] + banks.bwait(sbank), incs=[(sT, 1)])
            rc.on("s", lambda s, i, sbank=sbank: s.activation(
                out=B["lpt"][0:64, :], in_=sbank[0:64, :], func=AF.Exp,
                bias=B["negM2"][0:64, :], scale=0.125),
                waits=[sT.now(), sS.now(), sV.now()], incs=[(sS, 1)])
            banks.bfree(sbank, [sS.now()])
            rc.on("v", lambda v, i, qg=qg: v.tensor_tensor(
                out=B["lpt"][0:64, :], in0=B["lpt"][0:64, :],
                in1=B["mask_lat"][0:64, qg * 512:(qg + 1) * 512], op=ALU.mult),
                waits=[sS.now(), B["dCN"].now()], incs=[(sV, 1)])
            rc.on("t", lambda t_, i, lh=lh: t_.matmul(
                out=pfE[0:65, :], lhsT=rvt[0:64, lh, 0:65], rhs=B["lpt"][0:64, :],
                start=True, stop=True),
                waits=[sV.now()] + banks.bwait(pfE), incs=[(sT, 1)])
            rc.on("v", lambda v, i: v.tensor_scalar(
                out=B["lrow"][0:1, :], in0=pfE[64:65, 0:512], scalar1=1e-30,
                scalar2=None, op0=ALU.max), waits=[sT.now()], incs=[(sV, 1)])
            rc.on("v", lambda v, i: v.reciprocal(
                out=B["linv"][0:1, :], in_=B["lrow"][0:1, :]),
                waits=[sV.now()], incs=[(sV, 1)])
            rc.on("t", lambda t_, i: t_.matmul(
                out=pfF[0:64, :], lhsT=B["ones_row_f32"][0:1, 0:64],
                rhs=B["linv"][0:1, :], start=True, stop=True),
                waits=[sV.now()] + banks.bwait(pfF), incs=[(sT, 1)])
            rc.on("s", lambda s, i: s.copy(out=B["bcs"][0:64, :], in_=pfF[0:64, :]),
                waits=[sT.now(), sS.now()], incs=[(sS, 1)])
            banks.bfree(pfF, [sS.now()])
            rc.on("v", lambda v, i, lc=lc, lr=lr: v.tensor_tensor(
                out=latc[lr:lr + 64, lc, :], in0=pfE[0:64, 0:512],
                in1=B["bcs"][0:64, :], op=ALU.mult),
                waits=[sS.now()], incs=[(sV, 1)])
            banks.bfree(pfE, [sV.now()])
        lat_done = sV.now()

        for oc in range(8):
            bank = banks.f32[oc % 2]
            for kc in range(2):
                rc.on("t", lambda t_, i, oc=oc, kc=kc, bank=bank: t_.matmul(
                    out=bank[:], lhsT=B["routslab"][:, kc, oc * 128:(oc + 1) * 128],
                    rhs=latc[:, kc, :], start=(kc == 0), stop=(kc == 1)),
                    waits=([lat_done, B["dCB0"].now()] + banks.bwait(bank)
                           if kc == 0 else []),
                    incs=([(sT, 1)] if kc == 1 else []))
            rc.on("v", lambda v, i, oc=oc, bank=bank: v.tensor_scalar(
                out=remoteb[:, oc, :], in0=bank[:],
                scalar1=B["bias_pp"][:, BPP["l0_rout"] + oc:BPP["l0_rout"] + oc + 1],
                scalar2=None, op0=ALU.add), waits=[sT.now()], incs=[(sV, 1)])
            banks.bfree(bank, [sV.now()])
        rem_done = sV.now()

        for c in range(8):
            gb = banks.f32[5]
            rc.on("t", lambda t_, i, c=c, gb=gb: t_.matmul(
                out=gb[0:2, :], lhsT=B["gldiag"][:], rhs=localb[:, c, :],
                start=True, stop=False),
                waits=[rem_done, B["dCNG"].now()] + banks.bwait(gb), incs=[])
            rc.on("t", lambda t_, i, c=c, gb=gb: t_.matmul(
                out=gb[0:2, :], lhsT=B["grdiag"][:], rhs=remoteb[:, c, :],
                start=False, stop=True), incs=[(sT, 1)])
            rc.on("s", lambda s, i, gb=gb: s.activation(
                out=B["gsig"][0:2, :], in_=gb[0:2, :], func=AF.Exp, scale=-1.0),
                waits=[sT.now(), sV.now()], incs=[(sS, 1)])
            banks.bfree(gb, [sS.now()])
            rc.on("v", lambda v, i: v.tensor_scalar(
                out=B["gsig"][0:2, :], in0=B["gsig"][0:2, :], scalar1=1.0,
                scalar2=None, op0=ALU.add), waits=[sS.now()], incs=[(sV, 1)])
            rc.on("v", lambda v, i: v.reciprocal(
                out=B["gsig"][0:2, :], in_=B["gsig"][0:2, :]),
                waits=[sV.now()], incs=[(sV, 1)])
            gbc = banks.f32[5]
            rc.on("t", lambda t_, i, gbc=gbc: t_.matmul(
                out=gbc[:], lhsT=B["sel2"][:], rhs=B["gsig"][0:2, :],
                start=True, stop=True),
                waits=[sV.now()] + banks.bwait(gbc), incs=[(sT, 1)])
            rc.on("v", lambda v, i, c=c: v.tensor_tensor(
                out=B["scr1"][:], in0=localb[:, c, :], in1=remoteb[:, c, :],
                op=ALU.subtract), waits=[sT.now()], incs=[(sV, 1)])
            rc.on("v", lambda v, i, gbc=gbc: v.tensor_tensor(
                out=B["scr1"][:], in0=B["scr1"][:], in1=gbc[:], op=ALU.mult),
                waits=[sV.now()], incs=[(sV, 1)])
            banks.bfree(gbc, [sV.now()])
            rc.on("v", lambda v, i, c=c: v.tensor_tensor(
                out=localb[:, c, :], in0=B["scr1"][:], in1=remoteb[:, c, :],
                op=ALU.add), waits=[sV.now()], incs=[(sV, 1)])
        mix_done = sV.now()

        rc.on("g", lambda g, i: g.dma_start(
            out=slab[:, 0, :, :], in_=B["dr"]["L0_attn_out_w"]
            .rearrange("(c p) m -> p c m", p=128)),
            waits=[sT.now()], incs=[(B["dSL0"], 16)])
        for tl in range(4):
            tt = qg * 4 + tl
            xslot = tl % 2
            rc.on("y", lambda y, i, tt=tt, xslot=xslot: y.dma_start(
                out=B["xld2"][:, xslot, :], in_=xsrc_fn(i)[tt * 128:(tt + 1) * 128, :]),
                waits=B.pop(("xld2free", xslot), []), incs=[(B["dX2"][xslot], 16)])
            for og in range(2):
                bank = banks.f32[og % 2]
                rc.on("t", lambda t_, i, og=og, bank=bank: t_.matmul(
                    out=bank[:], lhsT=B["ones_row_bf"][0:1, :],
                    rhs=B["bias_row"][0:1, BROW["l0_out"] + og * 512:BROW["l0_out"] + og * 512 + 512],
                    start=True, stop=False),
                    waits=[mix_done, B["dSL0"].now()] + banks.bwait(bank), incs=[])
                for kc in range(8):
                    rc.on("t", lambda t_, i, tl=tl, og=og, kc=kc, bank=bank: t_.matmul(
                        out=bank[:], lhsT=localb[:, kc, tl * 128:(tl + 1) * 128],
                        rhs=slab[:, 0, kc, og * 512:(og + 1) * 512],
                        start=False, stop=(kc == 7)),
                        incs=([(sT, 1)] if kc == 7 else []))
                rc.on("v", lambda v, i, og=og, xslot=xslot, bank=bank: v.tensor_tensor(
                    out=B["yst"][:, xslot, og * 512:(og + 1) * 512], in0=bank[:],
                    in1=B["xld2"][:, xslot, og * 512:(og + 1) * 512], op=ALU.add),
                    waits=[sT.now(), B["dX2"][xslot].now()] + B.pop(("ystfree", xslot), []),
                    incs=[(sV, 1)])
                banks.bfree(bank, [sV.now()])
            rc.on("y", lambda y, i, tt=tt, xslot=xslot: y.dma_start(
                out=xdst_fn(i)[tt * 128:(tt + 1) * 128, :], in_=B["yst"][:, xslot, :]),
                waits=[sV.now()], incs=[(B["dST"][xslot], 16)])
            B[("xld2free", xslot)] = [sV.now()]
            B[("ystfree", xslot)] = [B["dST"][xslot].now()]


def _l0ffn_tq_flush(rc, B, banks, item, sT, sS, sV, sG):
    ig, tl, nic, hslot = item
    pb = banks.bf[tl % 2]
    for j in range(nic):
        rc.on("t", lambda t_, i, j=j, pb=pb, hslot=hslot: t_.transpose(
            out=pb[:, j * 128:(j + 1) * 128],
            in_=B["hq_t"][:, hslot, j * 128:(j + 1) * 128],
            identity=B["ident_bf"][:]),
            waits=([sG.now()] + banks.bwait(pb) if j == 0 else []),
            incs=([(sT, 1)] if j == nic - 1 else []))
    rc.on("s", lambda s, i, ig=ig, tl=tl, nic=nic, pb=pb: s.copy(
        out=B["hqf"][:, ig * 4:ig * 4 + nic, tl * 128:(tl + 1) * 128],
        in_=pb[:, 0:nic * 128].rearrange("p (c q) -> p c q", q=128)),
        waits=[sT.now()], incs=[(sS, 1)])
    banks.bfree(pb, [sS.now()])


def emit_phase_l0ffn(rc, B, xsrc_fn, xdst_fn):
    sV, sS, sT, sG = B["sV"], B["sS"], B["sT"], B["sG"]
    banks = B["banks"]
    xqf = B["xnf"]
    scr = B["scr"]

    lnrdy = []
    for t in range(16):
        emit_ln_tile(rc, B, xsrc_fn, t, t % 2, B["xnq"][:, t % 2, :])
        emit_quant_tile(rc, B, B["xnq"][:, t % 2, :], B["xqt"][:, t % 2, :], 1024,
                        src_wait=[sV.now()], dst_wait=[sT.now()], qsel=t % 2)
        emit_tile_to_feature(rc, B, B["xqt"][:, t % 2, :],
                             xqf[:, 0:8, t * 128:(t + 1) * 128], t % 2)
        B[("xldfree", t % 2)] = B[("xldfree", t % 2)] + [sS.now()]
        lnrdy.append(sS.now())
    ln_done = sS.now()

    for tg in range(4):
        units = [(ig, tl) for ig in range(6) for tl in range(4)]
        tq = []
        for ui, (ig, tl) in enumerate(units):
            w = 512 if ig < 5 else 256
            nic = w // 128
            if tl == 0:
                igs = ig % 2
                rc.on("y", lambda y, i, ig=ig, w=w, igs=igs: y.dma_start(
                    out=B["gslab"][:, igs, :, 0:w],
                    in_=scr["wp_l0g"][:, ig * 512:ig * 512 + w]
                    .rearrange("(c p) m -> p c m", p=128)),
                    waits=B.pop(("gsfree", igs), []),
                    incs=[(B["dGS" if igs == 0 else "dGS1"], 16)])
                rc.on("y", lambda y, i, ig=ig, w=w, igs=igs: y.dma_start(
                    out=B["uslab"][:, igs, :, 0:w],
                    in_=scr["wp_l0u"][:, ig * 512:ig * 512 + w]
                    .rearrange("(c p) m -> p c m", p=128)),
                    waits=B.pop(("usfree", igs), []),
                    incs=[(B["dUS" if igs == 0 else "dUS1"], 16)])
            igs = ig % 2
            tt = tg * 4 + tl
            gb = banks.f32[(ui % 2) * 2]
            ub = banks.f32[(ui % 2) * 2 + 1]
            for kc in range(8):
                rc.on("t", lambda t_, i, tt=tt, kc=kc, w=w, gb=gb, igs=igs: t_.matmul(
                    out=gb[:, 0:w], lhsT=xqf[:, kc, tt * 128:(tt + 1) * 128],
                    rhs=B["gslab"][:, igs, kc, 0:w], start=(kc == 0), stop=(kc == 7)),
                    waits=([B["dGS" if igs == 0 else "dGS1"].now(), lnrdy[tt]]
                           + banks.bwait(gb) if kc == 0 else []),
                    incs=([(sT, 1)] if kc == 7 else []))
            for kc in range(8):
                rc.on("t", lambda t_, i, tt=tt, kc=kc, w=w, ub=ub, igs=igs: t_.matmul(
                    out=ub[:, 0:w], lhsT=xqf[:, kc, tt * 128:(tt + 1) * 128],
                    rhs=B["uslab"][:, igs, kc, 0:w], start=(kc == 0), stop=(kc == 7)),
                    waits=([B["dUS" if igs == 0 else "dUS1"].now()]
                           + banks.bwait(ub) if kc == 0 else []),
                    incs=([(sT, 1)] if kc == 7 else []))
            if tl == 3:
                B[("gsfree", igs)] = [sT.now()]
                B[("usfree", igs)] = [sT.now()]
            if len(tq) > 2:
                _l0ffn_tq_flush(rc, B, banks, tq.pop(0), sT, sS, sV, sG)
            rc.on("s", lambda s, i, w=w, gb=gb, ui=ui: s.activation(
                out=B["hstage"][:, ui % 2, 0:w], in_=gb[:, 0:w], func=AF.Silu),
                waits=[sT.now(), sV.now()], incs=[(sS, 1)])
            banks.bfree(gb, [sS.now()])
            rc.on("v", lambda v, i, w=w, ub=ub, ui=ui: v.tensor_tensor(
                out=B["xnq"][:, ui % 2, 0:w], in0=B["hstage"][:, ui % 2, 0:w],
                in1=ub[:, 0:w], op=ALU.mult), waits=[sS.now()], incs=[(sV, 1)])
            banks.bfree(ub, [sV.now()])
            hslot = ui % 3
            emit_quant_tile(rc, B, B["xnq"][:, ui % 2, 0:w], B["hq_t"][:, hslot, 0:w],
                            w, src_wait=[sV.now()], dst_wait=[sT.now()], qsel=ui % 2)
            tq.append((ig, tl, nic, hslot))
        while tq:
            _l0ffn_tq_flush(rc, B, banks, tq.pop(0), sT, sS, sV, sG)
        hq_done = sS.now()

        dwn = ([(banks.f32[j], banks.f32[j][:]) for j in range(6)]
               + [(bb, bb[:].bitcast(F32)) for bb in banks.bf])
        for bko, _ in dwn:
            for wsp in banks.bwait(bko):
                rc.on("t", None, waits=[wsp])
        for og in range(2):
            rc.on("y", lambda y, i, og=og: y.dma_start(
                out=B["dslab"][:],
                in_=scr["wp_l0d"][:, og * 512:(og + 1) * 512]
                .rearrange("(c p) m -> p c m", p=128)),
                waits=B.pop(("dsfree2",), []), incs=[(B["dDS"], 16)])
            for tl in range(4):
                bk = dwn[tl * 2 + og][1]
                for kc in range(22):
                    rc.on("t", lambda t_, i, tl=tl, kc=kc, bk=bk: t_.matmul(
                        out=bk, lhsT=B["hqf"][:, kc, tl * 128:(tl + 1) * 128],
                        rhs=B["dslab"][:, kc, :], start=(kc == 0), stop=(kc == 21)),
                        waits=([hq_done, B["dDS"].now()] if kc == 0 else []),
                        incs=([(sT, 1)] if kc == 21 else []))
            B[("dsfree2",)] = [sT.now()]
        down_done = sT.now()
        for tl in range(4):
            ttg = tg * 4 + tl
            xslot = tl % 2
            rc.on("y", lambda y, i, ttg=ttg, xslot=xslot: y.dma_start(
                out=B["xld2"][:, xslot, :], in_=xsrc_fn(i)[ttg * 128:(ttg + 1) * 128, :]),
                waits=B.pop(("xld2free", xslot), []), incs=[(B["dX2"][xslot], 16)])
            for og in range(2):
                bko, bk = dwn[tl * 2 + og]
                rc.on("v", lambda v, i, og=og, xslot=xslot, bk=bk: v.tensor_tensor(
                    out=B["yst"][:, xslot, og * 512:(og + 1) * 512], in0=bk,
                    in1=B["xld2"][:, xslot, og * 512:(og + 1) * 512], op=ALU.add),
                    waits=[down_done, B["dX2"][xslot].now()] + B.pop(("ystfree", xslot), []),
                    incs=[(sV, 1)])
                banks.bfree(bko, [sV.now()])
            rc.on("y", lambda y, i, ttg=ttg, xslot=xslot: y.dma_start(
                out=xdst_fn(i)[ttg * 128:(ttg + 1) * 128, :], in_=B["yst"][:, xslot, :]),
                waits=[sV.now()], incs=[(B["dST"][xslot], 16)])
            B[("xld2free", xslot)] = [sV.now()]
            B[("ystfree", xslot)] = [B["dST"][xslot].now()]


def _in_maps(inputs, phases):
    consts = host_consts()
    bias_pp = _bias_pp_host(inputs)
    bias_row = _bias_row_host(inputs)
    x_full = np.ascontiguousarray(np.asarray(inputs["L0_attn_x"], dtype=np.float32))
    maps = []
    for c in range(8):
        m = {"x": x_full[c * NSEQ:(c + 1) * NSEQ]}
        for k, v in inputs.items():
            if k != "L0_attn_x":
                m[k] = np.ascontiguousarray(np.asarray(v, dtype=np.float32))
        for k, v in consts.items():
            m[k] = v
        m["c_bias_pp"] = bias_pp
        m["c_bias_row"] = bias_row
        maps.append(m)
    return maps


_PHASES_ALL = ["l0attn", "l0ffn", "l1attn", "l1ffn"]


def kernel(**inputs):
    nc = build_program(_PHASES_ALL)
    maps = _in_maps(inputs, _PHASES_ALL)
    res = run_bass_kernel_spmd(nc, maps, core_ids=list(range(8)))
    return np.concatenate([r["y"] for r in res.results], axis=0)


# version 07
# speedup vs baseline: 1.3218x; 1.0084x over previous
import contextlib
import numpy as np
import ml_dtypes
import concourse.bass as bass
import concourse.mybir as mybir
from concourse.bass_utils import run_bass_kernel_spmd

F32 = mybir.dt.float32
BF16 = mybir.dt.bfloat16
AF = mybir.ActivationFunctionType
ALU = mybir.AluOpType
AX = mybir.AxisListType

NSEQ = 4
S = 2048
D = 1024
NH = 16
HD = 64
NLAT = 64
CHUNK = 32
LHEADS = 4
IFF = 2816
NEXP = 16
MIDR = 256
MIDS = 512
RNE_C = 12582912.0

ENGS = ("t", "v", "s", "g", "y")


def host_consts():
    bf = ml_dtypes.bfloat16
    c = {}
    c["c_ident_bf"] = np.eye(128, dtype=bf)
    c["c_ident_f32"] = np.eye(128, dtype=np.float32)
    c["c_ones_row_bf"] = np.ones((1, 128), dtype=bf)
    c["c_ones_row_f32"] = np.ones((1, 128), dtype=np.float32)
    c["c_ones_col_bf"] = np.ones((128, 1), dtype=bf)
    hm = np.zeros((128, 2), dtype=bf)
    hm[0:64, 0] = 1
    hm[64:128, 1] = 1
    c["c_headmask"] = hm
    s2 = np.zeros((2, 128), dtype=np.float32)
    s2[0, 0:64] = 1
    s2[1, 64:128] = 1
    c["c_sel2"] = s2
    s16 = np.zeros((16, 16 * 128), dtype=bf)
    for e in range(16):
        s16[e, e * 128:(e + 1) * 128] = 1
    c["c_sel16"] = s16
    i_idx = np.arange(128)
    c["c_mask_k_lt_q"] = (i_idx[:, None] < i_idx[None, :]).astype(bf)
    c["c_mask_k_ge_q"] = (i_idx[:, None] >= i_idx[None, :]).astype(bf)
    c["c_mask_k_le_q"] = (i_idx[:, None] <= i_idx[None, :]).astype(bf)
    t_idx = np.arange(S)
    c["c_mask_lat"] = (np.arange(NLAT)[:, None] < (t_idx // CHUNK)[None, :]).astype(bf)
    return c


class RSem:
    __slots__ = ("name", "base", "q", "cur", "handle", "rec")

    def __init__(self, rec, name):
        self.rec = rec
        self.name = name
        self.base = 0
        self.q = 0
        self.cur = 0
        self.handle = None

    def _bump(self, n):
        if self.rec.seg == "prep":
            self.base += n
        else:
            self.cur += n

    def now(self, dit=0):
        if self.rec.seg == "prep":
            return ("abs", self, self.base)
        if self.rec.seg == "epi":
            return ("epi", self, self.cur)
        return ("lin", self, dit, self.cur)


class Rec:
    def __init__(self, nc, nseq=NSEQ):
        self.nc = nc
        self.nseq = nseq
        self.seg = "prep"
        self.cmds = {seg: {e: [] for e in ENGS} for seg in ("prep", "body", "epi")}
        self.sems = []
        self.ctx = contextlib.ExitStack()

    def sem(self, name):
        s = RSem(self, name)
        self.sems.append(s)
        return s

    def begin_body(self):
        self.seg = "body"
        for s in self.sems:
            s.cur = 0

    def end_body(self):
        for s in self.sems:
            s.q = s.cur
            s.cur = 0
        self.seg = "epi"

    def _resolve(self, spec, i):
        kind = spec[0]
        if kind == "abs":
            return spec[2]
        if kind == "epi":
            _, sem, const = spec
            return sem.base + self.nseq * sem.q + const
        _, sem, dit, const = spec
        flat = sem.base + const + dit * sem.q
        if sem.q == 0:
            return flat
        assert flat >= 0, (sem.name, dit, const, sem.base, sem.q)
        return i * sem.q + flat

    def on(self, e, fn, waits=(), incs=()):
        seg = self.seg
        wspecs = list(waits)
        ispecs = [(s, n) for (s, n) in incs]
        for s, n in ispecs:
            s._bump(n)
        rec = self

        def cmd(eng, i, regctx, fn=fn, wspecs=wspecs, ispecs=ispecs):
            for spec in wspecs:
                if spec[0] == "lin" and spec[1].q != 0 and regctx is not None:
                    sem, dit, const = spec[1], spec[2], spec[3]
                    flat = sem.base + const + dit * sem.q
                    assert flat >= 0, (sem.name, dit, const, sem.base, sem.q)
                    regctx.wait(eng, sem, flat)
                else:
                    eng.wait_ge(spec[1].handle, rec._resolve(spec, i))
            ins = fn(eng, i) if fn is not None else None
            if ispecs:
                if ins is None:
                    ins = eng.nop()
                ins = ins.then_inc(ispecs[0][0].handle, ispecs[0][1])
                for s, n in ispecs[1:]:
                    eng.nop().then_inc(s.handle, n)
            return ins

        self.cmds[seg][e].append(cmd)

    def build(self, block):
        for s in self.sems:
            s.handle = self.ctx.enter_context(self.nc.semaphore(s.name))
        for e in ENGS:
            if not (self.cmds["prep"][e] or self.cmds["body"][e] or self.cmds["epi"][e]):
                continue
            prep = self.cmds["prep"][e]
            body = self.cmds["body"][e]
            epi = self.cmds["epi"][e]
            dec = getattr(block, {"t": "tensor", "v": "vector", "s": "scalar",
                                  "g": "gpsimd", "y": "sync"}[e])

            def run(eng, prep=prep, body=body, epi=epi):
                for cmd in prep:
                    cmd(eng, 0, None)
                if body:
                    with eng.Fori(0, self.nseq) as i:
                        regctx = _RegCtx(eng, i)
                        for cmd in body:
                            cmd(eng, i, regctx)
                for cmd in epi:
                    cmd(eng, 0, None)

            dec(run)


class _RegCtx:

    def __init__(self, eng, i):
        self.eng = eng
        self.i = i
        self.ireg = eng.to_reg(i)
        self.scratch = eng.alloc_register("waitval")
        self.last = None

    def wait(self, eng, sem, flat):
        key = (id(sem), sem.q, flat)
        if self.last != key:
            eng.reg_mul(self.scratch, self.ireg, sem.q)
            eng.reg_add(self.scratch, self.scratch, flat)
            self.last = key
        eng.wait_ge(sem.handle, self.scratch)


def emit_phase_l1attn(rc, B, xsrc_fn, xdst_fn):
    sV, sS, sT, sG = B["sV"], B["sS"], B["sT"], B["sG"]
    banks = B["banks"]
    pfE, pfF = banks.f32[4], banks.f32[5]
    xnf = B["xnf"]
    kbuf = B["kbuf"]
    vt = B["vt"]
    qgb = B["qgb"]
    ctxb = B["ctxb"]
    slab = B["slab"]
    wsrc = B["wp_l1qkvo"]

    rc.on("v", lambda v, i: v.memset(vt[:, :, :, 64:65], 1.0), incs=[(sV, 1)])

    for t in range(16):
        emit_ln_tile(rc, B, xsrc_fn, t, t % 2, B["xnt"][:, t % 2, :])
        emit_tile_to_feature(rc, B, B["xnt"][:, t % 2, :],
                             xnf[:, 0:8, t * 128:(t + 1) * 128], t % 2)
        B[("xldfree", t % 2)] = B[("xldfree", t % 2)] + [sS.now()]
        B[("lnrdy_l1", t)] = sS.now()

    wnames = ["L1_attn_Wq", "L1_attn_Wk", "L1_attn_Wv", "L1_attn_Wo"]

    def slab_load(slot, widx, waitspecs):
        rc.on("g", lambda g, i, slot=slot, widx=widx: g.dma_start(
            out=slab[:, slot, :, :],
            in_=B["dr"][wnames[widx]].rearrange("(c p) m -> p c m", p=128)),
            waits=waitspecs, incs=[(B["dSL"][slot], 16)])

    slab_load(0, 1, [])
    slab_load(1, 2, [])
    wk_loaded = B["dSL0"].now()

    wv_loaded = B["dSL1"].now()
    for tt in range(16):
        for og in range(2):
            bank = banks.f32[(tt * 2 + og) % 2]
            rc.on("t", lambda t_, i, og=og, bank=bank: t_.matmul(
                out=bank[:], lhsT=B["ones_row_bf"][0:1, :],
                rhs=B["bias_row"][0:1, BROW["l1_bv"] + og * 512:BROW["l1_bv"] + og * 512 + 512],
                start=True, stop=False),
                waits=[wv_loaded, B[("lnrdy_l1", tt)]] + banks.bwait(bank), incs=[])
            for kc in range(8):
                rc.on("t", lambda t_, i, tt=tt, og=og, kc=kc, bank=bank: t_.matmul(
                    out=bank[:], lhsT=xnf[:, kc, tt * 128:(tt + 1) * 128],
                    rhs=slab[:, 1, kc, og * 512:(og + 1) * 512],
                    start=False, stop=(kc == 7)),
                    incs=([(sT, 1)] if kc == 7 else []))
            rc.on("s", lambda s, i, tt=tt, og=og, bank=bank: s.copy(
                out=vt[:, tt, og * 8:(og + 1) * 8, 0:64],
                in_=bank[:].rearrange("p (h d) -> p h d", d=64)),
                waits=[sT.now()], incs=[(sS, 1)])
            banks.bfree(bank, [sS.now()])
    v_done = sS.now()

    for oc in range(8):
        for kc in range(8):
            for tg in range(4):
                bank = banks.f32[tg]
                rc.on("t", lambda t_, i, oc=oc, tg=tg, kc=kc, bank=bank: t_.matmul(
                    out=bank[:], lhsT=slab[:, 0, kc, oc * 128:(oc + 1) * 128],
                    rhs=xnf[:, kc, tg * 512:(tg + 1) * 512],
                    start=(kc == 0), stop=(kc == 7)),
                    waits=([wk_loaded, sS.now()] + banks.bwait(bank)
                           if kc == 0 else []),
                    incs=([(sT, 1)] if kc == 7 and tg == 3 else []))
        for tg in range(4):
            bank = banks.f32[tg]
            rc.on("v", lambda v, i, oc=oc, tg=tg, bank=bank: v.tensor_scalar(
                out=kbuf[:, oc, tg * 512:(tg + 1) * 512], in0=bank[:],
                scalar1=B["bias_pp"][:, BPP["l1_bk"] + oc:BPP["l1_bk"] + oc + 1],
                scalar2=None, op0=ALU.add),
                waits=([sT.now()] if tg == 0 else []), incs=[(sV, 1)])
            banks.bfree(bank, [sV.now()])
    kq_done = sV.now()

    for oc in range(8):
        rc.on("s", lambda s, i, oc=oc: s.activation(
            out=B["sqscr"][:], in_=kbuf[:, oc, :], func=AF.Square),
            waits=[kq_done, sT.now(), sS.now()], incs=[(sS, 1)])
        for tg in range(4):
            bank = banks.f32[2 + (oc * 4 + tg) % 2]
            rc.on("t", lambda t_, i, oc=oc, tg=tg, bank=bank: t_.matmul(
                out=bank[0:2, :], lhsT=B["headmask"][:],
                rhs=B["sqscr"][:, tg * 512:(tg + 1) * 512], start=True, stop=True),
                waits=[sS.now()] + banks.bwait(bank), incs=[(sT, 1)])
            rc.on("v", lambda v, i, oc=oc, tg=tg, bank=bank: v.tensor_reduce(
                out=B["nrm"][:, oc * 4 + tg:oc * 4 + tg + 1], in_=bank[0:2, :],
                axis=AX.X, op=ALU.max), waits=[sT.now()], incs=[(sV, 1)])
            banks.bfree(bank, [sV.now()])
    rc.on("v", lambda v, i: v.tensor_reduce(
        out=B["kn2a"][0:2, :], in_=B["nrm"][:, 0:32], axis=AX.X, op=ALU.max),
        waits=[sV.now()], incs=[(sV, 1)])
    rc.on("t", lambda t_, i: t_.transpose(
        out=pfF[0:1, 0:2], in_=B["kn2a"][0:2, 0:1], identity=B["ident_f32"][0:2, 0:2]),
        waits=[sV.now()] + banks.bwait(pfF), incs=[(sT, 1)])
    rc.on("v", lambda v, i: v.tensor_reduce(
        out=B["kn2"][0:1, :], in_=pfF[0:1, 0:2], axis=AX.X, op=ALU.max),
        waits=[sT.now()], incs=[(sV, 1)])
    banks.bfree(pfF, [sV.now()])

    slab_load(0, 0, [kq_done, sT.now()])
    wq_loaded = B["dSL0"].now()
    slab_load(1, 3, [v_done, sT.now()])
    wo_loaded = B["dSL1"].now()

    for qg in range(4):
        for oc in range(8):
            bank = banks.f32[oc % 2]
            for kc in range(8):
                rc.on("t", lambda t_, i, oc=oc, kc=kc, qg=qg, bank=bank: t_.matmul(
                    out=bank[:], lhsT=slab[:, 0, kc, oc * 128:(oc + 1) * 128],
                    rhs=xnf[:, kc, qg * 512:(qg + 1) * 512],
                    start=(kc == 0), stop=(kc == 7)),
                    waits=([wq_loaded, sV.now()] + banks.bwait(bank) if kc == 0 else []),
                    incs=([(sT, 1)] if kc == 7 else []))
            rc.on("v", lambda v, i, oc=oc, bank=bank: v.tensor_scalar(
                out=qgb[:, oc, :], in0=bank[:],
                scalar1=B["bias_pp"][:, BPP["l1_bq"] + oc:BPP["l1_bq"] + oc + 1],
                scalar2=None, op0=ALU.add), waits=[sT.now()], incs=[(sV, 1)])
            banks.bfree(bank, [sV.now()])
        q_done = sV.now()
        for oc in range(8):
            rc.on("s", lambda s, i, oc=oc: s.activation(
                out=B["sqscr"][:, 0:512], in_=qgb[:, oc, :], func=AF.Square),
                waits=[q_done, sT.now(), sS.now()], incs=[(sS, 1)])
            bank = banks.f32[2 + oc % 2]
            rc.on("t", lambda t_, i, oc=oc, bank=bank: t_.matmul(
                out=bank[0:2, :], lhsT=B["headmask"][:], rhs=B["sqscr"][:, 0:512],
                start=True, stop=True), waits=[sS.now()] + banks.bwait(bank),
                incs=[(sT, 1)])
            rc.on("v", lambda v, i, oc=oc, bank=bank: v.tensor_reduce(
                out=B["nrm"][:, 32 + oc:33 + oc], in_=bank[0:2, :], axis=AX.X,
                op=ALU.max), waits=[sT.now()], incs=[(sV, 1)])
            banks.bfree(bank, [sV.now()])
        rc.on("v", lambda v, i: v.tensor_reduce(
            out=B["kn2a"][0:2, :], in_=B["nrm"][:, 32:40], axis=AX.X, op=ALU.max),
            waits=[sV.now()], incs=[(sV, 1)])
        rc.on("t", lambda t_, i: t_.transpose(
            out=pfF[0:1, 2:4], in_=B["kn2a"][0:2, 0:1], identity=B["ident_f32"][0:2, 0:2]),
            waits=[sV.now()] + banks.bwait(pfF), incs=[(sT, 1)])
        rc.on("v", lambda v, i: v.tensor_reduce(
            out=B["qn2"][0:1, :], in_=pfF[0:1, 2:4], axis=AX.X, op=ALU.max),
            waits=[sT.now()], incs=[(sV, 1)])
        banks.bfree(pfF, [sV.now()])
        rc.on("v", lambda v, i: v.tensor_tensor(
            out=B["pp1"][0:1, :], in0=B["qn2"][0:1, :], in1=B["kn2"][0:1, :],
            op=ALU.mult), waits=[sV.now()], incs=[(sV, 1)])
        rc.on("s", lambda s, i: s.activation(
            out=B["pp1"][0:1, :], in_=B["pp1"][0:1, :], func=AF.Ln,
            bias=B["tiny"][0:1, :]), waits=[sV.now()], incs=[(sS, 1)])
        rc.on("s", lambda s, i: s.activation(
            out=B["pp2"][0:1, :], in_=B["pp1"][0:1, :], func=AF.Exp, scale=0.5),
            waits=[sS.now()], incs=[(sS, 1)])
        rc.on("v", lambda v, i: v.tensor_scalar(
            out=B["pp2f"][0:1, :], in0=B["pp2"][0:1, :], scalar1=-0.125,
            scalar2=None, op0=ALU.mult), waits=[sS.now()], incs=[(sV, 1)])
        rc.on("t", lambda t_, i: t_.matmul(
            out=pfF[:, 0:1], lhsT=B["ones_row_f32"][0:1, :], rhs=B["pp2f"][0:1, 0:1],
            start=True, stop=True),
            waits=[sV.now()] + banks.bwait(pfF), incs=[(sT, 1)])
        rc.on("s", lambda s, i: s.copy(out=B["negM"][:], in_=pfF[:, 0:1]),
            waits=[sT.now()], incs=[(sS, 1)])
        banks.bfree(pfF, [sS.now()])
        negm_ready = sS.now()

        nkt = 4 * qg + 4
        norm_q = []

        def l1_norm_tail(item):
            h2, c2, r2, ctxbank2 = item
            rc.on("t", lambda t_, i: t_.matmul(
                out=pfF[0:64, :], lhsT=B["ones_row_f32"][0:1, 0:64],
                rhs=B["linv"][0:1, :], start=True, stop=True),
                waits=[sV.now()] + banks.bwait(pfF), incs=[(sT, 1)])
            rc.on("s", lambda s, i: s.copy(out=B["bcs"][0:64, :], in_=pfF[0:64, :]),
                waits=[sT.now(), sS.now()], incs=[(sS, 1)])
            banks.bfree(pfF, [sS.now()])
            rc.on("v", lambda v, i, c=c2, r0=r2, ctxbank=ctxbank2: v.tensor_tensor(
                out=ctxb[r0:r0 + 64, c, :], in0=ctxbank[0:64, 0:512],
                in1=B["bcs"][0:64, :], op=ALU.mult),
                waits=[sS.now()], incs=[(sV, 1)])
            banks.bfree(ctxbank2, [sV.now()])

        for h in range(16):
            c = h // 2
            r0 = (h % 2) * 64
            ctxbank = banks.f32[2 + h % 2]
            pv_q = []
            for kt in range(nkt):
                joff = max(0, kt * 128 - qg * 512)
                sbank = banks.f32[kt % 2]
                rc.on("t", lambda t_, i, c=c, r0=r0, kt=kt, joff=joff, qg=qg,
                      sbank=sbank: t_.matmul(
                    out=sbank[:, joff:512],
                    lhsT=kbuf[r0:r0 + 64, c, kt * 128:(kt + 1) * 128],
                    rhs=qgb[r0:r0 + 64, c, joff:512], start=True, stop=True),
                    waits=[negm_ready] + banks.bwait(sbank), incs=[(sT, 1)])
                rc.on("s", lambda s, i, kt=kt, joff=joff, sbank=sbank: s.activation(
                    out=B["ptile"][:, kt % 2, joff:512], in_=sbank[:, joff:512],
                    func=AF.Exp, bias=B["negM"][:], scale=0.125),
                    waits=[sT.now(), sS.now()], incs=[(sS, 1)])
                banks.bfree(sbank, [sS.now()])
                if kt >= 4 * qg:
                    rc.on("v", lambda v, i, kt=kt, joff=joff: v.tensor_tensor(
                        out=B["ptile"][:, kt % 2, joff:joff + 128],
                        in0=B["ptile"][:, kt % 2, joff:joff + 128],
                        in1=B["mask_le"][:], op=ALU.mult),
                        waits=[sS.now()], incs=[(sV, 1)])
                    pw = sV.now()
                else:
                    pw = sS.now()
                pv_q.append((kt, joff, pw))
                if kt == 1 and norm_q:
                    l1_norm_tail(norm_q.pop(0))
                if len(pv_q) > 1:
                    k2, j2, pw2 = pv_q.pop(0)
                    rc.on("t", lambda t_, i, h=h, kt=k2, joff=j2, nkt=nkt,
                          ctxbank=ctxbank: t_.matmul(
                        out=ctxbank[0:65, joff:512], lhsT=vt[:, kt, h, 0:65],
                        rhs=B["ptile"][:, kt % 2, joff:512],
                        start=(kt == 0), stop=(kt == nkt - 1)),
                        waits=[pw2] + (banks.bwait(ctxbank) if k2 == 0 else []),
                        incs=[(sT, 1)])
            k2, j2, pw2 = pv_q.pop(0)
            rc.on("t", lambda t_, i, h=h, kt=k2, joff=j2, nkt=nkt,
                  ctxbank=ctxbank: t_.matmul(
                out=ctxbank[0:65, joff:512], lhsT=vt[:, kt, h, 0:65],
                rhs=B["ptile"][:, kt % 2, joff:512],
                start=(kt == 0), stop=(kt == nkt - 1)),
                waits=[pw2] + (banks.bwait(ctxbank) if k2 == 0 else []),
                incs=[(sT, 1)])
            rc.on("v", lambda v, i, ctxbank=ctxbank: v.tensor_scalar(
                out=B["lrow"][0:1, :], in0=ctxbank[64:65, 0:512], scalar1=1e-30,
                scalar2=None, op0=ALU.max), waits=[sT.now(), sV.now()], incs=[(sV, 1)])
            rc.on("v", lambda v, i: v.reciprocal(
                out=B["linv"][0:1, :], in_=B["lrow"][0:1, :]),
                waits=[sV.now()], incs=[(sV, 1)])
            norm_q.append((h, c, r0, ctxbank))
        l1_norm_tail(norm_q.pop(0))

        ctx_done = sV.now()
        for tl in range(4):
            tt = qg * 4 + tl
            xslot = tl % 2
            rc.on("y", lambda y, i, tt=tt, xslot=xslot: y.dma_start(
                out=B["xld2"][:, xslot, :], in_=xsrc_fn(i)[tt * 128:(tt + 1) * 128, :]),
                waits=B.pop(("xld2free", xslot), []), incs=[(B["dX2"][xslot], 16)])
            for og in range(2):
                bank = banks.f32[og % 2]
                rc.on("t", lambda t_, i, og=og, bank=bank: t_.matmul(
                    out=bank[:], lhsT=B["ones_row_bf"][0:1, :],
                    rhs=B["bias_row"][0:1, BROW["l1_bo"] + og * 512:BROW["l1_bo"] + og * 512 + 512],
                    start=True, stop=False),
                    waits=[ctx_done, wo_loaded] + banks.bwait(bank), incs=[])
                for kc in range(8):
                    rc.on("t", lambda t_, i, tl=tl, og=og, kc=kc, bank=bank: t_.matmul(
                        out=bank[:], lhsT=ctxb[:, kc, tl * 128:(tl + 1) * 128],
                        rhs=slab[:, 1, kc, og * 512:(og + 1) * 512],
                        start=False, stop=(kc == 7)),
                        incs=([(sT, 1)] if kc == 7 else []))
                rc.on("v", lambda v, i, og=og, xslot=xslot, bank=bank: v.tensor_tensor(
                    out=B["yst"][:, xslot, og * 512:(og + 1) * 512], in0=bank[:],
                    in1=B["xld2"][:, xslot, og * 512:(og + 1) * 512], op=ALU.add),
                    waits=[sT.now(), B["dX2"][xslot].now()] + B.pop(("ystfree", xslot), []),
                    incs=[(sV, 1)])
                banks.bfree(bank, [sV.now()])
            rc.on("y", lambda y, i, tt=tt, xslot=xslot: y.dma_start(
                out=xdst_fn(i)[tt * 128:(tt + 1) * 128, :], in_=B["yst"][:, xslot, :]),
                waits=[sV.now()], incs=[(B["dST"][xslot], 16)])
            B[("xld2free", xslot)] = [sV.now()]
            B[("ystfree", xslot)] = [B["dST"][xslot].now()]


class Arena:

    def __init__(self, nc, ctx, cols_f32):
        self.t = ctx.enter_context(nc.sbuf_tensor("arena", [128, cols_f32], F32))
        self.cols = cols_f32
        self.cur = 0
        self.mark = 0
        self.hwm = 0

    def persist_done(self):
        self.mark = self.cur

    def phase_reset(self):
        self.cur = self.mark

    @property
    def cur_bf(self):
        return self.cur

    @cur_bf.setter
    def cur_bf(self, v):
        self.cur = v

    @property
    def cur_f32(self):
        return self.cur

    @cur_f32.setter
    def cur_f32(self, v):
        pass

    def view(self, shape, dt=BF16, parts=128):
        n = int(np.prod(shape))
        nf32 = (n + 1) // 2 if dt == BF16 else n
        off = self.cur
        self.cur = off + nf32
        self.hwm = max(self.hwm, self.cur)
        assert self.cur <= self.cols, f"arena overflow {self.cur} > {self.cols}"
        ap = self.t[0:parts, off:off + nf32]
        if dt == BF16:
            ap = ap.bitcast(BF16)
            if n % 2:
                ap = ap[:, 0:n]
        if len(shape) > 1:
            letters = "abcdef"[: len(shape)]
            kw = {letters[j]: shape[j] for j in range(1, len(shape))}
            ap = ap.rearrange("p (" + " ".join(letters) + ") -> p " + " ".join(letters), **kw)
        return ap


class PBanks:

    def __init__(self, nc, ctx, rc):
        self.f32 = [ctx.enter_context(nc.psum_tensor(f"pb{j}", [128, 512], F32))
                    for j in range(6)]
        self.bf = [ctx.enter_context(nc.psum_tensor(f"pt{j}", [128, 1024], BF16))
                   for j in range(2)]
        self.guard = {}

    def bwait(self, bank):
        return list(self.guard.get(id(bank), []))

    def bfree(self, bank, specs):
        self.guard[id(bank)] = list(specs)


def _bias_pp_host(I):
    cols = []
    def chunks(v):
        return [v[c * 128:(c + 1) * 128] for c in range(len(v) // 128)]
    cols += chunks(I["L1_attn_bq"])
    cols += chunks(I["L1_attn_bk"])
    cols += chunks(I["L0_attn_qkv_b"][0:1024])
    cols += chunks(I["L0_attn_qkv_b"][1024:2048])
    cols += chunks(I["L0_attn_rq_b"])
    cols += chunks(I["L0_attn_rkv_b"][0:256])
    cols += chunks(I["L0_attn_rout_b"])
    return np.stack(cols, axis=1).astype(np.float32)

BPP = dict(l1_bq=0, l1_bk=8, l0_q=16, l0_k=24, l0_rq=32, l0_rk=34, l0_rout=36)

def _bias_row_host(I):
    v = np.concatenate([
        I["L1_attn_bv"],
        I["L1_attn_bo"],
        I["L0_attn_qkv_b"][2048:3072],
        I["L0_attn_out_b"],
        I["L0_attn_rkv_b"][256:512],
    ]).astype(np.float32)
    return v[None, :]

BROW = dict(l1_bv=0, l1_bo=1024, l0_v=2048, l0_out=3072, l0_rv=4096)
NROW = 4352

PHASE_IO = {
    "l0attn": ("x", "xres"),
    "l0ffn": ("xres", "xres"),
    "l1attn": ("xres", "xres"),
    "l1ffn": ("xres", "y"),
}


def build_program(phases, dev_io=False, nseq=NSEQ):
    nc = bass.Bass("TRN2", target_bir_lowering=False, debug=False)
    dr = {}
    dr["x"] = nc.dram_tensor("x", [NSEQ, S, D], F32, kind="ExternalInput").ap()
    dr["y"] = nc.dram_tensor("y", [NSEQ, S, D], F32, kind="ExternalOutput").ap()
    dr["xres"] = nc.dram_tensor("xres", [NSEQ, S, D], F32, kind="Internal").ap()
    wnames = [
        ("L0_attn_qkv_w", [D, 3 * D]), ("L0_attn_qkv_b", [3 * D]),
        ("L0_attn_rq_w", [D, 256]), ("L0_attn_rq_b", [256]),
        ("L0_attn_rkv_w", [D, 512]), ("L0_attn_rkv_b", [512]),
        ("L0_attn_rout_w", [256, D]), ("L0_attn_rout_b", [D]),
        ("L0_attn_out_w", [D, D]), ("L0_attn_out_b", [D]),
        ("L0_attn_gl_w", [64, 1]), ("L0_attn_gr_w", [64, 1]),
        ("L0_ffn_w_gate", [IFF, D]), ("L0_ffn_w_up", [IFF, D]),
        ("L0_ffn_w_down", [D, IFF]),
        ("L1_attn_Wq", [D, D]), ("L1_attn_bq", [D]),
        ("L1_attn_Wk", [D, D]), ("L1_attn_bk", [D]),
        ("L1_attn_Wv", [D, D]), ("L1_attn_bv", [D]),
        ("L1_attn_Wo", [D, D]), ("L1_attn_bo", [D]),
        ("L1_ffn_router_w", [D, NEXP]),
        ("L1_ffn_wg_r", [NEXP, D, MIDR]), ("L1_ffn_wu_r", [NEXP, D, MIDR]),
        ("L1_ffn_wd_r", [NEXP, MIDR, D]),
        ("L1_ffn_wg_s", [2, D, MIDS]), ("L1_ffn_wu_s", [2, D, MIDS]),
        ("L1_ffn_wd_s", [2, MIDS, D]),
    ]
    for nm, sh in wnames:
        dr[nm] = nc.dram_tensor(nm, sh, F32, kind="ExternalInput").ap()
    cvals = host_consts()
    for nm, arr in cvals.items():
        dt = BF16 if arr.dtype == ml_dtypes.bfloat16 else F32
        dr[nm] = nc.dram_tensor(nm, list(arr.shape), dt, kind="ExternalInput").ap()
    dr["c_bias_pp"] = nc.dram_tensor("c_bias_pp", [128, 44], F32, kind="ExternalInput").ap()
    dr["c_bias_row"] = nc.dram_tensor("c_bias_row", [1, NROW], F32, kind="ExternalInput").ap()
    scr = {}
    scr["wp_l1qkvo"] = nc.dram_tensor("wp_l1qkvo", [4, D, D], BF16, kind="Internal").ap()
    scr["wp_l0qkv"] = nc.dram_tensor("wp_l0qkv", [D, 3 * D], BF16, kind="Internal").ap()
    scr["wp_l0out"] = nc.dram_tensor("wp_l0out", [D, D], BF16, kind="Internal").ap()
    scr["wp_l0rq"] = nc.dram_tensor("wp_l0rq", [D, 256], BF16, kind="Internal").ap()
    scr["wp_l0rkv"] = nc.dram_tensor("wp_l0rkv", [D, 512], BF16, kind="Internal").ap()
    scr["wp_l0rout"] = nc.dram_tensor("wp_l0rout", [256, D], BF16, kind="Internal").ap()
    scr["wp_l0g"] = nc.dram_tensor("wp_l0g", [D, IFF], BF16, kind="Internal").ap()
    scr["wp_l0u"] = nc.dram_tensor("wp_l0u", [D, IFF], BF16, kind="Internal").ap()
    scr["wp_l0d"] = nc.dram_tensor("wp_l0d", [IFF, D], BF16, kind="Internal").ap()
    scr["wp_l1rg"] = nc.dram_tensor("wp_l1rg", [NEXP, D, MIDR], BF16, kind="Internal").ap()
    scr["wp_l1ru"] = nc.dram_tensor("wp_l1ru", [NEXP, D, MIDR], BF16, kind="Internal").ap()
    scr["wp_l1rd"] = nc.dram_tensor("wp_l1rd", [NEXP, MIDR, D], BF16, kind="Internal").ap()
    scr["wp_l1sg"] = nc.dram_tensor("wp_l1sg", [2, D, MIDS], BF16, kind="Internal").ap()
    scr["wp_l1su"] = nc.dram_tensor("wp_l1su", [2, D, MIDS], BF16, kind="Internal").ap()
    scr["wp_l1sd"] = nc.dram_tensor("wp_l1sd", [2, MIDS, D], BF16, kind="Internal").ap()
    scr["wp_router"] = nc.dram_tensor("wp_router", [D, NEXP], BF16, kind="Internal").ap()

    rc = Rec(nc, nseq)
    with rc.ctx:
        arena = Arena(nc, rc.ctx, cols_f32=53180)
        banks = PBanks(nc, rc.ctx, rc)
        B = {"banks": banks, "dr": dr, "scr": scr, "arena": arena}
        for snm in ("sT", "sV", "sS", "sG", "sJ"):
            B[snm] = rc.sem(snm)
        for snm in ("dXL0", "dXL1", "dX20", "dX21", "dST0", "dST1",
                    "dSL0", "dSL1", "dCB0", "dCB1", "dCS0", "dCS1",
                    "dPS0", "dPS1", "dCN", "dCNG",
                    "dGS", "dGS1", "dUS", "dUS1", "dDS", "dES0", "dES1", "dUS20", "dUS21",
                    "dDS20", "dDS21", "dSG", "dSU", "dSDW", "dRT", "dGA", "dGB"):
            B[snm] = rc.sem(snm)
        B["dXL"] = [B["dXL0"], B["dXL1"]]
        B["dX2"] = [B["dX20"], B["dX21"]]
        B["dST"] = [B["dST0"], B["dST1"]]
        B["dSL"] = [B["dSL0"], B["dSL1"]]
        B["dCB"] = [B["dCB0"], B["dCB1"]]
        B["dCS"] = [B["dCS0"], B["dCS1"]]
        B["dPS"] = [B["dPS0"], B["dPS1"]]
        sY, sGD = B["dCN"], B["dCNG"]

        def pview(shape, dt=BF16, parts=128):
            return arena.view(shape, dt, parts)

        B["ident_bf"] = pview([128], BF16)
        B["ident_f32"] = pview([128], F32)
        B["ones_row_bf"] = pview([128], BF16, parts=1)
        B["ones_row_f32"] = pview([128], F32, parts=1)
        B["headmask"] = pview([2], BF16)
        B["sel2"] = pview([128], F32, parts=2)
        B["mask_lt"] = pview([128], BF16)
        B["mask_ge"] = pview([128], BF16)
        B["mask_le"] = pview([128], BF16)
        B["bias_pp"] = pview([44], F32)
        B["bias_row"] = pview([NROW], BF16, parts=1)
        B["gldiag"] = pview([2], BF16)
        B["grdiag"] = pview([2], BF16)
        B["eps"] = pview([1], F32)
        B["tiny"] = pview([1], F32, parts=1)
        for nm in ("lnms", "lnvs", "lnm", "lnmsq", "lnvar", "lnt", "lnr"):
            B[nm] = pview([2], F32)
        B["dump"] = pview([1024], BF16)
        B["xld"] = pview([2, 1024], F32)
        B["xld2"] = pview([2, 1024], F32)
        B["yst"] = pview([2, 1024], F32)
        B["nrm"] = pview([56], F32, parts=2)
        B["kn2a"] = pview([1], F32, parts=2)
        B["kn2"] = pview([1], F32, parts=1)
        B["qn2"] = pview([1], F32, parts=1)
        B["pp1"] = pview([1], F32, parts=1)
        B["pp2"] = pview([1], F32, parts=1)
        B["pp2f"] = pview([1], F32, parts=1)
        B["negM"] = pview([1], F32)
        B["negM2"] = pview([1], F32)
        for k, v in scr.items():
            B[k] = v
        for k in list(dr.keys()):
            B.setdefault(k, dr[k])
        arena.persist_done()

        def cload(dst_ap, key, cast=False):
            if cast:
                rc.on("g", lambda g, i, dst_ap=dst_ap, key=key:
                      g.dma_start(out=dst_ap, in_=dr[key]), incs=[(sGD, 16)])
            else:
                rc.on("y", lambda y, i, dst_ap=dst_ap, key=key:
                      y.dma_start(out=dst_ap, in_=dr[key]), incs=[(sY, 16)])

        cload(B["ident_bf"], "c_ident_bf")
        cload(B["ident_f32"], "c_ident_f32")
        cload(B["ones_row_bf"], "c_ones_row_bf")
        cload(B["ones_row_f32"], "c_ones_row_f32")
        cload(B["headmask"], "c_headmask")
        cload(B["sel2"], "c_sel2")
        cload(B["mask_lt"], "c_mask_k_lt_q")
        cload(B["mask_ge"], "c_mask_k_ge_q")
        cload(B["mask_le"], "c_mask_k_le_q")
        cload(B["bias_pp"], "c_bias_pp")
        cload(B["bias_row"], "c_bias_row", cast=True)
        rc.on("v", lambda v, i: v.memset(B["eps"], 1e-5), incs=[(B["sV"], 1)])
        rc.on("v", lambda v, i: v.memset(B["tiny"], 1e-20), incs=[(B["sV"], 1)])

        need = set(phases)
        alloc_phase_bufs(rc, B, need)
        jobs = []
        if "l1ffn" in need:
            for e in range(NEXP):
                jobs += [(dr["L1_ffn_wg_r"][e], scr["wp_l1rg"][e], D, MIDR),
                         (dr["L1_ffn_wu_r"][e], scr["wp_l1ru"][e], D, MIDR),
                         (dr["L1_ffn_wd_r"][e], scr["wp_l1rd"][e], MIDR, D)]
            for e in range(2):
                jobs += [(dr["L1_ffn_wg_s"][e], scr["wp_l1sg"][e], D, MIDS),
                         (dr["L1_ffn_wu_s"][e], scr["wp_l1su"][e], D, MIDS),
                         (dr["L1_ffn_wd_s"][e], scr["wp_l1sd"][e], MIDS, D)]
            jobs.append((dr["L1_ffn_router_w"], scr["wp_router"], D, NEXP))
        emit_prep_cast(rc, B, jobs)
        if "l0attn" in need:
            rc.on("v", lambda v, i: v.memset(B["gldiag"], 0.0), incs=[(B["sV"], 1)])
            rc.on("v", lambda v, i: v.memset(B["grdiag"], 0.0), incs=[(B["sV"], 1)])
            for (dst, key) in [(B["gldiag"], "L0_attn_gl_w"), (B["grdiag"], "L0_attn_gr_w")]:
                rc.on("g", lambda g, i, dst=dst, key=key: g.dma_start(
                    out=dst[0:64, 0:1], in_=dr[key]),
                    waits=[B["sV"].now()], incs=[(sGD, 16)])
                rc.on("g", lambda g, i, dst=dst, key=key: g.dma_start(
                    out=dst[64:128, 1:2], in_=dr[key]), incs=[(sGD, 16)])
        if "l0ffn" in need:
            emit_prep_quant_weights(rc, B)

        rc.on("y", None, waits=[B[s].now() for s in
              ("dCN", "dCS0", "dCS1", "dPS0", "dPS1", "dXL0", "dXL1")])
        rc.on("g", None, waits=[B["dCNG"].now()])
        phase_join(rc, B)
        rc.begin_body()
        for ph in phases:
            skey, dkey = PHASE_IO[ph]
            if dev_io:
                skey, dkey = "x", "y"
            xsrc_fn = (lambda i, t=dr[skey]: t[i])
            xdst_fn = (lambda i, t=dr[dkey]: t[i])
            if ph == "l1attn":
                emit_phase_l1attn(rc, B, xsrc_fn, xdst_fn)
            elif ph == "l1ffn":
                emit_phase_l1ffn(rc, B, xsrc_fn, xdst_fn)
            elif ph == "l0attn":
                emit_phase_l0attn(rc, B, xsrc_fn, xdst_fn)
            elif ph == "l0ffn":
                emit_phase_l0ffn(rc, B, xsrc_fn, xdst_fn)
            phase_join(rc, B)
        rc.end_body()
        for e in ENGS:
            rc.on(e, None, waits=[B["sJ"].now()])
        rc.on("y", None, waits=[B["dST0"].now(), B["dST1"].now(), B["dCN"].now()])
        block = rc.ctx.enter_context(nc.Block())
        rc.build(block)
    return nc


_DMA_SEMS_Y = ("dXL0", "dXL1", "dX20", "dX21", "dST0", "dST1", "dSL0", "dSL1",
               "dCS0", "dCS1", "dPS0", "dPS1", "dCN", "dGS", "dGS1", "dUS", "dUS1", "dDS",
               "dES0", "dES1", "dUS20", "dUS21", "dDS20", "dDS21",
               "dSG", "dSU", "dSDW", "dRT")


def phase_join(rc, B):
    sJ = B["sJ"]
    own = {"t": "sT", "v": "sV", "s": "sS", "g": "sG"}
    rc.on("y", None, waits=[B[s].now() for s in _DMA_SEMS_Y], incs=[(sJ, 1)])
    rc.on("g", None, waits=[B["dCNG"].now(), B["sG"].now()], incs=[(sJ, 1)])
    for e in ("t", "v", "s"):
        rc.on(e, None, waits=[B[own[e]].now()], incs=[(sJ, 1)])
    target = sJ.now()
    for e in ENGS:
        rc.on(e, None, waits=[target])


def alloc_phase_bufs(rc, B, need):
    arena = B["arena"]
    arena.phase_reset()
    B["xnf"] = arena.view([8, 2048], BF16)
    B["xnt"] = arena.view([2, 1024], BF16)
    mark_bf, mark_f = arena.cur_bf, arena.cur_f32
    peak = (mark_bf, mark_f)

    def reset():
        arena.cur_bf, arena.cur_f32 = mark_bf, mark_f

    def done(nm):
        nonlocal peak
        peak = (max(peak[0], arena.cur_bf), max(peak[1], arena.cur_f32))

    if "l1attn" in need or "l0attn" in need:
        reset()
        B["slab"] = arena.view([2, 8, 1024], BF16)
        B["lrow"] = arena.view([512], F32, parts=1)
        B["linv"] = arena.view([512], F32, parts=1)
        B["bcs"] = arena.view([512], F32, parts=64)
        B["sqscr"] = arena.view([2048], BF16)
        B["qgb"] = arena.view([8, 512], BF16)
        B["ptile"] = arena.view([2, 512], BF16)
        B["kbuf"] = arena.view([8, 2048], BF16)
        kmark = (arena.cur_bf, arena.cur_f32)
        B["vt"] = arena.view([16, 16, 65], BF16)
        B["ctxb"] = arena.view([8, 512], BF16)
        done("l1attn")
        if "l0attn" in need:
            arena.cur_bf, arena.cur_f32 = kmark
            B["vt6"] = arena.view([5, 16, 65], BF16)
            B["routslab"] = arena.view([2, 1024], BF16)
            B["mask_lat"] = arena.view([2048], BF16, parts=64)
            B["rqb"] = arena.view([2, 2048], BF16)
            B["localb"] = arena.view([8, 512], BF16)
            B["remoteb"] = arena.view([8, 512], BF16)
            B["pooledb"] = arena.view([8, 64], BF16)
            B["rkf"] = arena.view([2, 64], BF16)
            B["rvt"] = arena.view([4, 65], BF16, parts=64)
            B["lpt"] = arena.view([512], BF16, parts=64)
            B["gsig"] = arena.view([512], F32, parts=2)
            B["scr1"] = arena.view([512], BF16)
            B["latc"] = arena.view([2, 512], BF16)
            done("l0attn")
    if "l0ffn" in need:
        reset()
        for qn in ("q", "q2"):
            B[qn] = {
                "mx": arena.view([64], F32), "scl": arena.view([64], F32),
                "rcp": arena.view([64], F32), "abs": arena.view([1024], F32),
                "sgn": arena.view([1024], BF16), "axn": arena.view([1024], F32),
                "w": arena.view([1024], BF16), "tA": arena.view([1024], BF16),
                "tB": arena.view([1024], BF16),
            }
        B["xnq"] = arena.view([2, 1024], F32)
        B["xqt"] = arena.view([2, 1024], BF16)
        B["hq_t"] = arena.view([2, 1024], BF16)
        fmark = (arena.cur_bf, arena.cur_f32)
        B["hqf"] = arena.view([22, 512], BF16)
        B["gslab"] = arena.view([2, 8, 512], BF16)
        B["uslab"] = arena.view([2, 8, 512], BF16)
        B["dslab"] = arena.view([22, 512], BF16)
        B["hstage"] = arena.view([2, 512], F32)
        done("l0ffn")
        arena.cur_bf, arena.cur_f32 = fmark
        B["pstage"] = arena.view([2, 2816], BF16)
        B["wdq"] = arena.view([2, 2816], BF16)
        B["castbuf"] = arena.view([2, 3072], BF16)
        done("prep")
    if "l1ffn" in need:
        reset()
        B["sel16"] = arena.view([16 * 128], BF16, parts=16)
        B["hs"] = arena.view([8, 512], BF16)
        B["hr"] = arena.view([32, 512], BF16)
        B["combT"] = arena.view([2048], BF16, parts=16)
        B["lstage"] = arena.view([512], F32, parts=16)
        B["tklog"] = arena.view([2, 16], F32)
        B["tkexp"] = arena.view([2, 16], F32)
        B["tkmx"] = arena.view([8], F32)
        B["tke1"] = arena.view([16], F32)
        B["tkpm"] = arena.view([16], F32)
        B["tkt"] = arena.view([16], F32)
        B["comb"] = arena.view([2, 16], F32)
        B["sil"] = arena.view([512], F32)
        B["rslab"] = arena.view([8, 16], BF16)
        B["eslab"] = arena.view([2, 8, 256], BF16)
        B["uslab2"] = arena.view([2, 8, 256], BF16)
        B["dslab2"] = arena.view([2, 2, 1024], BF16)
        B["sgslab"] = arena.view([2, 8, 512], BF16)
        B["suslab"] = arena.view([2, 8, 512], BF16)
        B["sdslab"] = arena.view([2, 4, 1024], BF16)
        done("l1ffn")
    if "castbuf" not in B:
        reset()
        B["castbuf"] = arena.view([2, 3072], BF16)


def emit_prep_quant_weights(rc, B):
    sT, sS, sV, sG = B["sT"], B["sS"], B["sV"], B["sG"]
    dXL, dPS = B["dXL"], B["dPS"]
    banks = B["banks"]
    dr, scr = B["dr"], B["scr"]
    wld = B["castbuf"]
    jobs = [(dr["L0_ffn_w_gate"], scr["wp_l0g"], IFF, D),
            (dr["L0_ffn_w_up"], scr["wp_l0u"], IFF, D)]
    for (src, dst, R, C) in jobs:
        for ri in range(R // 128):
            slot = ri % 2
            rc.on("y", lambda y, i, src=src, ri=ri, slot=slot: y.dma_start(
                out=B["xld"][:, slot, :], in_=src[ri * 128:(ri + 1) * 128, :]),
                waits=B.pop(("pqfree", slot), []), incs=[(dXL[slot], 16)])
            emit_quant_tile(rc, B, B["xld"][:, slot, :], B["xqt"][:, slot, :], 1024,
                            src_wait=[dXL[slot].now()], dst_wait=[sT.now()],
                            qsel=slot)
            pb = banks.bf[slot]
            for c in range(8):
                rc.on("t", lambda t_, i, c=c, slot=slot, pb=pb: t_.transpose(
                    out=pb[:, c * 128:(c + 1) * 128],
                    in_=B["xqt"][:, slot, c * 128:(c + 1) * 128],
                    identity=B["ident_bf"]),
                    waits=([sV.now(), sG.now(), B["dCN"].now()] + banks.bwait(pb)
                           if c == 0 else []),
                    incs=[(sT, 1)])
            rc.on("s", lambda s, i, slot=slot, pb=pb: s.copy(
                out=B["pstage"][:, slot, 0:1024], in_=pb[:]),
                waits=[sT.now(), dPS[slot].now()], incs=[(sS, 1)])
            banks.bfree(pb, [sS.now()])
            for c in range(8):
                rc.on("y", lambda y, i, dst=dst, ri=ri, c=c, slot=slot: y.dma_start(
                    out=dst[c * 128:(c + 1) * 128, ri * 128:(ri + 1) * 128],
                    in_=B["pstage"][:, slot, c * 128:(c + 1) * 128]),
                    waits=([sS.now()] if c == 0 else []), incs=[(dPS[slot], 16)])
            B[("pqfree", slot)] = [sV.now(), sS.now(), sT.now()]
    src, dst = dr["L0_ffn_w_down"], scr["wp_l0d"]
    for ri in range(8):
        slot = ri % 2
        for p0, pw in ((0, 1024), (1024, 1024), (2048, 768)):
            rc.on("y", lambda y, i, src=src, ri=ri, slot=slot, p0=p0, pw=pw:
                  y.dma_start(out=B["xld"][:, slot, 0:pw],
                              in_=src[ri * 128:(ri + 1) * 128, p0:p0 + pw]),
                  waits=B.pop(("pqdfree", slot), []), incs=[(dXL[slot], 16)])
            emit_quant_tile(rc, B, B["xld"][:, slot, 0:pw],
                            B["wdq"][:, slot, p0:p0 + pw], pw,
                            src_wait=[dXL[slot].now()], dst_wait=[sT.now()],
                            qsel=slot)
            B[("pqdfree", slot)] = [B["sV"].now(), sS.now()]
        for cg in range(6):
            w = 512 if cg < 5 else 256
            pb = banks.bf[cg % 2]
            for c in range(w // 128):
                rc.on("t", lambda t_, i, cg=cg, c=c, slot=slot, pb=pb: t_.transpose(
                    out=pb[:, c * 128:(c + 1) * 128],
                    in_=B["wdq"][:, slot, cg * 512 + c * 128:cg * 512 + (c + 1) * 128],
                    identity=B["ident_bf"]),
                    waits=([sV.now(), sG.now(), B["dCN"].now()] + banks.bwait(pb)
                           if c == 0 else []),
                    incs=[(sT, 1)])
            rc.on("s", lambda s, i, cg=cg, slot=slot, pb=pb, w=w: s.copy(
                out=B["pstage"][:, slot, cg * 512:cg * 512 + w], in_=pb[:, 0:w]),
                waits=[sT.now(), dPS[slot].now()], incs=[(sS, 1)])
            banks.bfree(pb, [sS.now()])
        for c in range(22):
            rc.on("y", lambda y, i, dst=dst, ri=ri, c=c, slot=slot: y.dma_start(
                out=dst[c * 128:(c + 1) * 128, ri * 128:(ri + 1) * 128],
                in_=B["pstage"][:, slot, c * 128:(c + 1) * 128]),
                waits=([sS.now()] if c == 0 else []), incs=[(dPS[slot], 16)])
        B[("pqdfree", slot)] = [sV.now(), sS.now(), sT.now()]


def emit_prep_cast(rc, B, jobs):
    stage = B["castbuf"]
    for (src, dst, R, C) in jobs:
        for ri, r0 in enumerate(range(0, R, 128)):
            slot = ri % 2
            rc.on("g", lambda g, i, src=src, r0=r0, C=C, slot=slot:
                  g.dma_start(out=stage[:, slot, 0:C], in_=src[r0:r0 + 128, :]),
                  waits=[B["dCS"][slot].now()], incs=[(B["dCB"][slot], 16)])
            rc.on("y", lambda y, i, dst=dst, r0=r0, C=C, slot=slot:
                  y.dma_start(out=dst[r0:r0 + 128, :], in_=stage[:, slot, 0:C]),
                  waits=[B["dCB"][slot].now()], incs=[(B["dCS"][slot], 16)])


def emit_quant_tile(rc, B, src_ap, dst_ap, F, src_wait=(), dst_wait=(), qsel=0):
    nb = F // 16
    q = B["q"] if qsel == 0 else B["q2"]
    sV, sS, sG = B["sV"], B["sS"], B["sG"]
    src3 = src_ap.rearrange("p (a b) -> p a b", b=16)

    rc.on("v", lambda v, i, a=src3, nb=nb: v.tensor_reduce(
        out=q["mx"][:, 0:nb], in_=a, axis=AX.X, op=ALU.max,
        apply_absolute_value=True), waits=list(src_wait), incs=[(sV, 1)])
    rc.on("v", lambda v, i, nb=nb: v.tensor_scalar(
        out=q["scl"][:, 0:nb], in0=q["mx"][:, 0:nb], scalar1=1.0 / 6.0,
        scalar2=1e-12, op0=ALU.mult, op1=ALU.max),
        waits=[sV.now()], incs=[(sV, 1)])
    rc.on("v", lambda v, i, nb=nb: v.reciprocal(
        out=q["rcp"][:, 0:nb], in_=q["scl"][:, 0:nb]),
        waits=[sV.now()], incs=[(sV, 1)])
    rc.on("s", lambda s, i, a=src_ap, F=F: s.activation(
        out=q["abs"][:, 0:F], in_=a, func=AF.Abs),
        waits=list(src_wait) + [sG.now(), sV.now()], incs=[(sS, 1)])
    rc.on("s", lambda s, i, a=src_ap, F=F: s.activation(
        out=q["sgn"][:, 0:F], in_=a, func=AF.Sign), incs=[(sS, 1)])
    rc.on("g", lambda g, i, F=F, nb=nb: g.tensor_tensor(
        out=q["axn"][:, 0:F].rearrange("p (a b) -> p a b", b=16),
        in0=q["abs"][:, 0:F].rearrange("p (a b) -> p a b", b=16),
        in1=q["rcp"][:, 0:nb].unsqueeze(2).broadcast_to([128, nb, 16]),
        op=ALU.mult), waits=[sV.now(), sS.now()], incs=[(sG, 1)])
    rc.on("g", lambda g, i, F=F, nb=nb: g.tensor_tensor(
        out=q["w"][:, 0:F].rearrange("p (a b) -> p a b", b=16),
        in0=q["sgn"][:, 0:F].rearrange("p (a b) -> p a b", b=16),
        in1=q["scl"][:, 0:nb].unsqueeze(2).broadcast_to([128, nb, 16]),
        op=ALU.mult), incs=[(sG, 1)])
    A, Bv = q["tA"], q["tB"]
    rc.on("v", lambda v, i, F=F: v.tensor_scalar(
        out=Bv[:, 0:F], in0=q["axn"][:, 0:F], scalar1=5.0, scalar2=2.0,
        op0=ALU.is_gt, op1=ALU.mult), waits=[sG.now()], incs=[(sV, 1)])
    rc.on("v", lambda v, i, F=F: v.tensor_single_scalar(
        out=A[:, 0:F], in_=q["axn"][:, 0:F], scalar=0.25, op=ALU.is_gt),
        incs=[(sV, 1)])
    for thr, tgt, oth in ((2.5, "B", "A"), (0.75, "A", "B"), (3.5, "B", "A"),
                          (1.25, "A", "B"), (1.75, "A", "B")):
        dst = Bv if tgt == "B" else A
        rc.on("v", lambda v, i, F=F, thr=thr, dst=dst: v.scalar_tensor_tensor(
            out=dst[:, 0:F], in0=q["axn"][:, 0:F], scalar=thr, in1=dst[:, 0:F],
            op0=ALU.is_gt, op1=ALU.add), waits=[sV.now()], incs=[(sV, 1)])
    rc.on("v", lambda v, i, F=F: v.scalar_tensor_tensor(
        out=A[:, 0:F], in0=A[:, 0:F], scalar=0.5, in1=Bv[:, 0:F],
        op0=ALU.mult, op1=ALU.add), waits=[sV.now()], incs=[(sV, 1)])
    rc.on("g", lambda g, i, F=F, dst=dst_ap: g.tensor_tensor(
        out=dst, in0=A[:, 0:F], in1=q["w"][:, 0:F], op=ALU.mult),
        waits=[sV.now()] + list(dst_wait), incs=[(sG, 1)])


def emit_ln_tile(rc, B, xsrc_fn, t, slot, out_ap, out_f32=False):
    sV, sS = B["sV"], B["sS"]
    dXL = B["dXL"][slot]
    xld = B["xld"]
    rc.on("y", lambda y, i, t=t, slot=slot: y.dma_start(
        out=xld[:, slot, :], in_=xsrc_fn(i)[t * 128:(t + 1) * 128, :]),
        waits=B.pop(("xldfree", slot), []), incs=[(dXL, 16)])
    sl = slice(slot, slot + 1)
    rc.on("v", lambda v, i, slot=slot, sl=sl: v.tensor_reduce(
        out=B["lnms"][:, sl], in_=xld[:, slot, :], axis=AX.X, op=ALU.add),
        waits=[dXL.now()], incs=[(sV, 1)])
    rc.on("s", lambda s, i, slot=slot, sl=sl: s.activation(
        out=B["dump"][:], in_=xld[:, slot, :], func=AF.Square,
        accum_out=B["lnvs"][:, sl]), waits=[dXL.now(), sS.now()], incs=[(sS, 1)])
    rc.on("v", lambda v, i, sl=sl: v.tensor_scalar(
        out=B["lnm"][:, sl], in0=B["lnms"][:, sl], scalar1=1.0 / D, scalar2=None,
        op0=ALU.mult), waits=[sV.now()], incs=[(sV, 1)])
    rc.on("s", lambda s, i, sl=sl: s.activation(
        out=B["lnmsq"][:, sl], in_=B["lnm"][:, sl], func=AF.Square),
        waits=[sV.now()], incs=[(sS, 1)])
    rc.on("v", lambda v, i, sl=sl: v.tensor_scalar(
        out=B["lnvar"][:, sl], in0=B["lnvs"][:, sl], scalar1=1.0 / D,
        scalar2=B["lnmsq"][:, sl], op0=ALU.mult, op1=ALU.subtract),
        waits=[sS.now()], incs=[(sV, 1)])
    rc.on("s", lambda s, i, sl=sl: s.activation(
        out=B["lnt"][:, sl], in_=B["lnvar"][:, sl], func=AF.Ln, bias=B["eps"][:]),
        waits=[sV.now()], incs=[(sS, 1)])
    rc.on("s", lambda s, i, sl=sl: s.activation(
        out=B["lnr"][:, sl], in_=B["lnt"][:, sl], func=AF.Exp, scale=-0.5),
        waits=[sS.now()], incs=[(sS, 1)])
    rc.on("v", lambda v, i, slot=slot, sl=sl, out_ap=out_ap: v.tensor_scalar(
        out=out_ap, in0=xld[:, slot, :], scalar1=B["lnm"][:, sl],
        scalar2=B["lnr"][:, sl], op0=ALU.subtract, op1=ALU.mult),
        waits=[sS.now()], incs=[(sV, 1)])
    B[("xldfree", slot)] = [sV.now(), sS.now()]


def emit_tile_to_feature(rc, B, src_ap, dst3_ap, tbank_idx):
    sV, sT = B["sV"], B["sT"]
    banks = B["banks"]
    pb = banks.bf[tbank_idx]
    for c in range(8):
        rc.on("t", lambda t, i, c=c, src=src_ap, pb=pb: t.transpose(
            out=pb[:, c * 128:(c + 1) * 128], in_=src[:, c * 128:(c + 1) * 128],
            identity=B["ident_bf"][:]),
            waits=([sV.now(), B["sG"].now()] + banks.bwait(pb) if c == 0 else []),
            incs=([(sT, 1)] if c == 7 else []))
    rc.on("s", lambda s, i, pb=pb, dst=dst3_ap: s.copy(
        out=dst, in_=pb[:].rearrange("p (c w) -> p c w", w=128)),
        waits=[sT.now()], incs=[(B["sS"], 1)])
    banks.bfree(pb, [B["sS"].now()])


def emit_phase_l1ffn(rc, B, xsrc_fn, xdst_fn):
    sV, sS, sT, sG = B["sV"], B["sS"], B["sT"], B["sG"]
    banks = B["banks"]
    xnf = B["xnf"]
    scr = B["scr"]

    rc.on("y", lambda y, i: y.dma_start(
        out=B["sel16"], in_=B["dr"]["c_sel16"]),
        waits=[sT.now()], incs=[(B["dCN"], 16)])
    rc.on("y", lambda y, i: y.dma_start(
        out=B["rslab"][:], in_=scr["wp_router"].rearrange("(c p) m -> p c m", p=128)),
        incs=[(B["dRT"], 16)])
    for e in range(2):
        rc.on("y", lambda y, i, e=e: y.dma_start(
            out=B["sgslab"][:, e, :, :],
            in_=scr["wp_l1sg"][e].rearrange("(c p) m -> p c m", p=128)),
            waits=([sT.now()] if e == 0 else []), incs=[(B["dSG"], 16)])
        rc.on("y", lambda y, i, e=e: y.dma_start(
            out=B["suslab"][:, e, :, :],
            in_=scr["wp_l1su"][e].rearrange("(c p) m -> p c m", p=128)),
            waits=([sT.now()] if e == 0 else []), incs=[(B["dSU"], 16)])
        rc.on("y", lambda y, i, e=e: y.dma_start(
            out=B["sdslab"][:, e, :, :],
            in_=scr["wp_l1sd"][e].rearrange("(c p) m -> p c m", p=128)),
            waits=([sT.now()] if e == 0 else []), incs=[(B["dSDW"], 16)])

    lnrdy = []
    for t in range(16):
        emit_ln_tile(rc, B, xsrc_fn, t, t % 2, B["xnt"][:, t % 2, :])
        emit_tile_to_feature(rc, B, B["xnt"][:, t % 2, :],
                             xnf[:, 0:8, t * 128:(t + 1) * 128, ], t % 2)
        B[("xldfree", t % 2)] = B[("xldfree", t % 2)] + [sS.now()]
        lnrdy.append(sS.now())
    ln_done = sS.now()

    for tg in range(4):
        bank = banks.f32[5]
        rc.on("t", lambda t_, i, tg=tg, bank=bank: t_.matmul(
            out=bank[0:16, :], lhsT=B["rslab"][:, 0, :],
            rhs=xnf[:, 0, tg * 512:(tg + 1) * 512], start=True, stop=False),
            waits=[lnrdy[tg * 4 + 3], B["dRT"].now()] + banks.bwait(bank), incs=[])
        for kc in range(1, 8):
            rc.on("t", lambda t_, i, tg=tg, kc=kc, bank=bank: t_.matmul(
                out=bank[0:16, :], lhsT=B["rslab"][:, kc, :],
                rhs=xnf[:, kc, tg * 512:(tg + 1) * 512],
                start=False, stop=(kc == 7)), incs=([(sT, 1)] if kc == 7 else []))
        rc.on("v", lambda v, i, bank=bank: v.tensor_copy(
            out=B["lstage"][0:16, :], in_=bank[0:16, :]),
            waits=[sT.now()], incs=[(sV, 1)])
        banks.bfree(bank, [sV.now()])
        for ts_ in range(4):
            tile = tg * 4 + ts_
            slot = tile % 2
            tb = banks.f32[4]
            rc.on("t", lambda t_, i, ts_=ts_, tb=tb: t_.transpose(
                out=tb[0:128, 0:16], in_=B["lstage"][0:16, ts_ * 128:(ts_ + 1) * 128],
                identity=B["ident_f32"][0:16, 0:16]),
                waits=[sV.now()] + banks.bwait(tb), incs=[(sT, 1)])
            rc.on("s", lambda s, i, slot=slot, tb=tb: s.copy(
                out=B["tklog"][:, slot, :], in_=tb[0:128, 0:16]),
                waits=[sT.now(), sG.now(), sV.now()], incs=[(sS, 1)])
            banks.bfree(tb, [sS.now()])
            tl = B["tklog"][:, slot, :]
            rc.on("v", lambda v, i, tl=tl: v.tensor_reduce(
                out=B["tkmx"][:, 0:1], in_=tl, axis=AX.X, op=ALU.max, negate=True),
                waits=[sS.now(), sG.now()], incs=[(sV, 1)])
            rc.on("s", lambda s, i, tl=tl, slot=slot: s.activation(
                out=B["tkexp"][:, slot, :], in_=tl, func=AF.Exp,
                bias=B["tkmx"][:, 0:1]), waits=[sV.now()], incs=[(sS, 1)])
            rc.on("v", lambda v, i, slot=slot: v.tensor_reduce(
                out=B["tkmx"][:, 1:2], in_=B["tkexp"][:, slot, :], axis=AX.X,
                op=ALU.max), waits=[sS.now()], incs=[(sV, 1)])
            rc.on("v", lambda v, i, slot=slot: v.tensor_single_scalar(
                out=B["tke1"][:], in_=B["tkexp"][:, slot, :],
                scalar=B["tkmx"][:, 1:2], op=ALU.is_ge),
                waits=[sV.now()], incs=[(sV, 1)])
            rc.on("v", lambda v, i: v.tensor_scalar(
                out=B["tkt"][:], in0=B["tke1"][:], scalar1=-1.0, scalar2=1.0,
                op0=ALU.mult, op1=ALU.add), waits=[sV.now()], incs=[(sV, 1)])
            rc.on("g", lambda g, i, slot=slot: g.tensor_tensor(
                out=B["tkpm"][:], in0=B["tkt"][:], in1=B["tkexp"][:, slot, :],
                op=ALU.mult), waits=[sV.now()], incs=[(sG, 1)])
            rc.on("v", lambda v, i: v.tensor_reduce(
                out=B["tkmx"][:, 2:3], in_=B["tkpm"][:], axis=AX.X, op=ALU.max),
                waits=[sG.now()], incs=[(sV, 1)])
            rc.on("g", lambda g, i: g.tensor_tensor(
                out=B["tkmx"][:, 3:4], in0=B["tkmx"][:, 1:2], in1=B["tkmx"][:, 2:3],
                op=ALU.add), waits=[sV.now()], incs=[(sG, 1)])
            rc.on("v", lambda v, i: v.tensor_scalar(
                out=B["tkmx"][:, 4:5], in0=B["tkmx"][:, 3:4], scalar1=1e-9,
                scalar2=None, op0=ALU.add), waits=[sG.now()], incs=[(sV, 1)])
            rc.on("v", lambda v, i: v.reciprocal(
                out=B["tkmx"][:, 5:6], in_=B["tkmx"][:, 4:5]),
                waits=[sV.now()], incs=[(sV, 1)])
            rc.on("g", lambda g, i: g.tensor_tensor(
                out=B["tkmx"][:, 6:7], in0=B["tkmx"][:, 1:2], in1=B["tkmx"][:, 5:6],
                op=ALU.mult), waits=[sV.now()], incs=[(sG, 1)])
            rc.on("g", lambda g, i: g.tensor_tensor(
                out=B["tkmx"][:, 7:8], in0=B["tkmx"][:, 2:3], in1=B["tkmx"][:, 5:6],
                op=ALU.mult), incs=[(sG, 1)])
            rc.on("v", lambda v, i: v.tensor_single_scalar(
                out=B["tkt"][:], in_=B["tkpm"][:], scalar=B["tkmx"][:, 2:3],
                op=ALU.is_ge), waits=[sG.now()], incs=[(sV, 1)])
            rc.on("v", lambda v, i: v.tensor_single_scalar(
                out=B["tke1"][:], in_=B["tke1"][:], scalar=B["tkmx"][:, 6:7],
                op=ALU.mult), waits=[sV.now(), sG.now()], incs=[(sV, 1)])
            rc.on("v", lambda v, i, slot=slot: v.scalar_tensor_tensor(
                out=B["comb"][:, slot, :], in0=B["tkt"][:], scalar=B["tkmx"][:, 7:8],
                in1=B["tke1"][:], op0=ALU.mult, op1=ALU.add),
                waits=[sV.now(), sG.now()], incs=[(sV, 1)])
            tb2 = banks.f32[4]
            rc.on("t", lambda t_, i, slot=slot, tb2=tb2: t_.transpose(
                out=tb2[0:16, 0:128], in_=B["comb"][:, slot, :],
                identity=B["ident_f32"][:]),
                waits=[sV.now()] + banks.bwait(tb2), incs=[(sT, 1)])
            rc.on("s", lambda s, i, tile=tile, tb2=tb2: s.copy(
                out=B["combT"][0:16, tile * 128:(tile + 1) * 128], in_=tb2[0:16, 0:128]),
                waits=[sT.now(), sS.now()], incs=[(sS, 1)])
            banks.bfree(tb2, [sS.now()])
    comb_done = sS.now()

    for tg in range(4):
        for e in range(2):
            for mc in range(4):
                gb = banks.f32[0]
                ub = banks.f32[1]
                for kc in range(8):
                    rc.on("t", lambda t_, i, e=e, mc=mc, kc=kc, tg=tg, gb=gb: t_.matmul(
                        out=gb[:], lhsT=B["sgslab"][:, e, kc, mc * 128:(mc + 1) * 128],
                        rhs=xnf[:, kc, tg * 512:(tg + 1) * 512],
                        start=(kc == 0), stop=(kc == 7)),
                        waits=([lnrdy[tg * 4 + 3], B["dSG"].now()] + banks.bwait(gb)
                               if kc == 0 else []),
                        incs=([(sT, 1)] if kc == 7 else []))
                for kc in range(8):
                    rc.on("t", lambda t_, i, e=e, mc=mc, kc=kc, tg=tg, ub=ub: t_.matmul(
                        out=ub[:], lhsT=B["suslab"][:, e, kc, mc * 128:(mc + 1) * 128],
                        rhs=xnf[:, kc, tg * 512:(tg + 1) * 512],
                        start=(kc == 0), stop=(kc == 7)),
                        waits=([B["dSU"].now()] + banks.bwait(ub) if kc == 0 else []),
                        incs=([(sT, 1)] if kc == 7 else []))
                rc.on("s", lambda s, i, gb=gb: s.activation(
                    out=B["sil"][:], in_=gb[:], func=AF.Silu),
                    waits=[sT.now(), sV.now()], incs=[(sS, 1)])
                banks.bfree(gb, [sS.now()])
                rc.on("v", lambda v, i, e=e, mc=mc, ub=ub: v.tensor_tensor(
                    out=B["hs"][:, e * 4 + mc, :], in0=B["sil"][:], in1=ub[:],
                    op=ALU.mult), waits=[sS.now()], incs=[(sV, 1)])
                banks.bfree(ub, [sV.now()])
        for e in range(NEXP):
            slot = e % 2
            rc.on("y", lambda y, i, e=e, slot=slot: y.dma_start(
                out=B["eslab"][:, slot, :, :],
                in_=scr["wp_l1rg"][e].rearrange("(c p) m -> p c m", p=128)),
                waits=B.pop(("esfree", slot), []), incs=[(B["dES0" if slot == 0 else "dES1"], 16)])
            rc.on("y", lambda y, i, e=e, slot=slot: y.dma_start(
                out=B["uslab2"][:, slot, :, :],
                in_=scr["wp_l1ru"][e].rearrange("(c p) m -> p c m", p=128)),
                waits=B.pop(("usfree", slot), []), incs=[(B["dUS20" if slot == 0 else "dUS21"], 16)])
            bc = banks.f32[2]
            rc.on("t", lambda t_, i, e=e, tg=tg, bc=bc: t_.matmul(
                out=bc[:], lhsT=B["sel16"][:, e * 128:(e + 1) * 128],
                rhs=B["combT"][0:16, tg * 512:(tg + 1) * 512], start=True, stop=True),
                waits=[comb_done, B["dCN"].now()] + banks.bwait(bc), incs=[(sT, 1)])
            bc_ready = sT.now()
            for mc in range(2):
                gb = banks.f32[3]
                ub = banks.f32[0]
                for kc in range(8):
                    rc.on("t", lambda t_, i, e=e, mc=mc, kc=kc, tg=tg, slot=slot,
                          gb=gb: t_.matmul(
                        out=gb[:], lhsT=B["eslab"][:, slot, kc, mc * 128:(mc + 1) * 128],
                        rhs=xnf[:, kc, tg * 512:(tg + 1) * 512],
                        start=(kc == 0), stop=(kc == 7)),
                        waits=([B["dES0" if slot == 0 else "dES1"].now()]
                               + banks.bwait(gb) if kc == 0 else []),
                        incs=([(sT, 1)] if kc == 7 else []))
                for kc in range(8):
                    rc.on("t", lambda t_, i, e=e, mc=mc, kc=kc, tg=tg, slot=slot,
                          ub=ub: t_.matmul(
                        out=ub[:], lhsT=B["uslab2"][:, slot, kc, mc * 128:(mc + 1) * 128],
                        rhs=xnf[:, kc, tg * 512:(tg + 1) * 512],
                        start=(kc == 0), stop=(kc == 7)),
                        waits=([B["dUS20" if slot == 0 else "dUS21"].now()]
                               + banks.bwait(ub) if kc == 0 else []),
                        incs=([(sT, 1)] if kc == 7 else []))
                rc.on("s", lambda s, i, gb=gb: s.activation(
                    out=B["sil"][:], in_=gb[:], func=AF.Silu),
                    waits=[sT.now(), sV.now()], incs=[(sS, 1)])
                banks.bfree(gb, [sS.now()])
                rc.on("v", lambda v, i, ub=ub: v.tensor_tensor(
                    out=B["sil"][:], in0=B["sil"][:], in1=ub[:], op=ALU.mult),
                    waits=[sS.now()], incs=[(sV, 1)])
                banks.bfree(ub, [sV.now()])
                rc.on("v", lambda v, i, e=e, mc=mc, bc=bc: v.tensor_tensor(
                    out=B["hr"][:, e * 2 + mc, :], in0=B["sil"][:], in1=bc[:],
                    op=ALU.mult), waits=[sV.now()], incs=[(sV, 1)])
                if mc == 1:
                    banks.bfree(bc, [sV.now()])
            B[("esfree", slot)] = [sT.now()]
            B[("usfree", slot)] = [sT.now()]
        h_done = sV.now()

        dwn = ([(banks.f32[j], banks.f32[j][:]) for j in range(6)]
               + [(bb, bb[:].bitcast(F32)) for bb in banks.bf])
        for bko, _ in dwn:
            for w in banks.bwait(bko):
                rc.on("t", None, waits=[w])
        nsrcmc = 2 * 4 + NEXP * 2
        si = 0
        for src_i in range(2 + NEXP):
            routed = src_i >= 2
            e = src_i - 2 if routed else src_i
            if routed:
                slot = e % 2
                rc.on("y", lambda y, i, e=e, slot=slot: y.dma_start(
                    out=B["dslab2"][:, slot, :, :],
                    in_=scr["wp_l1rd"][e].rearrange("(c p) m -> p c m", p=128)),
                    waits=B.pop(("dsfree", slot), []),
                    incs=[(B["dDS20" if slot == 0 else "dDS21"], 16)])
            nmc = 2 if routed else 4
            for mc in range(nmc):
                hsrc = B["hr"][:, e * 2 + mc, :] if routed else B["hs"][:, e * 4 + mc, :]
                for tt in range(4):
                    for og in range(2):
                        bk = dwn[tt * 2 + og][1]
                        if routed:
                            rhs_fn = (lambda i, slot=slot, mc=mc, og=og:
                                      B["dslab2"][:, slot, mc, og * 512:(og + 1) * 512])
                            w0 = ([B["dDS20" if slot == 0 else "dDS21"].now(), h_done]
                                  if (tt == 0 and og == 0 and mc == 0) else [])
                        else:
                            rhs_fn = (lambda i, e=e, mc=mc, og=og:
                                      B["sdslab"][:, e, mc, og * 512:(og + 1) * 512])
                            w0 = ([B["dSDW"].now(), h_done]
                                  if (tt == 0 and og == 0 and mc == 0) else [])
                        rc.on("t", lambda t_, i, hsrc=hsrc, rhs_fn=rhs_fn, tt=tt,
                              bk=bk, si=si, nsrcmc=nsrcmc: t_.matmul(
                            out=bk, lhsT=hsrc[:, tt * 128:(tt + 1) * 128],
                            rhs=rhs_fn(i), start=(si == 0), stop=(si == nsrcmc - 1)),
                            waits=w0, incs=([(sT, 1)] if (tt == 3 and og == 1 and
                                                          mc == nmc - 1) else []))
                si += 1
            if routed:
                B[("dsfree", slot)] = [sT.now()]
        down_done = sT.now()
        for tt in range(4):
            ttg = tg * 4 + tt
            xslot = tt % 2
            rc.on("y", lambda y, i, ttg=ttg, xslot=xslot: y.dma_start(
                out=B["xld2"][:, xslot, :], in_=xsrc_fn(i)[ttg * 128:(ttg + 1) * 128, :]),
                waits=B.pop(("xld2free", xslot), []), incs=[(B["dX2"][xslot], 16)])
            for og in range(2):
                bko, bk = dwn[tt * 2 + og]
                rc.on("v", lambda v, i, og=og, xslot=xslot, bk=bk: v.tensor_tensor(
                    out=B["yst"][:, xslot, og * 512:(og + 1) * 512], in0=bk,
                    in1=B["xld2"][:, xslot, og * 512:(og + 1) * 512], op=ALU.add),
                    waits=[down_done, B["dX2"][xslot].now()] + B.pop(("ystfree", xslot), []),
                    incs=[(sV, 1)])
                banks.bfree(bko, [sV.now()])
            rc.on("y", lambda y, i, ttg=ttg, xslot=xslot: y.dma_start(
                out=xdst_fn(i)[ttg * 128:(ttg + 1) * 128, :], in_=B["yst"][:, xslot, :]),
                waits=[sV.now()], incs=[(B["dST"][xslot], 16)])
            B[("xld2free", xslot)] = [sV.now()]
            B[("ystfree", xslot)] = [B["dST"][xslot].now()]


def _emit_local_pv(rc, B, banks, pfE, vt6, qg, h, item, sT):
    kt, c0, c1, pw, ki = item
    for rcol in range(c0, c1, 128):
        tl_r = rcol // 128
        first_kt = max(qg * 4 + tl_r - 1, 0)
        rc.on("t", lambda t_, i, h=h, kt=kt, rcol=rcol, qg=qg,
              st=(kt == first_kt),
              sp=(kt == qg * 4 + tl_r): t_.matmul(
            out=pfE[0:65, rcol:rcol + 128], lhsT=vt6[:, kt % 5, h, 0:65],
            rhs=B["ptile"][:, kt % 2, rcol:rcol + 128],
            start=st, stop=sp),
            waits=[pw] + (banks.bwait(pfE) if ki == 0 and rcol == c0 else []),
            incs=[(sT, 1)])


def emit_phase_l0attn(rc, B, xsrc_fn, xdst_fn):
    sV, sS, sT, sG = B["sV"], B["sS"], B["sT"], B["sG"]
    banks = B["banks"]
    pfE, pfF = banks.f32[4], banks.f32[5]
    xnf, kbuf, qgb = B["xnf"], B["kbuf"], B["qgb"]
    vt6, rqb, rkf, rvt = B["vt6"], B["rqb"], B["rkf"], B["rvt"]
    localb, remoteb, latc = B["localb"], B["remoteb"], B["latc"]
    slab = B["slab"]
    scr = B["scr"]
    NRQ, NRK, NK, NQ = 0, 8, 10, 42

    rc.on("v", lambda v, i: v.memset(vt6[:, :, :, 64:65], 1.0), incs=[(sV, 1)])
    rc.on("v", lambda v, i: v.memset(rvt[:, :, 64:65], 1.0),
          waits=[sV.now()], incs=[(sV, 1)])
    rc.on("y", lambda y, i: y.dma_start(out=B["mask_lat"], in_=B["dr"]["c_mask_lat"]),
          waits=[sV.now()], incs=[(B["dCN"], 16)])

    rkvslab = slab[:, 0, :, 0:512]
    rqslab = slab[:, 0, :, 512:768]
    rc.on("g", lambda g, i: g.dma_start(
        out=rqslab, in_=B["dr"]["L0_attn_rq_w"].rearrange("(c p) m -> p c m", p=128)),
        waits=[sT.now()], incs=[(B["dGA"], 16)])
    rc.on("g", lambda g, i: g.dma_start(
        out=rkvslab, in_=B["dr"]["L0_attn_rkv_w"].rearrange("(c p) m -> p c m", p=128)),
        waits=[sT.now()], incs=[(B["dGB"], 16)])
    rc.on("g", lambda g, i: g.dma_start(
        out=B["routslab"][:], in_=B["dr"]["L0_attn_rout_w"]
        .rearrange("(c p) m -> p c m", p=128)),
        waits=[sT.now()], incs=[(B["dCB0"], 16)])
    rc.on("g", lambda g, i: g.dma_start(
        out=slab[:, 1, :, :], in_=B["dr"]["L0_attn_qkv_w"][:, 2048:3072]
        .rearrange("(c p) m -> p c m", p=128)),
        waits=[sT.now()], incs=[(B["dSL1"], 16)])

    for t in range(16):
        emit_ln_tile(rc, B, xsrc_fn, t, t % 2, B["xnt"][:, t % 2, :])
        emit_tile_to_feature(rc, B, B["xnt"][:, t % 2, :],
                             xnf[:, 0:8, t * 128:(t + 1) * 128], t % 2)
        B[("xldfree", t % 2)] = B[("xldfree", t % 2)] + [sS.now()]
    ln_done = sS.now()

    def _pool_reduce(v, i, c):
        with v.bass.allow_low_precision(reason="32-chunk mean pooled to bf16"):
            return v.tensor_reduce(
                out=B["pooledb"][:, c, :], in_=xnf[:, c, :]
                .rearrange("p (a b) -> p a b", b=32), axis=AX.X, op=ALU.add)

    for c in range(8):
        rc.on("v", lambda v, i, c=c: _pool_reduce(v, i, c),
              waits=([ln_done] if c == 0 else []), incs=[(sV, 1)])
        rc.on("v", lambda v, i, c=c: v.tensor_scalar(
            out=B["pooledb"][:, c, :], in0=B["pooledb"][:, c, :],
            scalar1=1.0 / 32.0, scalar2=None, op0=ALU.mult),
            waits=[sV.now()], incs=[(sV, 1)])
    pool_done = sV.now()

    for oc in range(2):
        bank = banks.f32[oc % 2]
        for kc in range(8):
            rc.on("t", lambda t_, i, oc=oc, kc=kc, bank=bank: t_.matmul(
                out=bank[:, 0:64], lhsT=rkvslab[:, kc, oc * 128:(oc + 1) * 128],
                rhs=B["pooledb"][:, kc, :], start=(kc == 0), stop=(kc == 7)),
                waits=([pool_done, B["dGB"].now()] + banks.bwait(bank)
                       if kc == 0 else []),
                incs=([(sT, 1)] if kc == 7 else []))
        rc.on("v", lambda v, i, oc=oc, bank=bank: v.tensor_scalar(
            out=rkf[:, oc, :], in0=bank[:, 0:64],
            scalar1=B["bias_pp"][:, BPP["l0_rk"] + oc:BPP["l0_rk"] + oc + 1],
            scalar2=None, op0=ALU.add), waits=[sT.now()], incs=[(sV, 1)])
        banks.bfree(bank, [sV.now()])
    bank = banks.f32[2]
    rc.on("t", lambda t_, i, bank=bank: t_.matmul(
        out=bank[0:64, 0:256], lhsT=B["ones_row_bf"][0:1, 0:64],
        rhs=B["bias_row"][0:1, BROW["l0_rv"]:BROW["l0_rv"] + 256],
        start=True, stop=False),
        waits=[pool_done, B["dGB"].now()] + banks.bwait(bank), incs=[])
    for kc in range(8):
        rc.on("t", lambda t_, i, kc=kc, bank=bank: t_.matmul(
            out=bank[0:64, 0:256], lhsT=B["pooledb"][:, kc, :],
            rhs=rkvslab[:, kc, 256:512], start=False, stop=(kc == 7)),
            incs=([(sT, 1)] if kc == 7 else []))
    rc.on("s", lambda s, i, bank=bank: s.copy(
        out=rvt[0:64, 0:4, 0:64],
        in_=bank[0:64, 0:256].rearrange("p (h d) -> p h d", d=64)),
        waits=[sT.now()], incs=[(sS, 1)])
    banks.bfree(bank, [sS.now()])

    for oc in range(2):
        for kc in range(8):
            for tg in range(4):
                bank = banks.f32[tg]
                rc.on("t", lambda t_, i, oc=oc, tg=tg, kc=kc, bank=bank: t_.matmul(
                    out=bank[:], lhsT=rqslab[:, kc, oc * 128:(oc + 1) * 128],
                    rhs=xnf[:, kc, tg * 512:(tg + 1) * 512],
                    start=(kc == 0), stop=(kc == 7)),
                    waits=([ln_done, B["dGA"].now()] + banks.bwait(bank)
                           if kc == 0 else []),
                    incs=([(sT, 1)] if kc == 7 and tg == 3 else []))
        for tg in range(4):
            bank = banks.f32[tg]
            rc.on("v", lambda v, i, oc=oc, tg=tg, bank=bank: v.tensor_scalar(
                out=rqb[:, oc, tg * 512:(tg + 1) * 512], in0=bank[:],
                scalar1=B["bias_pp"][:, BPP["l0_rq"] + oc:BPP["l0_rq"] + oc + 1],
                scalar2=None, op0=ALU.add),
                waits=([sT.now()] if tg == 0 else []), incs=[(sV, 1)])
            banks.bfree(bank, [sV.now()])
    rq_done = sV.now()

    def norm_cols(src_fn, n, base, width=512, wait0=()):
        for j in range(n):
            rc.on("s", lambda s, i, j=j: s.activation(
                out=B["sqscr"][:, 0:width], in_=src_fn(j), func=AF.Square),
                waits=list(wait0) + [sT.now(), sS.now()], incs=[(sS, 1)])
            bank = banks.f32[2 + j % 2]
            rc.on("t", lambda t_, i, bank=bank: t_.matmul(
                out=bank[0:2, 0:width], lhsT=B["headmask"][:],
                rhs=B["sqscr"][:, 0:width], start=True, stop=True),
                waits=[sS.now()] + banks.bwait(bank), incs=[(sT, 1)])
            rc.on("v", lambda v, i, j=j, bank=bank: v.tensor_reduce(
                out=B["nrm"][:, base + j:base + j + 1], in_=bank[0:2, 0:width],
                axis=AX.X, op=ALU.max), waits=[sT.now()], incs=[(sV, 1)])
            banks.bfree(bank, [sV.now()])

    def reduce_norm(base, n, dst):
        rc.on("v", lambda v, i: v.tensor_reduce(
            out=B["kn2a"][0:2, :], in_=B["nrm"][:, base:base + n], axis=AX.X,
            op=ALU.max), waits=[sV.now()], incs=[(sV, 1)])
        rc.on("t", lambda t_, i: t_.transpose(
            out=pfF[0:1, 0:2], in_=B["kn2a"][0:2, 0:1],
            identity=B["ident_f32"][0:2, 0:2]),
            waits=[sV.now()] + banks.bwait(pfF), incs=[(sT, 1)])
        rc.on("v", lambda v, i, dst=dst: v.tensor_reduce(
            out=dst, in_=pfF[0:1, 0:2], axis=AX.X, op=ALU.max),
            waits=[sT.now()], incs=[(sV, 1)])
        banks.bfree(pfF, [sV.now()])

    def make_neg_bound(qsrc, ksrc, dst):
        rc.on("v", lambda v, i, qsrc=qsrc, ksrc=ksrc: v.tensor_tensor(
            out=B["pp1"][0:1, :], in0=qsrc, in1=ksrc, op=ALU.mult),
            waits=[sV.now()], incs=[(sV, 1)])
        rc.on("s", lambda s, i: s.activation(
            out=B["pp1"][0:1, :], in_=B["pp1"][0:1, :], func=AF.Ln,
            bias=B["tiny"][0:1, :]), waits=[sV.now()], incs=[(sS, 1)])
        rc.on("s", lambda s, i: s.activation(
            out=B["pp2"][0:1, :], in_=B["pp1"][0:1, :], func=AF.Exp, scale=0.5),
            waits=[sS.now()], incs=[(sS, 1)])
        rc.on("v", lambda v, i: v.tensor_scalar(
            out=B["pp2f"][0:1, :], in0=B["pp2"][0:1, :], scalar1=-0.125,
            scalar2=None, op0=ALU.mult), waits=[sS.now()], incs=[(sV, 1)])
        rc.on("t", lambda t_, i: t_.matmul(
            out=pfF[:, 0:1], lhsT=B["ones_row_f32"][0:1, :], rhs=B["pp2f"][0:1, 0:1],
            start=True, stop=True),
            waits=[sV.now()] + banks.bwait(pfF), incs=[(sT, 1)])
        rc.on("s", lambda s, i, dst=dst: s.copy(out=dst, in_=pfF[:, 0:1]),
            waits=[sT.now(), sS.now()], incs=[(sS, 1)])
        banks.bfree(pfF, [sS.now()])

    norm_cols(lambda j: rqb[:, j // 4, (j % 4) * 512:(j % 4) * 512 + 512], 8, NRQ,
              wait0=[rq_done])
    reduce_norm(NRQ, 8, B["qn2"][0:1, :])
    norm_cols(lambda j: rkf[:, j, :], 2, NRK, width=64)
    reduce_norm(NRK, 2, B["kn2"][0:1, :])
    make_neg_bound(B["qn2"][0:1, :], B["kn2"][0:1, :], B["negM2"][:])

    rc.on("g", lambda g, i: g.dma_start(
        out=slab[:, 0, :, :], in_=B["dr"]["L0_attn_qkv_w"][:, 1024:2048]
        .rearrange("(c p) m -> p c m", p=128)),
        waits=[sT.now()], incs=[(B["dSL0"], 16)])
    for oc in range(8):
        for kc in range(8):
            for tg in range(4):
                bank = banks.f32[tg]
                rc.on("t", lambda t_, i, oc=oc, tg=tg, kc=kc, bank=bank: t_.matmul(
                    out=bank[:], lhsT=slab[:, 0, kc, oc * 128:(oc + 1) * 128],
                    rhs=xnf[:, kc, tg * 512:(tg + 1) * 512],
                    start=(kc == 0), stop=(kc == 7)),
                    waits=([ln_done, B["dSL0"].now()] + banks.bwait(bank)
                           if kc == 0 else []),
                    incs=([(sT, 1)] if kc == 7 and tg == 3 else []))
        for tg in range(4):
            bank = banks.f32[tg]
            rc.on("v", lambda v, i, oc=oc, tg=tg, bank=bank: v.tensor_scalar(
                out=kbuf[:, oc, tg * 512:(tg + 1) * 512], in0=bank[:],
                scalar1=B["bias_pp"][:, BPP["l0_k"] + oc:BPP["l0_k"] + oc + 1],
                scalar2=None, op0=ALU.add),
                waits=([sT.now()] if tg == 0 else []), incs=[(sV, 1)])
            banks.bfree(bank, [sV.now()])
    k_done = sV.now()
    norm_cols(lambda j: kbuf[:, j // 4, (j % 4) * 512:(j % 4) * 512 + 512], 32, NK,
              wait0=[k_done])
    reduce_norm(NK, 32, B["kn2"][0:1, :])

    for qg in range(4):
        rc.on("g", lambda g, i: g.dma_start(
            out=slab[:, 0, :, :], in_=B["dr"]["L0_attn_qkv_w"][:, 0:1024]
            .rearrange("(c p) m -> p c m", p=128)),
            waits=[sT.now()], incs=[(B["dSL0"], 16)])
        for oc in range(8):
            bank = banks.f32[oc % 2]
            for kc in range(8):
                rc.on("t", lambda t_, i, oc=oc, kc=kc, qg=qg, bank=bank: t_.matmul(
                    out=bank[:], lhsT=slab[:, 0, kc, oc * 128:(oc + 1) * 128],
                    rhs=xnf[:, kc, qg * 512:(qg + 1) * 512],
                    start=(kc == 0), stop=(kc == 7)),
                    waits=([B["dSL0"].now(), sV.now()] + banks.bwait(bank)
                           if kc == 0 else []),
                    incs=([(sT, 1)] if kc == 7 else []))
            rc.on("v", lambda v, i, oc=oc, bank=bank: v.tensor_scalar(
                out=qgb[:, oc, :], in0=bank[:],
                scalar1=B["bias_pp"][:, BPP["l0_q"] + oc:BPP["l0_q"] + oc + 1],
                scalar2=None, op0=ALU.add), waits=[sT.now()], incs=[(sV, 1)])
            banks.bfree(bank, [sV.now()])
        q_done = sV.now()
        norm_cols(lambda j: qgb[:, j, :], 8, NQ, wait0=[q_done])
        reduce_norm(NQ, 8, B["qn2"][0:1, :])
        make_neg_bound(B["qn2"][0:1, :], B["kn2"][0:1, :], B["negM"][:])
        negm_ready = sS.now()

        for kt in range(qg * 4, qg * 4 + 4):
            for og in range(2):
                bank = banks.f32[(kt * 2 + og) % 2]
                rc.on("t", lambda t_, i, og=og, bank=bank: t_.matmul(
                    out=bank[:], lhsT=B["ones_row_bf"][0:1, :],
                    rhs=B["bias_row"][0:1, BROW["l0_v"] + og * 512:BROW["l0_v"] + og * 512 + 512],
                    start=True, stop=False),
                    waits=[B["dSL1"].now(), sS.now()] + banks.bwait(bank), incs=[])
                for kc in range(8):
                    rc.on("t", lambda t_, i, kt=kt, og=og, kc=kc, bank=bank: t_.matmul(
                        out=bank[:], lhsT=xnf[:, kc, kt * 128:(kt + 1) * 128],
                        rhs=slab[:, 1, kc, og * 512:(og + 1) * 512],
                        start=False, stop=(kc == 7)),
                        incs=([(sT, 1)] if kc == 7 else []))
                rc.on("s", lambda s, i, kt=kt, og=og, bank=bank: s.copy(
                    out=vt6[:, kt % 5, og * 8:(og + 1) * 8, 0:64],
                    in_=bank[:].rearrange("p (h d) -> p h d", d=64)),
                    waits=[sT.now()], incs=[(sS, 1)])
                banks.bfree(bank, [sS.now()])
        v_ready = sS.now()

        for h in range(16):
            c = h // 2
            r0 = (h % 2) * 64
            kts = [kt for kt in range(qg * 4 - 1, qg * 4 + 4) if kt >= 0]
            pv_q = []
            for ki, kt in enumerate(kts):
                j = kt - qg * 4
                c0 = max(0, j * 128)
                c1 = min(512, (j + 2) * 128)
                sbank = banks.f32[2 + kt % 2]
                rc.on("t", lambda t_, i, c=c, r0=r0, kt=kt, c0=c0, c1=c1,
                      sbank=sbank: t_.matmul(
                    out=sbank[:, c0:c1],
                    lhsT=kbuf[r0:r0 + 64, c, kt * 128:(kt + 1) * 128],
                    rhs=qgb[r0:r0 + 64, c, c0:c1], start=True, stop=True),
                    waits=[negm_ready, v_ready] + banks.bwait(sbank), incs=[(sT, 1)])
                rc.on("s", lambda s, i, kt=kt, c0=c0, c1=c1, sbank=sbank: s.activation(
                    out=B["ptile"][:, kt % 2, c0:c1], in_=sbank[:, c0:c1],
                    func=AF.Exp, bias=B["negM"][:], scale=0.125),
                    waits=[sT.now(), sS.now()], incs=[(sS, 1)])
                banks.bfree(sbank, [sS.now()])
                mspecs = []
                if j >= 0:
                    mspecs.append((j * 128, "mask_lt"))
                if (j + 2) * 128 <= 512 + 128 and (j + 1) * 128 < 512:
                    mspecs.append(((j + 1) * 128, "mask_ge"))
                for (mc0, mkey) in mspecs:
                    rc.on("v", lambda v, i, kt=kt, mc0=mc0, mkey=mkey: v.tensor_tensor(
                        out=B["ptile"][:, kt % 2, mc0:mc0 + 128],
                        in0=B["ptile"][:, kt % 2, mc0:mc0 + 128],
                        in1=B[mkey][:], op=ALU.mult),
                        waits=[sS.now()], incs=[(sV, 1)])
                pw = sV.now() if mspecs else sS.now()
                pv_q.append((kt, c0, c1, pw, ki))
                if len(pv_q) > 1:
                    _emit_local_pv(rc, B, banks, pfE, vt6, qg, h, pv_q.pop(0), sT)
            _emit_local_pv(rc, B, banks, pfE, vt6, qg, h, pv_q.pop(0), sT)
            rc.on("v", lambda v, i: v.tensor_scalar(
                out=B["lrow"][0:1, :], in0=pfE[64:65, 0:512], scalar1=1e-30,
                scalar2=None, op0=ALU.max), waits=[sT.now()], incs=[(sV, 1)])
            rc.on("v", lambda v, i: v.reciprocal(
                out=B["linv"][0:1, :], in_=B["lrow"][0:1, :]),
                waits=[sV.now()], incs=[(sV, 1)])
            rc.on("t", lambda t_, i: t_.matmul(
                out=pfF[0:64, :], lhsT=B["ones_row_f32"][0:1, 0:64],
                rhs=B["linv"][0:1, :], start=True, stop=True),
                waits=[sV.now()] + banks.bwait(pfF), incs=[(sT, 1)])
            rc.on("s", lambda s, i: s.copy(out=B["bcs"][0:64, :], in_=pfF[0:64, :]),
                waits=[sT.now(), sS.now()], incs=[(sS, 1)])
            banks.bfree(pfF, [sS.now()])
            rc.on("v", lambda v, i, c=c, r0=r0: v.tensor_tensor(
                out=localb[r0:r0 + 64, c, :], in0=pfE[0:64, 0:512],
                in1=B["bcs"][0:64, :], op=ALU.mult),
                waits=[sS.now()], incs=[(sV, 1)])
            banks.bfree(pfE, [sV.now()])

        for lh in range(4):
            lc = lh // 2
            lr = (lh % 2) * 64
            sbank = banks.f32[2 + lh % 2]
            rc.on("t", lambda t_, i, lc=lc, lr=lr, qg=qg, sbank=sbank: t_.matmul(
                out=sbank[0:64, :], lhsT=rkf[lr:lr + 64, lc, :],
                rhs=rqb[lr:lr + 64, lc, qg * 512:(qg + 1) * 512],
                start=True, stop=True),
                waits=[negm_ready] + banks.bwait(sbank), incs=[(sT, 1)])
            rc.on("s", lambda s, i, sbank=sbank: s.activation(
                out=B["lpt"][0:64, :], in_=sbank[0:64, :], func=AF.Exp,
                bias=B["negM2"][0:64, :], scale=0.125),
                waits=[sT.now(), sS.now(), sV.now()], incs=[(sS, 1)])
            banks.bfree(sbank, [sS.now()])
            rc.on("v", lambda v, i, qg=qg: v.tensor_tensor(
                out=B["lpt"][0:64, :], in0=B["lpt"][0:64, :],
                in1=B["mask_lat"][0:64, qg * 512:(qg + 1) * 512], op=ALU.mult),
                waits=[sS.now(), B["dCN"].now()], incs=[(sV, 1)])
            rc.on("t", lambda t_, i, lh=lh: t_.matmul(
                out=pfE[0:65, :], lhsT=rvt[0:64, lh, 0:65], rhs=B["lpt"][0:64, :],
                start=True, stop=True),
                waits=[sV.now()] + banks.bwait(pfE), incs=[(sT, 1)])
            rc.on("v", lambda v, i: v.tensor_scalar(
                out=B["lrow"][0:1, :], in0=pfE[64:65, 0:512], scalar1=1e-30,
                scalar2=None, op0=ALU.max), waits=[sT.now()], incs=[(sV, 1)])
            rc.on("v", lambda v, i: v.reciprocal(
                out=B["linv"][0:1, :], in_=B["lrow"][0:1, :]),
                waits=[sV.now()], incs=[(sV, 1)])
            rc.on("t", lambda t_, i: t_.matmul(
                out=pfF[0:64, :], lhsT=B["ones_row_f32"][0:1, 0:64],
                rhs=B["linv"][0:1, :], start=True, stop=True),
                waits=[sV.now()] + banks.bwait(pfF), incs=[(sT, 1)])
            rc.on("s", lambda s, i: s.copy(out=B["bcs"][0:64, :], in_=pfF[0:64, :]),
                waits=[sT.now(), sS.now()], incs=[(sS, 1)])
            banks.bfree(pfF, [sS.now()])
            rc.on("v", lambda v, i, lc=lc, lr=lr: v.tensor_tensor(
                out=latc[lr:lr + 64, lc, :], in0=pfE[0:64, 0:512],
                in1=B["bcs"][0:64, :], op=ALU.mult),
                waits=[sS.now()], incs=[(sV, 1)])
            banks.bfree(pfE, [sV.now()])
        lat_done = sV.now()

        for oc in range(8):
            bank = banks.f32[oc % 2]
            for kc in range(2):
                rc.on("t", lambda t_, i, oc=oc, kc=kc, bank=bank: t_.matmul(
                    out=bank[:], lhsT=B["routslab"][:, kc, oc * 128:(oc + 1) * 128],
                    rhs=latc[:, kc, :], start=(kc == 0), stop=(kc == 1)),
                    waits=([lat_done, B["dCB0"].now()] + banks.bwait(bank)
                           if kc == 0 else []),
                    incs=([(sT, 1)] if kc == 1 else []))
            rc.on("v", lambda v, i, oc=oc, bank=bank: v.tensor_scalar(
                out=remoteb[:, oc, :], in0=bank[:],
                scalar1=B["bias_pp"][:, BPP["l0_rout"] + oc:BPP["l0_rout"] + oc + 1],
                scalar2=None, op0=ALU.add), waits=[sT.now()], incs=[(sV, 1)])
            banks.bfree(bank, [sV.now()])
        rem_done = sV.now()

        for c in range(8):
            gb = banks.f32[5]
            rc.on("t", lambda t_, i, c=c, gb=gb: t_.matmul(
                out=gb[0:2, :], lhsT=B["gldiag"][:], rhs=localb[:, c, :],
                start=True, stop=False),
                waits=[rem_done, B["dCNG"].now()] + banks.bwait(gb), incs=[])
            rc.on("t", lambda t_, i, c=c, gb=gb: t_.matmul(
                out=gb[0:2, :], lhsT=B["grdiag"][:], rhs=remoteb[:, c, :],
                start=False, stop=True), incs=[(sT, 1)])
            rc.on("s", lambda s, i, gb=gb: s.activation(
                out=B["gsig"][0:2, :], in_=gb[0:2, :], func=AF.Exp, scale=-1.0),
                waits=[sT.now(), sV.now()], incs=[(sS, 1)])
            banks.bfree(gb, [sS.now()])
            rc.on("v", lambda v, i: v.tensor_scalar(
                out=B["gsig"][0:2, :], in0=B["gsig"][0:2, :], scalar1=1.0,
                scalar2=None, op0=ALU.add), waits=[sS.now()], incs=[(sV, 1)])
            rc.on("v", lambda v, i: v.reciprocal(
                out=B["gsig"][0:2, :], in_=B["gsig"][0:2, :]),
                waits=[sV.now()], incs=[(sV, 1)])
            gbc = banks.f32[5]
            rc.on("t", lambda t_, i, gbc=gbc: t_.matmul(
                out=gbc[:], lhsT=B["sel2"][:], rhs=B["gsig"][0:2, :],
                start=True, stop=True),
                waits=[sV.now()] + banks.bwait(gbc), incs=[(sT, 1)])
            rc.on("v", lambda v, i, c=c: v.tensor_tensor(
                out=B["scr1"][:], in0=localb[:, c, :], in1=remoteb[:, c, :],
                op=ALU.subtract), waits=[sT.now()], incs=[(sV, 1)])
            rc.on("v", lambda v, i, gbc=gbc: v.tensor_tensor(
                out=B["scr1"][:], in0=B["scr1"][:], in1=gbc[:], op=ALU.mult),
                waits=[sV.now()], incs=[(sV, 1)])
            banks.bfree(gbc, [sV.now()])
            rc.on("v", lambda v, i, c=c: v.tensor_tensor(
                out=localb[:, c, :], in0=B["scr1"][:], in1=remoteb[:, c, :],
                op=ALU.add), waits=[sV.now()], incs=[(sV, 1)])
        mix_done = sV.now()

        rc.on("g", lambda g, i: g.dma_start(
            out=slab[:, 0, :, :], in_=B["dr"]["L0_attn_out_w"]
            .rearrange("(c p) m -> p c m", p=128)),
            waits=[sT.now()], incs=[(B["dSL0"], 16)])
        for tl in range(4):
            tt = qg * 4 + tl
            xslot = tl % 2
            rc.on("y", lambda y, i, tt=tt, xslot=xslot: y.dma_start(
                out=B["xld2"][:, xslot, :], in_=xsrc_fn(i)[tt * 128:(tt + 1) * 128, :]),
                waits=B.pop(("xld2free", xslot), []), incs=[(B["dX2"][xslot], 16)])
            for og in range(2):
                bank = banks.f32[og % 2]
                rc.on("t", lambda t_, i, og=og, bank=bank: t_.matmul(
                    out=bank[:], lhsT=B["ones_row_bf"][0:1, :],
                    rhs=B["bias_row"][0:1, BROW["l0_out"] + og * 512:BROW["l0_out"] + og * 512 + 512],
                    start=True, stop=False),
                    waits=[mix_done, B["dSL0"].now()] + banks.bwait(bank), incs=[])
                for kc in range(8):
                    rc.on("t", lambda t_, i, tl=tl, og=og, kc=kc, bank=bank: t_.matmul(
                        out=bank[:], lhsT=localb[:, kc, tl * 128:(tl + 1) * 128],
                        rhs=slab[:, 0, kc, og * 512:(og + 1) * 512],
                        start=False, stop=(kc == 7)),
                        incs=([(sT, 1)] if kc == 7 else []))
                rc.on("v", lambda v, i, og=og, xslot=xslot, bank=bank: v.tensor_tensor(
                    out=B["yst"][:, xslot, og * 512:(og + 1) * 512], in0=bank[:],
                    in1=B["xld2"][:, xslot, og * 512:(og + 1) * 512], op=ALU.add),
                    waits=[sT.now(), B["dX2"][xslot].now()] + B.pop(("ystfree", xslot), []),
                    incs=[(sV, 1)])
                banks.bfree(bank, [sV.now()])
            rc.on("y", lambda y, i, tt=tt, xslot=xslot: y.dma_start(
                out=xdst_fn(i)[tt * 128:(tt + 1) * 128, :], in_=B["yst"][:, xslot, :]),
                waits=[sV.now()], incs=[(B["dST"][xslot], 16)])
            B[("xld2free", xslot)] = [sV.now()]
            B[("ystfree", xslot)] = [B["dST"][xslot].now()]


def _l0ffn_tq_flush(rc, B, banks, item, sT, sS, sV, sG):
    ig, tl, nic, hslot, hoff = item
    pb = banks.bf[tl % 2]
    for j in range(nic):
        rc.on("t", lambda t_, i, j=j, pb=pb, hslot=hslot, hoff=hoff: t_.transpose(
            out=pb[:, j * 128:(j + 1) * 128],
            in_=B["hq_t"][:, hslot, hoff + j * 128:hoff + (j + 1) * 128],
            identity=B["ident_bf"][:]),
            waits=([sG.now()] + banks.bwait(pb) if j == 0 else []),
            incs=([(sT, 1)] if j == nic - 1 else []))
    rc.on("s", lambda s, i, ig=ig, tl=tl, nic=nic, pb=pb: s.copy(
        out=B["hqf"][:, ig * 4:ig * 4 + nic, tl * 128:(tl + 1) * 128],
        in_=pb[:, 0:nic * 128].rearrange("p (c q) -> p c q", q=128)),
        waits=[sT.now()], incs=[(sS, 1)])
    banks.bfree(pb, [sS.now()])


def emit_phase_l0ffn(rc, B, xsrc_fn, xdst_fn):
    sV, sS, sT, sG = B["sV"], B["sS"], B["sT"], B["sG"]
    banks = B["banks"]
    xqf = B["xnf"]
    scr = B["scr"]

    lnrdy = []
    for t in range(16):
        emit_ln_tile(rc, B, xsrc_fn, t, t % 2, B["xnq"][:, t % 2, :])
        emit_quant_tile(rc, B, B["xnq"][:, t % 2, :], B["xqt"][:, t % 2, :], 1024,
                        src_wait=[sV.now()], dst_wait=[sT.now()], qsel=t % 2)
        emit_tile_to_feature(rc, B, B["xqt"][:, t % 2, :],
                             xqf[:, 0:8, t * 128:(t + 1) * 128], t % 2)
        B[("xldfree", t % 2)] = B[("xldfree", t % 2)] + [sS.now()]
        lnrdy.append(sS.now())
    ln_done = sS.now()

    for tg in range(4):
        units = [(ig, tl) for ig in range(6) for tl in range(4)]
        tq = []
        for ui, (ig, tl) in enumerate(units):
            w = 512 if ig < 5 else 256
            nic = w // 128
            if tl == 0:
                igs = ig % 2
                rc.on("y", lambda y, i, ig=ig, w=w, igs=igs: y.dma_start(
                    out=B["gslab"][:, igs, :, 0:w],
                    in_=scr["wp_l0g"][:, ig * 512:ig * 512 + w]
                    .rearrange("(c p) m -> p c m", p=128)),
                    waits=B.pop(("gsfree", igs), []),
                    incs=[(B["dGS" if igs == 0 else "dGS1"], 16)])
                rc.on("y", lambda y, i, ig=ig, w=w, igs=igs: y.dma_start(
                    out=B["uslab"][:, igs, :, 0:w],
                    in_=scr["wp_l0u"][:, ig * 512:ig * 512 + w]
                    .rearrange("(c p) m -> p c m", p=128)),
                    waits=B.pop(("usfree", igs), []),
                    incs=[(B["dUS" if igs == 0 else "dUS1"], 16)])
            igs = ig % 2
            tt = tg * 4 + tl
            gb = banks.f32[(ui % 2) * 2]
            ub = banks.f32[(ui % 2) * 2 + 1]
            for kc in range(8):
                rc.on("t", lambda t_, i, tt=tt, kc=kc, w=w, gb=gb, igs=igs: t_.matmul(
                    out=gb[:, 0:w], lhsT=xqf[:, kc, tt * 128:(tt + 1) * 128],
                    rhs=B["gslab"][:, igs, kc, 0:w], start=(kc == 0), stop=(kc == 7)),
                    waits=([B["dGS" if igs == 0 else "dGS1"].now(), lnrdy[tt]]
                           + banks.bwait(gb) if kc == 0 else []),
                    incs=([(sT, 1)] if kc == 7 else []))
            for kc in range(8):
                rc.on("t", lambda t_, i, tt=tt, kc=kc, w=w, ub=ub, igs=igs: t_.matmul(
                    out=ub[:, 0:w], lhsT=xqf[:, kc, tt * 128:(tt + 1) * 128],
                    rhs=B["uslab"][:, igs, kc, 0:w], start=(kc == 0), stop=(kc == 7)),
                    waits=([B["dUS" if igs == 0 else "dUS1"].now()]
                           + banks.bwait(ub) if kc == 0 else []),
                    incs=([(sT, 1)] if kc == 7 else []))
            if tl == 3:
                B[("gsfree", igs)] = [sT.now()]
                B[("usfree", igs)] = [sT.now()]
            if len(tq) > 2:
                _l0ffn_tq_flush(rc, B, banks, tq.pop(0), sT, sS, sV, sG)
            rc.on("s", lambda s, i, w=w, gb=gb, half=(ui % 2): s.activation(
                out=B["hstage"][:, half, 0:w], in_=gb[:, 0:w], func=AF.Silu),
                waits=[sT.now(), sV.now()], incs=[(sS, 1)])
            banks.bfree(gb, [sS.now()])
            half = ui % 2
            pslot = (ui // 2) % 2
            rc.on("v", lambda v, i, w=w, ub=ub, half=half, pslot=pslot: v.tensor_tensor(
                out=B["xnq"][:, pslot, half * w:(half + 1) * w],
                in0=B["hstage"][:, half, 0:w],
                in1=ub[:, 0:w], op=ALU.mult), waits=[sS.now()], incs=[(sV, 1)])
            banks.bfree(ub, [sV.now()])
            if half == 1:
                hslot = (ui // 2) % 2
                emit_quant_tile(rc, B, B["xnq"][:, pslot, 0:2 * w],
                                B["hq_t"][:, hslot, 0:2 * w],
                                2 * w, src_wait=[sV.now()], dst_wait=[sT.now()],
                                qsel=pslot)
                tq.append((ig, tl - 1, nic, hslot, 0))
                tq.append((ig, tl, nic, hslot, w))
        while tq:
            _l0ffn_tq_flush(rc, B, banks, tq.pop(0), sT, sS, sV, sG)
        hq_done = sS.now()

        dwn = ([(banks.f32[j], banks.f32[j][:]) for j in range(6)]
               + [(bb, bb[:].bitcast(F32)) for bb in banks.bf])
        for bko, _ in dwn:
            for wsp in banks.bwait(bko):
                rc.on("t", None, waits=[wsp])
        for og in range(2):
            rc.on("y", lambda y, i, og=og: y.dma_start(
                out=B["dslab"][:],
                in_=scr["wp_l0d"][:, og * 512:(og + 1) * 512]
                .rearrange("(c p) m -> p c m", p=128)),
                waits=B.pop(("dsfree2",), []), incs=[(B["dDS"], 16)])
            for tl in range(4):
                bk = dwn[tl * 2 + og][1]
                for kc in range(22):
                    rc.on("t", lambda t_, i, tl=tl, kc=kc, bk=bk: t_.matmul(
                        out=bk, lhsT=B["hqf"][:, kc, tl * 128:(tl + 1) * 128],
                        rhs=B["dslab"][:, kc, :], start=(kc == 0), stop=(kc == 21)),
                        waits=([hq_done, B["dDS"].now()] if kc == 0 else []),
                        incs=([(sT, 1)] if kc == 21 else []))
            B[("dsfree2",)] = [sT.now()]
        down_done = sT.now()
        for tl in range(4):
            ttg = tg * 4 + tl
            xslot = tl % 2
            rc.on("y", lambda y, i, ttg=ttg, xslot=xslot: y.dma_start(
                out=B["xld2"][:, xslot, :], in_=xsrc_fn(i)[ttg * 128:(ttg + 1) * 128, :]),
                waits=B.pop(("xld2free", xslot), []), incs=[(B["dX2"][xslot], 16)])
            for og in range(2):
                bko, bk = dwn[tl * 2 + og]
                rc.on("v", lambda v, i, og=og, xslot=xslot, bk=bk: v.tensor_tensor(
                    out=B["yst"][:, xslot, og * 512:(og + 1) * 512], in0=bk,
                    in1=B["xld2"][:, xslot, og * 512:(og + 1) * 512], op=ALU.add),
                    waits=[down_done, B["dX2"][xslot].now()] + B.pop(("ystfree", xslot), []),
                    incs=[(sV, 1)])
                banks.bfree(bko, [sV.now()])
            rc.on("y", lambda y, i, ttg=ttg, xslot=xslot: y.dma_start(
                out=xdst_fn(i)[ttg * 128:(ttg + 1) * 128, :], in_=B["yst"][:, xslot, :]),
                waits=[sV.now()], incs=[(B["dST"][xslot], 16)])
            B[("xld2free", xslot)] = [sV.now()]
            B[("ystfree", xslot)] = [B["dST"][xslot].now()]


def _in_maps(inputs, phases):
    consts = host_consts()
    bias_pp = _bias_pp_host(inputs)
    bias_row = _bias_row_host(inputs)
    x_full = np.ascontiguousarray(np.asarray(inputs["L0_attn_x"], dtype=np.float32))
    maps = []
    for c in range(8):
        m = {"x": x_full[c * NSEQ:(c + 1) * NSEQ]}
        for k, v in inputs.items():
            if k != "L0_attn_x":
                m[k] = np.ascontiguousarray(np.asarray(v, dtype=np.float32))
        for k, v in consts.items():
            m[k] = v
        m["c_bias_pp"] = bias_pp
        m["c_bias_row"] = bias_row
        maps.append(m)
    return maps


_PHASES_ALL = ["l0attn", "l0ffn", "l1attn", "l1ffn"]


def kernel(**inputs):
    nc = build_program(_PHASES_ALL)
    maps = _in_maps(inputs, _PHASES_ALL)
    res = run_bass_kernel_spmd(nc, maps, core_ids=list(range(8)))
    return np.concatenate([r["y"] for r in res.results], axis=0)


# version 09 (final)
# speedup vs baseline: 1.3380x; 1.0122x over previous
import contextlib
import numpy as np
import ml_dtypes
import concourse.bass as bass
import concourse.mybir as mybir
from concourse.bass_utils import run_bass_kernel_spmd

F32 = mybir.dt.float32
BF16 = mybir.dt.bfloat16
AF = mybir.ActivationFunctionType
ALU = mybir.AluOpType
AX = mybir.AxisListType

NSEQ = 4
S = 2048
D = 1024
NH = 16
HD = 64
NLAT = 64
CHUNK = 32
LHEADS = 4
IFF = 2816
NEXP = 16
MIDR = 256
MIDS = 512
RNE_C = 12582912.0

ENGS = ("t", "v", "s", "g", "y")


def host_consts():
    bf = ml_dtypes.bfloat16
    c = {}
    c["c_ident_bf"] = np.eye(128, dtype=bf)
    c["c_ident_f32"] = np.eye(128, dtype=np.float32)
    c["c_ones_row_bf"] = np.ones((1, 128), dtype=bf)
    c["c_ones_row_f32"] = np.ones((1, 128), dtype=np.float32)
    c["c_ones_col_bf"] = np.ones((128, 1), dtype=bf)
    hm = np.zeros((128, 2), dtype=bf)
    hm[0:64, 0] = 1
    hm[64:128, 1] = 1
    c["c_headmask"] = hm
    s2 = np.zeros((2, 128), dtype=np.float32)
    s2[0, 0:64] = 1
    s2[1, 64:128] = 1
    c["c_sel2"] = s2
    s16 = np.zeros((16, 16 * 128), dtype=bf)
    for e in range(16):
        s16[e, e * 128:(e + 1) * 128] = 1
    c["c_sel16"] = s16
    i_idx = np.arange(128)
    c["c_mask_k_lt_q"] = (i_idx[:, None] < i_idx[None, :]).astype(bf)
    c["c_mask_k_ge_q"] = (i_idx[:, None] >= i_idx[None, :]).astype(bf)
    c["c_mask_k_le_q"] = (i_idx[:, None] <= i_idx[None, :]).astype(bf)
    t_idx = np.arange(S)
    c["c_mask_lat"] = (np.arange(NLAT)[:, None] < (t_idx // CHUNK)[None, :]).astype(bf)
    return c


class RSem:
    __slots__ = ("name", "base", "q", "cur", "handle", "rec")

    def __init__(self, rec, name):
        self.rec = rec
        self.name = name
        self.base = 0
        self.q = 0
        self.cur = 0
        self.handle = None

    def _bump(self, n):
        if self.rec.seg == "prep":
            self.base += n
        else:
            self.cur += n

    def now(self, dit=0):
        if self.rec.seg == "prep":
            return ("abs", self, self.base)
        if self.rec.seg == "epi":
            return ("epi", self, self.cur)
        return ("lin", self, dit, self.cur)


class Rec:
    def __init__(self, nc, nseq=NSEQ):
        self.nc = nc
        self.nseq = nseq
        self.seg = "prep"
        self.cmds = {seg: {e: [] for e in ENGS} for seg in ("prep", "body", "epi")}
        self.sems = []
        self.ctx = contextlib.ExitStack()

    def sem(self, name):
        s = RSem(self, name)
        self.sems.append(s)
        return s

    def begin_body(self):
        self.seg = "body"
        for s in self.sems:
            s.cur = 0

    def end_body(self):
        for s in self.sems:
            s.q = s.cur
            s.cur = 0
        self.seg = "epi"

    def _resolve(self, spec, i):
        kind = spec[0]
        if kind == "abs":
            return spec[2]
        if kind == "epi":
            _, sem, const = spec
            return sem.base + self.nseq * sem.q + const
        _, sem, dit, const = spec
        flat = sem.base + const + dit * sem.q
        if sem.q == 0:
            return flat
        assert flat >= 0, (sem.name, dit, const, sem.base, sem.q)
        return i * sem.q + flat

    def on(self, e, fn, waits=(), incs=()):
        seg = self.seg
        wspecs = list(waits)
        ispecs = [(s, n) for (s, n) in incs]
        for s, n in ispecs:
            s._bump(n)
        rec = self

        def cmd(eng, i, regctx, fn=fn, wspecs=wspecs, ispecs=ispecs):
            for spec in wspecs:
                if spec[0] == "lin" and spec[1].q != 0 and regctx is not None:
                    sem, dit, const = spec[1], spec[2], spec[3]
                    flat = sem.base + const + dit * sem.q
                    assert flat >= 0, (sem.name, dit, const, sem.base, sem.q)
                    regctx.wait(eng, sem, flat)
                else:
                    eng.wait_ge(spec[1].handle, rec._resolve(spec, i))
            ins = fn(eng, i) if fn is not None else None
            if ispecs:
                if ins is None:
                    ins = eng.nop()
                ins = ins.then_inc(ispecs[0][0].handle, ispecs[0][1])
                for s, n in ispecs[1:]:
                    eng.nop().then_inc(s.handle, n)
            return ins

        self.cmds[seg][e].append(cmd)

    def build(self, block):
        for s in self.sems:
            s.handle = self.ctx.enter_context(self.nc.semaphore(s.name))
        for e in ENGS:
            if not (self.cmds["prep"][e] or self.cmds["body"][e] or self.cmds["epi"][e]):
                continue
            prep = self.cmds["prep"][e]
            body = self.cmds["body"][e]
            epi = self.cmds["epi"][e]
            dec = getattr(block, {"t": "tensor", "v": "vector", "s": "scalar",
                                  "g": "gpsimd", "y": "sync"}[e])

            def run(eng, prep=prep, body=body, epi=epi):
                for cmd in prep:
                    cmd(eng, 0, None)
                if body:
                    with eng.Fori(0, self.nseq) as i:
                        regctx = _RegCtx(eng, i)
                        for cmd in body:
                            cmd(eng, i, regctx)
                for cmd in epi:
                    cmd(eng, 0, None)

            dec(run)


class _RegCtx:

    def __init__(self, eng, i):
        self.eng = eng
        self.i = i
        self.ireg = eng.to_reg(i)
        self.scratch = eng.alloc_register("waitval")
        self.last = None

    def wait(self, eng, sem, flat):
        key = (id(sem), sem.q, flat)
        if self.last != key:
            eng.reg_mul(self.scratch, self.ireg, sem.q)
            eng.reg_add(self.scratch, self.scratch, flat)
            self.last = key
        eng.wait_ge(sem.handle, self.scratch)


def emit_phase_l1attn(rc, B, xsrc_fn, xdst_fn):
    sV, sS, sT, sG = B["sV"], B["sS"], B["sT"], B["sG"]
    banks = B["banks"]
    pfE, pfF = banks.f32[4], banks.f32[5]
    xnf = B["xnf"]
    kbuf = B["kbuf"]
    vt = B["vt"]
    qgb = B["qgb"]
    ctxb = B["ctxb"]
    slab = B["slab"]
    wsrc = B["wp_l1qkvo"]

    rc.on("v", lambda v, i: v.memset(vt[:, :, :, 64:65], 1.0), incs=[(sV, 1)])

    for t in range(16):
        emit_ln_tile(rc, B, xsrc_fn, t, t % 2, B["xnt"][:, t % 2, :])
        emit_tile_to_feature(rc, B, B["xnt"][:, t % 2, :],
                             xnf[:, 0:8, t * 128:(t + 1) * 128], t % 2)
        B[("xldfree", t % 2)] = B[("xldfree", t % 2)] + [sS.now()]
        B[("lnrdy_l1", t)] = sS.now()

    wnames = ["L1_attn_Wq", "L1_attn_Wk", "L1_attn_Wv", "L1_attn_Wo"]

    def slab_load(slot, widx, waitspecs):
        rc.on("g", lambda g, i, slot=slot, widx=widx: g.dma_start(
            out=slab[:, slot, :, :],
            in_=B["dr"][wnames[widx]].rearrange("(c p) m -> p c m", p=128)),
            waits=waitspecs, incs=[(B["dSL"][slot], 16)])

    slab_load(0, 1, [])
    slab_load(1, 2, [])
    wk_loaded = B["dSL0"].now()

    wv_loaded = B["dSL1"].now()
    for tt in range(16):
        for og in range(2):
            bank = banks.f32[(tt * 2 + og) % 2]
            rc.on("t", lambda t_, i, og=og, bank=bank: t_.matmul(
                out=bank[:], lhsT=B["ones_row_bf"][0:1, :],
                rhs=B["bias_row"][0:1, BROW["l1_bv"] + og * 512:BROW["l1_bv"] + og * 512 + 512],
                start=True, stop=False),
                waits=[wv_loaded, B[("lnrdy_l1", tt)]] + banks.bwait(bank), incs=[])
            for kc in range(8):
                rc.on("t", lambda t_, i, tt=tt, og=og, kc=kc, bank=bank: t_.matmul(
                    out=bank[:], lhsT=xnf[:, kc, tt * 128:(tt + 1) * 128],
                    rhs=slab[:, 1, kc, og * 512:(og + 1) * 512],
                    start=False, stop=(kc == 7)),
                    incs=([(sT, 1)] if kc == 7 else []))
            rc.on("s", lambda s, i, tt=tt, og=og, bank=bank: s.copy(
                out=vt[:, tt, og * 8:(og + 1) * 8, 0:64],
                in_=bank[:].rearrange("p (h d) -> p h d", d=64)),
                waits=[sT.now()], incs=[(sS, 1)])
            banks.bfree(bank, [sS.now()])
    v_done = sS.now()

    for oc in range(8):
        for kc in range(8):
            for tg in range(4):
                bank = banks.f32[tg]
                rc.on("t", lambda t_, i, oc=oc, tg=tg, kc=kc, bank=bank: t_.matmul(
                    out=bank[:], lhsT=slab[:, 0, kc, oc * 128:(oc + 1) * 128],
                    rhs=xnf[:, kc, tg * 512:(tg + 1) * 512],
                    start=(kc == 0), stop=(kc == 7)),
                    waits=([wk_loaded, sS.now()] + banks.bwait(bank)
                           if kc == 0 else []),
                    incs=([(sT, 1)] if kc == 7 and tg == 3 else []))
        for tg in range(4):
            bank = banks.f32[tg]
            rc.on("v", lambda v, i, oc=oc, tg=tg, bank=bank: v.tensor_scalar(
                out=kbuf[:, oc, tg * 512:(tg + 1) * 512], in0=bank[:],
                scalar1=B["bias_pp"][:, BPP["l1_bk"] + oc:BPP["l1_bk"] + oc + 1],
                scalar2=None, op0=ALU.add),
                waits=([sT.now()] if tg == 0 else []), incs=[(sV, 1)])
            banks.bfree(bank, [sV.now()])
    kq_done = sV.now()

    for oc in range(8):
        rc.on("s", lambda s, i, oc=oc: s.activation(
            out=B["sqscr"][:], in_=kbuf[:, oc, :], func=AF.Square),
            waits=[kq_done, sT.now(), sS.now()], incs=[(sS, 1)])
        for tg in range(4):
            bank = banks.f32[2 + (oc * 4 + tg) % 2]
            rc.on("t", lambda t_, i, oc=oc, tg=tg, bank=bank: t_.matmul(
                out=bank[0:2, :], lhsT=B["headmask"][:],
                rhs=B["sqscr"][:, tg * 512:(tg + 1) * 512], start=True, stop=True),
                waits=[sS.now()] + banks.bwait(bank), incs=[(sT, 1)])
            rc.on("v", lambda v, i, oc=oc, tg=tg, bank=bank: v.tensor_reduce(
                out=B["nrm"][:, oc * 4 + tg:oc * 4 + tg + 1], in_=bank[0:2, :],
                axis=AX.X, op=ALU.max), waits=[sT.now()], incs=[(sV, 1)])
            banks.bfree(bank, [sV.now()])
    rc.on("v", lambda v, i: v.tensor_reduce(
        out=B["kn2a"][0:2, :], in_=B["nrm"][:, 0:32], axis=AX.X, op=ALU.max),
        waits=[sV.now()], incs=[(sV, 1)])
    rc.on("t", lambda t_, i: t_.transpose(
        out=pfF[0:1, 0:2], in_=B["kn2a"][0:2, 0:1], identity=B["ident_f32"][0:2, 0:2]),
        waits=[sV.now()] + banks.bwait(pfF), incs=[(sT, 1)])
    rc.on("v", lambda v, i: v.tensor_reduce(
        out=B["kn2"][0:1, :], in_=pfF[0:1, 0:2], axis=AX.X, op=ALU.max),
        waits=[sT.now()], incs=[(sV, 1)])
    banks.bfree(pfF, [sV.now()])

    slab_load(0, 0, [kq_done, sT.now()])
    wq_loaded = B["dSL0"].now()
    slab_load(1, 3, [v_done, sT.now()])
    wo_loaded = B["dSL1"].now()

    for qg in range(4):
        for oc in range(8):
            bank = banks.f32[oc % 2]
            for kc in range(8):
                rc.on("t", lambda t_, i, oc=oc, kc=kc, qg=qg, bank=bank: t_.matmul(
                    out=bank[:], lhsT=slab[:, 0, kc, oc * 128:(oc + 1) * 128],
                    rhs=xnf[:, kc, qg * 512:(qg + 1) * 512],
                    start=(kc == 0), stop=(kc == 7)),
                    waits=([wq_loaded, sV.now()] + banks.bwait(bank) if kc == 0 else []),
                    incs=([(sT, 1)] if kc == 7 else []))
            rc.on("v", lambda v, i, oc=oc, bank=bank: v.tensor_scalar(
                out=qgb[:, oc, :], in0=bank[:],
                scalar1=B["bias_pp"][:, BPP["l1_bq"] + oc:BPP["l1_bq"] + oc + 1],
                scalar2=None, op0=ALU.add), waits=[sT.now()], incs=[(sV, 1)])
            banks.bfree(bank, [sV.now()])
        q_done = sV.now()
        for oc in range(8):
            rc.on("s", lambda s, i, oc=oc: s.activation(
                out=B["sqscr"][:, 0:512], in_=qgb[:, oc, :], func=AF.Square),
                waits=[q_done, sT.now(), sS.now()], incs=[(sS, 1)])
            bank = banks.f32[2 + oc % 2]
            rc.on("t", lambda t_, i, oc=oc, bank=bank: t_.matmul(
                out=bank[0:2, :], lhsT=B["headmask"][:], rhs=B["sqscr"][:, 0:512],
                start=True, stop=True), waits=[sS.now()] + banks.bwait(bank),
                incs=[(sT, 1)])
            rc.on("v", lambda v, i, oc=oc, bank=bank: v.tensor_reduce(
                out=B["nrm"][:, 32 + oc:33 + oc], in_=bank[0:2, :], axis=AX.X,
                op=ALU.max), waits=[sT.now()], incs=[(sV, 1)])
            banks.bfree(bank, [sV.now()])
        rc.on("v", lambda v, i: v.tensor_reduce(
            out=B["kn2a"][0:2, :], in_=B["nrm"][:, 32:40], axis=AX.X, op=ALU.max),
            waits=[sV.now()], incs=[(sV, 1)])
        rc.on("t", lambda t_, i: t_.transpose(
            out=pfF[0:1, 2:4], in_=B["kn2a"][0:2, 0:1], identity=B["ident_f32"][0:2, 0:2]),
            waits=[sV.now()] + banks.bwait(pfF), incs=[(sT, 1)])
        rc.on("v", lambda v, i: v.tensor_reduce(
            out=B["qn2"][0:1, :], in_=pfF[0:1, 2:4], axis=AX.X, op=ALU.max),
            waits=[sT.now()], incs=[(sV, 1)])
        banks.bfree(pfF, [sV.now()])
        rc.on("v", lambda v, i: v.tensor_tensor(
            out=B["pp1"][0:1, :], in0=B["qn2"][0:1, :], in1=B["kn2"][0:1, :],
            op=ALU.mult), waits=[sV.now()], incs=[(sV, 1)])
        rc.on("s", lambda s, i: s.activation(
            out=B["pp1"][0:1, :], in_=B["pp1"][0:1, :], func=AF.Ln,
            bias=B["tiny"][0:1, :]), waits=[sV.now()], incs=[(sS, 1)])
        rc.on("s", lambda s, i: s.activation(
            out=B["pp2"][0:1, :], in_=B["pp1"][0:1, :], func=AF.Exp, scale=0.5),
            waits=[sS.now()], incs=[(sS, 1)])
        rc.on("v", lambda v, i: v.tensor_scalar(
            out=B["pp2f"][0:1, :], in0=B["pp2"][0:1, :], scalar1=-0.125,
            scalar2=None, op0=ALU.mult), waits=[sS.now()], incs=[(sV, 1)])
        rc.on("t", lambda t_, i: t_.matmul(
            out=pfF[:, 0:1], lhsT=B["ones_row_f32"][0:1, :], rhs=B["pp2f"][0:1, 0:1],
            start=True, stop=True),
            waits=[sV.now()] + banks.bwait(pfF), incs=[(sT, 1)])
        rc.on("s", lambda s, i: s.copy(out=B["negM"][:], in_=pfF[:, 0:1]),
            waits=[sT.now()], incs=[(sS, 1)])
        banks.bfree(pfF, [sS.now()])
        negm_ready = sS.now()

        nkt = 4 * qg + 4
        norm_q = []

        def l1_norm_tail(item):
            h2, c2, r2, ctxbank2 = item
            rc.on("t", lambda t_, i: t_.matmul(
                out=pfF[0:64, :], lhsT=B["ones_row_f32"][0:1, 0:64],
                rhs=B["linv"][0:1, :], start=True, stop=True),
                waits=[sV.now()] + banks.bwait(pfF), incs=[(sT, 1)])
            rc.on("s", lambda s, i: s.copy(out=B["bcs"][0:64, :], in_=pfF[0:64, :]),
                waits=[sT.now(), sS.now()], incs=[(sS, 1)])
            banks.bfree(pfF, [sS.now()])
            rc.on("v", lambda v, i, c=c2, r0=r2, ctxbank=ctxbank2: v.tensor_tensor(
                out=ctxb[r0:r0 + 64, c, :], in0=ctxbank[0:64, 0:512],
                in1=B["bcs"][0:64, :], op=ALU.mult),
                waits=[sS.now()], incs=[(sV, 1)])
            banks.bfree(ctxbank2, [sV.now()])

        for h in range(16):
            c = h // 2
            r0 = (h % 2) * 64
            ctxbank = banks.f32[2 + h % 2]
            pv_q = []
            for kt in range(nkt):
                joff = max(0, kt * 128 - qg * 512)
                sbank = banks.f32[kt % 2]
                rc.on("t", lambda t_, i, c=c, r0=r0, kt=kt, joff=joff, qg=qg,
                      sbank=sbank: t_.matmul(
                    out=sbank[:, joff:512],
                    lhsT=kbuf[r0:r0 + 64, c, kt * 128:(kt + 1) * 128],
                    rhs=qgb[r0:r0 + 64, c, joff:512], start=True, stop=True),
                    waits=[negm_ready] + banks.bwait(sbank), incs=[(sT, 1)])
                rc.on("s", lambda s, i, kt=kt, joff=joff, sbank=sbank: s.activation(
                    out=B["ptile"][:, kt % 2, joff:512], in_=sbank[:, joff:512],
                    func=AF.Exp, bias=B["negM"][:], scale=0.125),
                    waits=[sT.now(), sS.now()], incs=[(sS, 1)])
                banks.bfree(sbank, [sS.now()])
                if kt >= 4 * qg:
                    rc.on("v", lambda v, i, kt=kt, joff=joff: v.tensor_tensor(
                        out=B["ptile"][:, kt % 2, joff:joff + 128],
                        in0=B["ptile"][:, kt % 2, joff:joff + 128],
                        in1=B["mask_le"][:], op=ALU.mult),
                        waits=[sS.now()], incs=[(sV, 1)])
                    pw = sV.now()
                else:
                    pw = sS.now()
                pv_q.append((kt, joff, pw))
                if kt == 1 and norm_q:
                    l1_norm_tail(norm_q.pop(0))
                if len(pv_q) > 1:
                    k2, j2, pw2 = pv_q.pop(0)
                    rc.on("t", lambda t_, i, h=h, kt=k2, joff=j2, nkt=nkt,
                          ctxbank=ctxbank: t_.matmul(
                        out=ctxbank[0:65, joff:512], lhsT=vt[:, kt, h, 0:65],
                        rhs=B["ptile"][:, kt % 2, joff:512],
                        start=(kt == 0), stop=(kt == nkt - 1)),
                        waits=[pw2] + (banks.bwait(ctxbank) if k2 == 0 else []),
                        incs=[(sT, 1)])
            k2, j2, pw2 = pv_q.pop(0)
            rc.on("t", lambda t_, i, h=h, kt=k2, joff=j2, nkt=nkt,
                  ctxbank=ctxbank: t_.matmul(
                out=ctxbank[0:65, joff:512], lhsT=vt[:, kt, h, 0:65],
                rhs=B["ptile"][:, kt % 2, joff:512],
                start=(kt == 0), stop=(kt == nkt - 1)),
                waits=[pw2] + (banks.bwait(ctxbank) if k2 == 0 else []),
                incs=[(sT, 1)])
            rc.on("v", lambda v, i, ctxbank=ctxbank: v.tensor_scalar(
                out=B["lrow"][0:1, :], in0=ctxbank[64:65, 0:512], scalar1=1e-30,
                scalar2=None, op0=ALU.max), waits=[sT.now(), sV.now()], incs=[(sV, 1)])
            rc.on("v", lambda v, i: v.reciprocal(
                out=B["linv"][0:1, :], in_=B["lrow"][0:1, :]),
                waits=[sV.now()], incs=[(sV, 1)])
            norm_q.append((h, c, r0, ctxbank))
        l1_norm_tail(norm_q.pop(0))

        ctx_done = sV.now()
        for tl in range(4):
            tt = qg * 4 + tl
            xslot = tl % 2
            rc.on("y", lambda y, i, tt=tt, xslot=xslot: y.dma_start(
                out=B["xld2"][:, xslot, :], in_=xsrc_fn(i)[tt * 128:(tt + 1) * 128, :]),
                waits=B.pop(("xld2free", xslot), []), incs=[(B["dX2"][xslot], 16)])
            for og in range(2):
                bank = banks.f32[og % 2]
                rc.on("t", lambda t_, i, og=og, bank=bank: t_.matmul(
                    out=bank[:], lhsT=B["ones_row_bf"][0:1, :],
                    rhs=B["bias_row"][0:1, BROW["l1_bo"] + og * 512:BROW["l1_bo"] + og * 512 + 512],
                    start=True, stop=False),
                    waits=[ctx_done, wo_loaded] + banks.bwait(bank), incs=[])
                for kc in range(8):
                    rc.on("t", lambda t_, i, tl=tl, og=og, kc=kc, bank=bank: t_.matmul(
                        out=bank[:], lhsT=ctxb[:, kc, tl * 128:(tl + 1) * 128],
                        rhs=slab[:, 1, kc, og * 512:(og + 1) * 512],
                        start=False, stop=(kc == 7)),
                        incs=([(sT, 1)] if kc == 7 else []))
                rc.on("v", lambda v, i, og=og, xslot=xslot, bank=bank: v.tensor_tensor(
                    out=B["yst"][:, xslot, og * 512:(og + 1) * 512], in0=bank[:],
                    in1=B["xld2"][:, xslot, og * 512:(og + 1) * 512], op=ALU.add),
                    waits=[sT.now(), B["dX2"][xslot].now()] + B.pop(("ystfree", xslot), []),
                    incs=[(sV, 1)])
                banks.bfree(bank, [sV.now()])
            rc.on("y", lambda y, i, tt=tt, xslot=xslot: y.dma_start(
                out=xdst_fn(i)[tt * 128:(tt + 1) * 128, :], in_=B["yst"][:, xslot, :]),
                waits=[sV.now()], incs=[(B["dST"][xslot], 16)])
            B[("xld2free", xslot)] = [sV.now()]
            B[("ystfree", xslot)] = [B["dST"][xslot].now()]


class Arena:

    def __init__(self, nc, ctx, cols_f32):
        self.t = ctx.enter_context(nc.sbuf_tensor("arena", [128, cols_f32], F32))
        self.cols = cols_f32
        self.cur = 0
        self.mark = 0
        self.hwm = 0

    def persist_done(self):
        self.mark = self.cur

    def phase_reset(self):
        self.cur = self.mark

    @property
    def cur_bf(self):
        return self.cur

    @cur_bf.setter
    def cur_bf(self, v):
        self.cur = v

    @property
    def cur_f32(self):
        return self.cur

    @cur_f32.setter
    def cur_f32(self, v):
        pass

    def view(self, shape, dt=BF16, parts=128):
        n = int(np.prod(shape))
        nf32 = (n + 1) // 2 if dt == BF16 else n
        off = self.cur
        self.cur = off + nf32
        self.hwm = max(self.hwm, self.cur)
        assert self.cur <= self.cols, f"arena overflow {self.cur} > {self.cols}"
        ap = self.t[0:parts, off:off + nf32]
        if dt == BF16:
            ap = ap.bitcast(BF16)
            if n % 2:
                ap = ap[:, 0:n]
        if len(shape) > 1:
            letters = "abcdef"[: len(shape)]
            kw = {letters[j]: shape[j] for j in range(1, len(shape))}
            ap = ap.rearrange("p (" + " ".join(letters) + ") -> p " + " ".join(letters), **kw)
        return ap


class PBanks:

    def __init__(self, nc, ctx, rc):
        self.f32 = [ctx.enter_context(nc.psum_tensor(f"pb{j}", [128, 512], F32))
                    for j in range(6)]
        self.bf = [ctx.enter_context(nc.psum_tensor(f"pt{j}", [128, 1024], BF16))
                   for j in range(2)]
        self.guard = {}

    def bwait(self, bank):
        return list(self.guard.get(id(bank), []))

    def bfree(self, bank, specs):
        self.guard[id(bank)] = list(specs)


def _bias_pp_host(I):
    cols = []
    def chunks(v):
        return [v[c * 128:(c + 1) * 128] for c in range(len(v) // 128)]
    cols += chunks(I["L1_attn_bq"])
    cols += chunks(I["L1_attn_bk"])
    cols += chunks(I["L0_attn_qkv_b"][0:1024])
    cols += chunks(I["L0_attn_qkv_b"][1024:2048])
    cols += chunks(I["L0_attn_rq_b"])
    cols += chunks(I["L0_attn_rkv_b"][0:256])
    cols += chunks(I["L0_attn_rout_b"])
    return np.stack(cols, axis=1).astype(np.float32)

BPP = dict(l1_bq=0, l1_bk=8, l0_q=16, l0_k=24, l0_rq=32, l0_rk=34, l0_rout=36)

def _bias_row_host(I):
    v = np.concatenate([
        I["L1_attn_bv"],
        I["L1_attn_bo"],
        I["L0_attn_qkv_b"][2048:3072],
        I["L0_attn_out_b"],
        I["L0_attn_rkv_b"][256:512],
    ]).astype(np.float32)
    return v[None, :]

BROW = dict(l1_bv=0, l1_bo=1024, l0_v=2048, l0_out=3072, l0_rv=4096)
NROW = 4352

PHASE_IO = {
    "l0attn": ("x", "xres"),
    "l0ffn": ("xres", "xres"),
    "l1attn": ("xres", "xres"),
    "l1ffn": ("xres", "y"),
}


def build_program(phases, dev_io=False, nseq=NSEQ):
    nc = bass.Bass("TRN2", target_bir_lowering=False, debug=False)
    dr = {}
    dr["x"] = nc.dram_tensor("x", [NSEQ, S, D], F32, kind="ExternalInput").ap()
    dr["y"] = nc.dram_tensor("y", [NSEQ, S, D], F32, kind="ExternalOutput").ap()
    dr["xres"] = nc.dram_tensor("xres", [NSEQ, S, D], F32, kind="Internal").ap()
    wnames = [
        ("L0_attn_qkv_w", [D, 3 * D]), ("L0_attn_qkv_b", [3 * D]),
        ("L0_attn_rq_w", [D, 256]), ("L0_attn_rq_b", [256]),
        ("L0_attn_rkv_w", [D, 512]), ("L0_attn_rkv_b", [512]),
        ("L0_attn_rout_w", [256, D]), ("L0_attn_rout_b", [D]),
        ("L0_attn_out_w", [D, D]), ("L0_attn_out_b", [D]),
        ("L0_attn_gl_w", [64, 1]), ("L0_attn_gr_w", [64, 1]),
        ("L0_ffn_w_gate", [IFF, D]), ("L0_ffn_w_up", [IFF, D]),
        ("L0_ffn_w_down", [D, IFF]),
        ("L1_attn_Wq", [D, D]), ("L1_attn_bq", [D]),
        ("L1_attn_Wk", [D, D]), ("L1_attn_bk", [D]),
        ("L1_attn_Wv", [D, D]), ("L1_attn_bv", [D]),
        ("L1_attn_Wo", [D, D]), ("L1_attn_bo", [D]),
        ("L1_ffn_router_w", [D, NEXP]),
        ("L1_ffn_wg_r", [NEXP, D, MIDR]), ("L1_ffn_wu_r", [NEXP, D, MIDR]),
        ("L1_ffn_wd_r", [NEXP, MIDR, D]),
        ("L1_ffn_wg_s", [2, D, MIDS]), ("L1_ffn_wu_s", [2, D, MIDS]),
        ("L1_ffn_wd_s", [2, MIDS, D]),
    ]
    for nm, sh in wnames:
        dr[nm] = nc.dram_tensor(nm, sh, F32, kind="ExternalInput").ap()
    cvals = host_consts()
    for nm, arr in cvals.items():
        dt = BF16 if arr.dtype == ml_dtypes.bfloat16 else F32
        dr[nm] = nc.dram_tensor(nm, list(arr.shape), dt, kind="ExternalInput").ap()
    dr["c_bias_pp"] = nc.dram_tensor("c_bias_pp", [128, 44], F32, kind="ExternalInput").ap()
    dr["c_bias_row"] = nc.dram_tensor("c_bias_row", [1, NROW], F32, kind="ExternalInput").ap()
    scr = {}
    scr["wp_l1qkvo"] = nc.dram_tensor("wp_l1qkvo", [4, D, D], BF16, kind="Internal").ap()
    scr["wp_l0qkv"] = nc.dram_tensor("wp_l0qkv", [D, 3 * D], BF16, kind="Internal").ap()
    scr["wp_l0out"] = nc.dram_tensor("wp_l0out", [D, D], BF16, kind="Internal").ap()
    scr["wp_l0rq"] = nc.dram_tensor("wp_l0rq", [D, 256], BF16, kind="Internal").ap()
    scr["wp_l0rkv"] = nc.dram_tensor("wp_l0rkv", [D, 512], BF16, kind="Internal").ap()
    scr["wp_l0rout"] = nc.dram_tensor("wp_l0rout", [256, D], BF16, kind="Internal").ap()
    scr["wp_l0g"] = nc.dram_tensor("wp_l0g", [D, IFF], BF16, kind="Internal").ap()
    scr["wp_l0u"] = nc.dram_tensor("wp_l0u", [D, IFF], BF16, kind="Internal").ap()
    scr["wp_l0d"] = nc.dram_tensor("wp_l0d", [IFF, D], BF16, kind="Internal").ap()
    scr["wp_l1rg"] = nc.dram_tensor("wp_l1rg", [NEXP, D, MIDR], BF16, kind="Internal").ap()
    scr["wp_l1ru"] = nc.dram_tensor("wp_l1ru", [NEXP, D, MIDR], BF16, kind="Internal").ap()
    scr["wp_l1rd"] = nc.dram_tensor("wp_l1rd", [NEXP, MIDR, D], BF16, kind="Internal").ap()
    scr["wp_l1sg"] = nc.dram_tensor("wp_l1sg", [2, D, MIDS], BF16, kind="Internal").ap()
    scr["wp_l1su"] = nc.dram_tensor("wp_l1su", [2, D, MIDS], BF16, kind="Internal").ap()
    scr["wp_l1sd"] = nc.dram_tensor("wp_l1sd", [2, MIDS, D], BF16, kind="Internal").ap()
    scr["wp_router"] = nc.dram_tensor("wp_router", [D, NEXP], BF16, kind="Internal").ap()

    rc = Rec(nc, nseq)
    with rc.ctx:
        arena = Arena(nc, rc.ctx, cols_f32=53180)
        banks = PBanks(nc, rc.ctx, rc)
        B = {"banks": banks, "dr": dr, "scr": scr, "arena": arena}
        for snm in ("sT", "sV", "sS", "sG", "sJ"):
            B[snm] = rc.sem(snm)
        for snm in ("dXL0", "dXL1", "dX20", "dX21", "dST0", "dST1",
                    "dSL0", "dSL1", "dCB0", "dCB1", "dCS0", "dCS1",
                    "dPS0", "dPS1", "dCN", "dCNG",
                    "dGS", "dGS1", "dUS", "dUS1", "dDS", "dES0", "dES1", "dUS20", "dUS21",
                    "dDS20", "dDS21", "dSG", "dSU", "dSDW", "dRT", "dGA", "dGB"):
            B[snm] = rc.sem(snm)
        B["dXL"] = [B["dXL0"], B["dXL1"]]
        B["dX2"] = [B["dX20"], B["dX21"]]
        B["dST"] = [B["dST0"], B["dST1"]]
        B["dSL"] = [B["dSL0"], B["dSL1"]]
        B["dCB"] = [B["dCB0"], B["dCB1"]]
        B["dCS"] = [B["dCS0"], B["dCS1"]]
        B["dPS"] = [B["dPS0"], B["dPS1"]]
        sY, sGD = B["dCN"], B["dCNG"]

        def pview(shape, dt=BF16, parts=128):
            return arena.view(shape, dt, parts)

        B["ident_bf"] = pview([128], BF16)
        B["ident_f32"] = pview([128], F32)
        B["ones_row_bf"] = pview([128], BF16, parts=1)
        B["ones_row_f32"] = pview([128], F32, parts=1)
        B["headmask"] = pview([2], BF16)
        B["sel2"] = pview([128], F32, parts=2)
        B["mask_lt"] = pview([128], BF16)
        B["mask_ge"] = pview([128], BF16)
        B["mask_le"] = pview([128], BF16)
        B["bias_pp"] = pview([44], F32)
        B["bias_row"] = pview([NROW], BF16, parts=1)
        B["gldiag"] = pview([2], BF16)
        B["grdiag"] = pview([2], BF16)
        B["eps"] = pview([1], F32)
        B["tiny"] = pview([1], F32, parts=1)
        for nm in ("lnms", "lnvs", "lnm", "lnmsq", "lnvar", "lnt", "lnr"):
            B[nm] = pview([2], F32)
        B["dump"] = pview([1024], BF16)
        B["xld"] = pview([2, 1024], F32)
        B["xld2"] = pview([2, 1024], F32)
        B["yst"] = pview([2, 1024], F32)
        B["nrm"] = pview([56], F32, parts=2)
        B["kn2a"] = pview([1], F32, parts=2)
        B["kn2"] = pview([1], F32, parts=1)
        B["qn2"] = pview([1], F32, parts=1)
        B["pp1"] = pview([1], F32, parts=1)
        B["pp2"] = pview([1], F32, parts=1)
        B["pp2f"] = pview([1], F32, parts=1)
        B["negM"] = pview([1], F32)
        B["negM2"] = pview([1], F32)
        for k, v in scr.items():
            B[k] = v
        for k in list(dr.keys()):
            B.setdefault(k, dr[k])
        arena.persist_done()

        def cload(dst_ap, key, cast=False):
            if cast:
                rc.on("g", lambda g, i, dst_ap=dst_ap, key=key:
                      g.dma_start(out=dst_ap, in_=dr[key]), incs=[(sGD, 16)])
            else:
                rc.on("y", lambda y, i, dst_ap=dst_ap, key=key:
                      y.dma_start(out=dst_ap, in_=dr[key]), incs=[(sY, 16)])

        cload(B["ident_bf"], "c_ident_bf")
        cload(B["ident_f32"], "c_ident_f32")
        cload(B["ones_row_bf"], "c_ones_row_bf")
        cload(B["ones_row_f32"], "c_ones_row_f32")
        cload(B["headmask"], "c_headmask")
        cload(B["sel2"], "c_sel2")
        cload(B["mask_lt"], "c_mask_k_lt_q")
        cload(B["mask_ge"], "c_mask_k_ge_q")
        cload(B["mask_le"], "c_mask_k_le_q")
        cload(B["bias_pp"], "c_bias_pp")
        cload(B["bias_row"], "c_bias_row", cast=True)
        rc.on("v", lambda v, i: v.memset(B["eps"], 1e-5), incs=[(B["sV"], 1)])
        rc.on("v", lambda v, i: v.memset(B["tiny"], 1e-20), incs=[(B["sV"], 1)])

        need = set(phases)
        alloc_phase_bufs(rc, B, need)
        jobs = []
        if "l1ffn" in need:
            for e in range(NEXP):
                jobs += [(dr["L1_ffn_wg_r"][e], scr["wp_l1rg"][e], D, MIDR),
                         (dr["L1_ffn_wu_r"][e], scr["wp_l1ru"][e], D, MIDR),
                         (dr["L1_ffn_wd_r"][e], scr["wp_l1rd"][e], MIDR, D)]
            for e in range(2):
                jobs += [(dr["L1_ffn_wg_s"][e], scr["wp_l1sg"][e], D, MIDS),
                         (dr["L1_ffn_wu_s"][e], scr["wp_l1su"][e], D, MIDS),
                         (dr["L1_ffn_wd_s"][e], scr["wp_l1sd"][e], MIDS, D)]
            jobs.append((dr["L1_ffn_router_w"], scr["wp_router"], D, NEXP))
        emit_prep_cast(rc, B, jobs)
        if "l0attn" in need:
            rc.on("v", lambda v, i: v.memset(B["gldiag"], 0.0), incs=[(B["sV"], 1)])
            rc.on("v", lambda v, i: v.memset(B["grdiag"], 0.0), incs=[(B["sV"], 1)])
            for (dst, key) in [(B["gldiag"], "L0_attn_gl_w"), (B["grdiag"], "L0_attn_gr_w")]:
                rc.on("g", lambda g, i, dst=dst, key=key: g.dma_start(
                    out=dst[0:64, 0:1], in_=dr[key]),
                    waits=[B["sV"].now()], incs=[(sGD, 16)])
                rc.on("g", lambda g, i, dst=dst, key=key: g.dma_start(
                    out=dst[64:128, 1:2], in_=dr[key]), incs=[(sGD, 16)])
        if "l0ffn" in need:
            emit_prep_quant_weights(rc, B)

        rc.on("y", None, waits=[B[s].now() for s in
              ("dCN", "dCS0", "dCS1", "dPS0", "dPS1", "dXL0", "dXL1")])
        rc.on("g", None, waits=[B["dCNG"].now()])
        phase_join(rc, B)
        rc.begin_body()
        for ph in phases:
            skey, dkey = PHASE_IO[ph]
            if dev_io:
                skey, dkey = "x", "y"
            xsrc_fn = (lambda i, t=dr[skey]: t[i])
            xdst_fn = (lambda i, t=dr[dkey]: t[i])
            if ph == "l1attn":
                emit_phase_l1attn(rc, B, xsrc_fn, xdst_fn)
            elif ph == "l1ffn":
                emit_phase_l1ffn(rc, B, xsrc_fn, xdst_fn)
            elif ph == "l0attn":
                emit_phase_l0attn(rc, B, xsrc_fn, xdst_fn)
            elif ph == "l0ffn":
                emit_phase_l0ffn(rc, B, xsrc_fn, xdst_fn)
            phase_join(rc, B)
        rc.end_body()
        for e in ENGS:
            rc.on(e, None, waits=[B["sJ"].now()])
        rc.on("y", None, waits=[B["dST0"].now(), B["dST1"].now(), B["dCN"].now()])
        block = rc.ctx.enter_context(nc.Block())
        rc.build(block)
    return nc


_DMA_SEMS_Y = ("dXL0", "dXL1", "dX20", "dX21", "dST0", "dST1", "dSL0", "dSL1",
               "dCS0", "dCS1", "dPS0", "dPS1", "dCN", "dGS", "dGS1", "dUS", "dUS1", "dDS",
               "dES0", "dES1", "dUS20", "dUS21", "dDS20", "dDS21",
               "dSG", "dSU", "dSDW", "dRT")


def phase_join(rc, B):
    sJ = B["sJ"]
    own = {"t": "sT", "v": "sV", "s": "sS", "g": "sG"}
    rc.on("y", None, waits=[B[s].now() for s in _DMA_SEMS_Y], incs=[(sJ, 1)])
    rc.on("g", None, waits=[B["dCNG"].now(), B["sG"].now()], incs=[(sJ, 1)])
    for e in ("t", "v", "s"):
        rc.on(e, None, waits=[B[own[e]].now()], incs=[(sJ, 1)])
    target = sJ.now()
    for e in ENGS:
        rc.on(e, None, waits=[target])


def alloc_phase_bufs(rc, B, need):
    arena = B["arena"]
    arena.phase_reset()
    B["xnf"] = arena.view([8, 2048], BF16)
    B["xnt"] = arena.view([2, 1024], BF16)
    mark_bf, mark_f = arena.cur_bf, arena.cur_f32
    peak = (mark_bf, mark_f)

    def reset():
        arena.cur_bf, arena.cur_f32 = mark_bf, mark_f

    def done(nm):
        nonlocal peak
        peak = (max(peak[0], arena.cur_bf), max(peak[1], arena.cur_f32))

    if "l1attn" in need or "l0attn" in need:
        reset()
        B["slab"] = arena.view([2, 8, 1024], BF16)
        B["lrow"] = arena.view([512], F32, parts=1)
        B["linv"] = arena.view([512], F32, parts=1)
        B["bcs"] = arena.view([512], F32, parts=64)
        B["sqscr"] = arena.view([2048], BF16)
        B["qgb"] = arena.view([8, 512], BF16)
        B["ptile"] = arena.view([2, 512], BF16)
        B["kbuf"] = arena.view([8, 2048], BF16)
        kmark = (arena.cur_bf, arena.cur_f32)
        B["vt"] = arena.view([16, 16, 65], BF16)
        B["ctxb"] = arena.view([8, 512], BF16)
        done("l1attn")
        if "l0attn" in need:
            arena.cur_bf, arena.cur_f32 = kmark
            B["vt6"] = arena.view([5, 16, 65], BF16)
            B["routslab"] = arena.view([2, 1024], BF16)
            B["mask_lat"] = arena.view([2048], BF16, parts=64)
            B["rqb"] = arena.view([2, 2048], BF16)
            B["localb"] = arena.view([8, 512], BF16)
            B["remoteb"] = arena.view([8, 512], BF16)
            B["pooledb"] = arena.view([8, 64], BF16)
            B["rkf"] = arena.view([2, 64], BF16)
            B["rvt"] = arena.view([4, 65], BF16, parts=64)
            B["lpt"] = arena.view([512], BF16, parts=64)
            B["gsig"] = arena.view([512], F32, parts=2)
            B["scr1"] = arena.view([512], BF16)
            B["latc"] = arena.view([2, 512], BF16)
            done("l0attn")
    if "l0ffn" in need:
        reset()
        for qn in ("q", "q2"):
            B[qn] = {
                "mx": arena.view([64], F32), "scl": arena.view([64], F32),
                "rcp": arena.view([64], F32), "abs": arena.view([1024], F32),
                "sgn": arena.view([1024], BF16), "axn": arena.view([1024], F32),
                "w": arena.view([1024], BF16), "tA": arena.view([1024], BF16),
                "tB": arena.view([1024], BF16),
            }
        B["xnq"] = arena.view([2, 1024], F32)
        B["xqt"] = arena.view([2, 1024], BF16)
        B["hq_t"] = arena.view([3, 512], BF16)
        fmark = (arena.cur_bf, arena.cur_f32)
        B["hqf"] = arena.view([22, 512], BF16)
        B["gslab"] = arena.view([2, 8, 512], BF16)
        B["uslab"] = arena.view([2, 8, 512], BF16)
        B["dslab"] = arena.view([22, 512], BF16)
        B["hstage"] = arena.view([2, 512], F32)
        done("l0ffn")
        arena.cur_bf, arena.cur_f32 = fmark
        B["pstage"] = arena.view([2, 2816], BF16)
        B["wdq"] = arena.view([2, 2816], BF16)
        B["castbuf"] = arena.view([2, 3072], BF16)
        done("prep")
    if "l1ffn" in need:
        reset()
        B["sel16"] = arena.view([16 * 128], BF16, parts=16)
        B["hs"] = arena.view([8, 512], BF16)
        B["hr"] = arena.view([32, 512], BF16)
        B["combT"] = arena.view([2048], BF16, parts=16)
        B["lstage"] = arena.view([512], F32, parts=16)
        B["tklog"] = arena.view([2, 16], F32)
        B["tkexp"] = arena.view([2, 16], F32)
        B["tkmx"] = arena.view([8], F32)
        B["tke1"] = arena.view([16], F32)
        B["tkpm"] = arena.view([16], F32)
        B["tkt"] = arena.view([16], F32)
        B["comb"] = arena.view([2, 16], F32)
        B["sil"] = arena.view([512], F32)
        B["rslab"] = arena.view([8, 16], BF16)
        B["eslab"] = arena.view([2, 8, 256], BF16)
        B["uslab2"] = arena.view([2, 8, 256], BF16)
        B["dslab2"] = arena.view([2, 2, 1024], BF16)
        B["sgslab"] = arena.view([2, 8, 512], BF16)
        B["suslab"] = arena.view([2, 8, 512], BF16)
        B["sdslab"] = arena.view([2, 4, 1024], BF16)
        done("l1ffn")
    if "castbuf" not in B:
        reset()
        B["castbuf"] = arena.view([2, 3072], BF16)


def emit_prep_quant_weights(rc, B):
    sT, sS, sV, sG = B["sT"], B["sS"], B["sV"], B["sG"]
    dXL, dPS = B["dXL"], B["dPS"]
    banks = B["banks"]
    dr, scr = B["dr"], B["scr"]
    wld = B["castbuf"]
    jobs = [(dr["L0_ffn_w_gate"], scr["wp_l0g"], IFF, D),
            (dr["L0_ffn_w_up"], scr["wp_l0u"], IFF, D)]
    for (src, dst, R, C) in jobs:
        for ri in range(R // 128):
            slot = ri % 2
            rc.on("y", lambda y, i, src=src, ri=ri, slot=slot: y.dma_start(
                out=B["xld"][:, slot, :], in_=src[ri * 128:(ri + 1) * 128, :]),
                waits=B.pop(("pqfree", slot), []), incs=[(dXL[slot], 16)])
            emit_quant_tile(rc, B, B["xld"][:, slot, :], B["xqt"][:, slot, :], 1024,
                            src_wait=[dXL[slot].now()], dst_wait=[sT.now()])
            pb = banks.bf[slot]
            for c in range(8):
                rc.on("t", lambda t_, i, c=c, slot=slot, pb=pb: t_.transpose(
                    out=pb[:, c * 128:(c + 1) * 128],
                    in_=B["xqt"][:, slot, c * 128:(c + 1) * 128],
                    identity=B["ident_bf"]),
                    waits=([sV.now(), sG.now(), B["dCN"].now()] + banks.bwait(pb)
                           if c == 0 else []),
                    incs=[(sT, 1)])
            rc.on("s", lambda s, i, slot=slot, pb=pb: s.copy(
                out=B["pstage"][:, slot, 0:1024], in_=pb[:]),
                waits=[sT.now(), dPS[slot].now()], incs=[(sS, 1)])
            banks.bfree(pb, [sS.now()])
            for c in range(8):
                rc.on("y", lambda y, i, dst=dst, ri=ri, c=c, slot=slot: y.dma_start(
                    out=dst[c * 128:(c + 1) * 128, ri * 128:(ri + 1) * 128],
                    in_=B["pstage"][:, slot, c * 128:(c + 1) * 128]),
                    waits=([sS.now()] if c == 0 else []), incs=[(dPS[slot], 16)])
            B[("pqfree", slot)] = [sV.now(), sS.now(), sT.now()]
    src, dst = dr["L0_ffn_w_down"], scr["wp_l0d"]
    for ri in range(8):
        slot = ri % 2
        for p0, pw in ((0, 1024), (1024, 1024), (2048, 768)):
            rc.on("y", lambda y, i, src=src, ri=ri, slot=slot, p0=p0, pw=pw:
                  y.dma_start(out=B["xld"][:, slot, 0:pw],
                              in_=src[ri * 128:(ri + 1) * 128, p0:p0 + pw]),
                  waits=B.pop(("pqdfree", slot), []), incs=[(dXL[slot], 16)])
            emit_quant_tile(rc, B, B["xld"][:, slot, 0:pw],
                            B["wdq"][:, slot, p0:p0 + pw], pw,
                            src_wait=[dXL[slot].now()], dst_wait=[sT.now()])
            B[("pqdfree", slot)] = [B["sV"].now(), sS.now()]
        for cg in range(6):
            w = 512 if cg < 5 else 256
            pb = banks.bf[cg % 2]
            for c in range(w // 128):
                rc.on("t", lambda t_, i, cg=cg, c=c, slot=slot, pb=pb: t_.transpose(
                    out=pb[:, c * 128:(c + 1) * 128],
                    in_=B["wdq"][:, slot, cg * 512 + c * 128:cg * 512 + (c + 1) * 128],
                    identity=B["ident_bf"]),
                    waits=([sV.now(), sG.now(), B["dCN"].now()] + banks.bwait(pb)
                           if c == 0 else []),
                    incs=[(sT, 1)])
            rc.on("s", lambda s, i, cg=cg, slot=slot, pb=pb, w=w: s.copy(
                out=B["pstage"][:, slot, cg * 512:cg * 512 + w], in_=pb[:, 0:w]),
                waits=[sT.now(), dPS[slot].now()], incs=[(sS, 1)])
            banks.bfree(pb, [sS.now()])
        for c in range(22):
            rc.on("y", lambda y, i, dst=dst, ri=ri, c=c, slot=slot: y.dma_start(
                out=dst[c * 128:(c + 1) * 128, ri * 128:(ri + 1) * 128],
                in_=B["pstage"][:, slot, c * 128:(c + 1) * 128]),
                waits=([sS.now()] if c == 0 else []), incs=[(dPS[slot], 16)])
        B[("pqdfree", slot)] = [sV.now(), sS.now(), sT.now()]


def emit_prep_cast(rc, B, jobs):
    stage = B["castbuf"]
    for (src, dst, R, C) in jobs:
        for ri, r0 in enumerate(range(0, R, 128)):
            slot = ri % 2
            rc.on("g", lambda g, i, src=src, r0=r0, C=C, slot=slot:
                  g.dma_start(out=stage[:, slot, 0:C], in_=src[r0:r0 + 128, :]),
                  waits=[B["dCS"][slot].now()], incs=[(B["dCB"][slot], 16)])
            rc.on("y", lambda y, i, dst=dst, r0=r0, C=C, slot=slot:
                  y.dma_start(out=dst[r0:r0 + 128, :], in_=stage[:, slot, 0:C]),
                  waits=[B["dCB"][slot].now()], incs=[(B["dCS"][slot], 16)])


def emit_quant_tile(rc, B, src_ap, dst_ap, F, src_wait=(), dst_wait=(), qsel=0):
    nb = F // 16
    q = B["q"] if qsel == 0 else B["q2"]
    sV, sS, sG = B["sV"], B["sS"], B["sG"]
    src3 = src_ap.rearrange("p (a b) -> p a b", b=16)

    rc.on("v", lambda v, i, a=src3, nb=nb: v.tensor_reduce(
        out=q["mx"][:, 0:nb], in_=a, axis=AX.X, op=ALU.max,
        apply_absolute_value=True), waits=list(src_wait), incs=[(sV, 1)])
    rc.on("v", lambda v, i, nb=nb: v.tensor_scalar(
        out=q["scl"][:, 0:nb], in0=q["mx"][:, 0:nb], scalar1=1.0 / 6.0,
        scalar2=1e-12, op0=ALU.mult, op1=ALU.max),
        waits=[sV.now()], incs=[(sV, 1)])
    rc.on("v", lambda v, i, nb=nb: v.reciprocal(
        out=q["rcp"][:, 0:nb], in_=q["scl"][:, 0:nb]),
        waits=[sV.now()], incs=[(sV, 1)])
    rc.on("s", lambda s, i, a=src_ap, F=F: s.activation(
        out=q["abs"][:, 0:F], in_=a, func=AF.Abs),
        waits=list(src_wait) + [sG.now(), sV.now()], incs=[(sS, 1)])
    rc.on("s", lambda s, i, a=src_ap, F=F: s.activation(
        out=q["sgn"][:, 0:F], in_=a, func=AF.Sign), incs=[(sS, 1)])
    rc.on("g", lambda g, i, F=F, nb=nb: g.tensor_tensor(
        out=q["axn"][:, 0:F].rearrange("p (a b) -> p a b", b=16),
        in0=q["abs"][:, 0:F].rearrange("p (a b) -> p a b", b=16),
        in1=q["rcp"][:, 0:nb].unsqueeze(2).broadcast_to([128, nb, 16]),
        op=ALU.mult), waits=[sV.now(), sS.now()], incs=[(sG, 1)])
    rc.on("g", lambda g, i, F=F, nb=nb: g.tensor_tensor(
        out=q["w"][:, 0:F].rearrange("p (a b) -> p a b", b=16),
        in0=q["sgn"][:, 0:F].rearrange("p (a b) -> p a b", b=16),
        in1=q["scl"][:, 0:nb].unsqueeze(2).broadcast_to([128, nb, 16]),
        op=ALU.mult), incs=[(sG, 1)])
    A, Bv = q["tA"], q["tB"]
    rc.on("v", lambda v, i, F=F: v.tensor_scalar(
        out=Bv[:, 0:F], in0=q["axn"][:, 0:F], scalar1=5.0, scalar2=2.0,
        op0=ALU.is_gt, op1=ALU.mult), waits=[sG.now()], incs=[(sV, 1)])
    rc.on("v", lambda v, i, F=F: v.tensor_single_scalar(
        out=A[:, 0:F], in_=q["axn"][:, 0:F], scalar=0.25, op=ALU.is_gt),
        incs=[(sV, 1)])
    for thr, tgt, oth in ((2.5, "B", "A"), (0.75, "A", "B"), (3.5, "B", "A"),
                          (1.25, "A", "B"), (1.75, "A", "B")):
        dst = Bv if tgt == "B" else A
        rc.on("v", lambda v, i, F=F, thr=thr, dst=dst: v.scalar_tensor_tensor(
            out=dst[:, 0:F], in0=q["axn"][:, 0:F], scalar=thr, in1=dst[:, 0:F],
            op0=ALU.is_gt, op1=ALU.add), waits=[sV.now()], incs=[(sV, 1)])
    rc.on("v", lambda v, i, F=F: v.scalar_tensor_tensor(
        out=A[:, 0:F], in0=A[:, 0:F], scalar=0.5, in1=Bv[:, 0:F],
        op0=ALU.mult, op1=ALU.add), waits=[sV.now()], incs=[(sV, 1)])
    rc.on("g", lambda g, i, F=F, dst=dst_ap: g.tensor_tensor(
        out=dst, in0=A[:, 0:F], in1=q["w"][:, 0:F], op=ALU.mult),
        waits=[sV.now()] + list(dst_wait), incs=[(sG, 1)])


def emit_ln_tile(rc, B, xsrc_fn, t, slot, out_ap, out_f32=False):
    sV, sS = B["sV"], B["sS"]
    dXL = B["dXL"][slot]
    xld = B["xld"]
    rc.on("y", lambda y, i, t=t, slot=slot: y.dma_start(
        out=xld[:, slot, :], in_=xsrc_fn(i)[t * 128:(t + 1) * 128, :]),
        waits=B.pop(("xldfree", slot), []), incs=[(dXL, 16)])
    sl = slice(slot, slot + 1)
    rc.on("v", lambda v, i, slot=slot, sl=sl: v.tensor_reduce(
        out=B["lnms"][:, sl], in_=xld[:, slot, :], axis=AX.X, op=ALU.add),
        waits=[dXL.now()], incs=[(sV, 1)])
    rc.on("s", lambda s, i, slot=slot, sl=sl: s.activation(
        out=B["dump"][:], in_=xld[:, slot, :], func=AF.Square,
        accum_out=B["lnvs"][:, sl]), waits=[dXL.now(), sS.now()], incs=[(sS, 1)])
    rc.on("v", lambda v, i, sl=sl: v.tensor_scalar(
        out=B["lnm"][:, sl], in0=B["lnms"][:, sl], scalar1=1.0 / D, scalar2=None,
        op0=ALU.mult), waits=[sV.now()], incs=[(sV, 1)])
    rc.on("s", lambda s, i, sl=sl: s.activation(
        out=B["lnmsq"][:, sl], in_=B["lnm"][:, sl], func=AF.Square),
        waits=[sV.now()], incs=[(sS, 1)])
    rc.on("v", lambda v, i, sl=sl: v.tensor_scalar(
        out=B["lnvar"][:, sl], in0=B["lnvs"][:, sl], scalar1=1.0 / D,
        scalar2=B["lnmsq"][:, sl], op0=ALU.mult, op1=ALU.subtract),
        waits=[sS.now()], incs=[(sV, 1)])
    rc.on("s", lambda s, i, sl=sl: s.activation(
        out=B["lnt"][:, sl], in_=B["lnvar"][:, sl], func=AF.Ln, bias=B["eps"][:]),
        waits=[sV.now()], incs=[(sS, 1)])
    rc.on("s", lambda s, i, sl=sl: s.activation(
        out=B["lnr"][:, sl], in_=B["lnt"][:, sl], func=AF.Exp, scale=-0.5),
        waits=[sS.now()], incs=[(sS, 1)])
    rc.on("v", lambda v, i, slot=slot, sl=sl, out_ap=out_ap: v.tensor_scalar(
        out=out_ap, in0=xld[:, slot, :], scalar1=B["lnm"][:, sl],
        scalar2=B["lnr"][:, sl], op0=ALU.subtract, op1=ALU.mult),
        waits=[sS.now()], incs=[(sV, 1)])
    B[("xldfree", slot)] = [sV.now(), sS.now()]


def emit_tile_to_feature(rc, B, src_ap, dst3_ap, tbank_idx):
    sV, sT = B["sV"], B["sT"]
    banks = B["banks"]
    pb = banks.bf[tbank_idx]
    for c in range(8):
        rc.on("t", lambda t, i, c=c, src=src_ap, pb=pb: t.transpose(
            out=pb[:, c * 128:(c + 1) * 128], in_=src[:, c * 128:(c + 1) * 128],
            identity=B["ident_bf"][:]),
            waits=([sV.now(), B["sG"].now()] + banks.bwait(pb) if c == 0 else []),
            incs=([(sT, 1)] if c == 7 else []))
    rc.on("s", lambda s, i, pb=pb, dst=dst3_ap: s.copy(
        out=dst, in_=pb[:].rearrange("p (c w) -> p c w", w=128)),
        waits=[sT.now()], incs=[(B["sS"], 1)])
    banks.bfree(pb, [B["sS"].now()])


def emit_phase_l1ffn(rc, B, xsrc_fn, xdst_fn):
    sV, sS, sT, sG = B["sV"], B["sS"], B["sT"], B["sG"]
    banks = B["banks"]
    xnf = B["xnf"]
    scr = B["scr"]

    rc.on("y", lambda y, i: y.dma_start(
        out=B["sel16"], in_=B["dr"]["c_sel16"]),
        waits=[sT.now()], incs=[(B["dCN"], 16)])
    rc.on("y", lambda y, i: y.dma_start(
        out=B["rslab"][:], in_=scr["wp_router"].rearrange("(c p) m -> p c m", p=128)),
        incs=[(B["dRT"], 16)])
    for e in range(2):
        rc.on("y", lambda y, i, e=e: y.dma_start(
            out=B["sgslab"][:, e, :, :],
            in_=scr["wp_l1sg"][e].rearrange("(c p) m -> p c m", p=128)),
            waits=([sT.now()] if e == 0 else []), incs=[(B["dSG"], 16)])
        rc.on("y", lambda y, i, e=e: y.dma_start(
            out=B["suslab"][:, e, :, :],
            in_=scr["wp_l1su"][e].rearrange("(c p) m -> p c m", p=128)),
            waits=([sT.now()] if e == 0 else []), incs=[(B["dSU"], 16)])
        rc.on("y", lambda y, i, e=e: y.dma_start(
            out=B["sdslab"][:, e, :, :],
            in_=scr["wp_l1sd"][e].rearrange("(c p) m -> p c m", p=128)),
            waits=([sT.now()] if e == 0 else []), incs=[(B["dSDW"], 16)])

    lnrdy = []
    for t in range(16):
        emit_ln_tile(rc, B, xsrc_fn, t, t % 2, B["xnt"][:, t % 2, :])
        emit_tile_to_feature(rc, B, B["xnt"][:, t % 2, :],
                             xnf[:, 0:8, t * 128:(t + 1) * 128, ], t % 2)
        B[("xldfree", t % 2)] = B[("xldfree", t % 2)] + [sS.now()]
        lnrdy.append(sS.now())
    ln_done = sS.now()

    for tg in range(4):
        bank = banks.f32[5]
        rc.on("t", lambda t_, i, tg=tg, bank=bank: t_.matmul(
            out=bank[0:16, :], lhsT=B["rslab"][:, 0, :],
            rhs=xnf[:, 0, tg * 512:(tg + 1) * 512], start=True, stop=False),
            waits=[lnrdy[tg * 4 + 3], B["dRT"].now()] + banks.bwait(bank), incs=[])
        for kc in range(1, 8):
            rc.on("t", lambda t_, i, tg=tg, kc=kc, bank=bank: t_.matmul(
                out=bank[0:16, :], lhsT=B["rslab"][:, kc, :],
                rhs=xnf[:, kc, tg * 512:(tg + 1) * 512],
                start=False, stop=(kc == 7)), incs=([(sT, 1)] if kc == 7 else []))
        rc.on("v", lambda v, i, bank=bank: v.tensor_copy(
            out=B["lstage"][0:16, :], in_=bank[0:16, :]),
            waits=[sT.now()], incs=[(sV, 1)])
        banks.bfree(bank, [sV.now()])
        for ts_ in range(4):
            tile = tg * 4 + ts_
            slot = tile % 2
            tb = banks.f32[4]
            rc.on("t", lambda t_, i, ts_=ts_, tb=tb: t_.transpose(
                out=tb[0:128, 0:16], in_=B["lstage"][0:16, ts_ * 128:(ts_ + 1) * 128],
                identity=B["ident_f32"][0:16, 0:16]),
                waits=[sV.now()] + banks.bwait(tb), incs=[(sT, 1)])
            rc.on("s", lambda s, i, slot=slot, tb=tb: s.copy(
                out=B["tklog"][:, slot, :], in_=tb[0:128, 0:16]),
                waits=[sT.now(), sG.now(), sV.now()], incs=[(sS, 1)])
            banks.bfree(tb, [sS.now()])
            tl = B["tklog"][:, slot, :]
            rc.on("v", lambda v, i, tl=tl: v.tensor_reduce(
                out=B["tkmx"][:, 0:1], in_=tl, axis=AX.X, op=ALU.max, negate=True),
                waits=[sS.now(), sG.now()], incs=[(sV, 1)])
            rc.on("s", lambda s, i, tl=tl, slot=slot: s.activation(
                out=B["tkexp"][:, slot, :], in_=tl, func=AF.Exp,
                bias=B["tkmx"][:, 0:1]), waits=[sV.now()], incs=[(sS, 1)])
            rc.on("v", lambda v, i, slot=slot: v.tensor_reduce(
                out=B["tkmx"][:, 1:2], in_=B["tkexp"][:, slot, :], axis=AX.X,
                op=ALU.max), waits=[sS.now()], incs=[(sV, 1)])
            rc.on("v", lambda v, i, slot=slot: v.tensor_single_scalar(
                out=B["tke1"][:], in_=B["tkexp"][:, slot, :],
                scalar=B["tkmx"][:, 1:2], op=ALU.is_ge),
                waits=[sV.now()], incs=[(sV, 1)])
            rc.on("v", lambda v, i: v.tensor_scalar(
                out=B["tkt"][:], in0=B["tke1"][:], scalar1=-1.0, scalar2=1.0,
                op0=ALU.mult, op1=ALU.add), waits=[sV.now()], incs=[(sV, 1)])
            rc.on("g", lambda g, i, slot=slot: g.tensor_tensor(
                out=B["tkpm"][:], in0=B["tkt"][:], in1=B["tkexp"][:, slot, :],
                op=ALU.mult), waits=[sV.now()], incs=[(sG, 1)])
            rc.on("v", lambda v, i: v.tensor_reduce(
                out=B["tkmx"][:, 2:3], in_=B["tkpm"][:], axis=AX.X, op=ALU.max),
                waits=[sG.now()], incs=[(sV, 1)])
            rc.on("g", lambda g, i: g.tensor_tensor(
                out=B["tkmx"][:, 3:4], in0=B["tkmx"][:, 1:2], in1=B["tkmx"][:, 2:3],
                op=ALU.add), waits=[sV.now()], incs=[(sG, 1)])
            rc.on("v", lambda v, i: v.tensor_scalar(
                out=B["tkmx"][:, 4:5], in0=B["tkmx"][:, 3:4], scalar1=1e-9,
                scalar2=None, op0=ALU.add), waits=[sG.now()], incs=[(sV, 1)])
            rc.on("v", lambda v, i: v.reciprocal(
                out=B["tkmx"][:, 5:6], in_=B["tkmx"][:, 4:5]),
                waits=[sV.now()], incs=[(sV, 1)])
            rc.on("g", lambda g, i: g.tensor_tensor(
                out=B["tkmx"][:, 6:7], in0=B["tkmx"][:, 1:2], in1=B["tkmx"][:, 5:6],
                op=ALU.mult), waits=[sV.now()], incs=[(sG, 1)])
            rc.on("g", lambda g, i: g.tensor_tensor(
                out=B["tkmx"][:, 7:8], in0=B["tkmx"][:, 2:3], in1=B["tkmx"][:, 5:6],
                op=ALU.mult), incs=[(sG, 1)])
            rc.on("v", lambda v, i: v.tensor_single_scalar(
                out=B["tkt"][:], in_=B["tkpm"][:], scalar=B["tkmx"][:, 2:3],
                op=ALU.is_ge), waits=[sG.now()], incs=[(sV, 1)])
            rc.on("v", lambda v, i: v.tensor_single_scalar(
                out=B["tke1"][:], in_=B["tke1"][:], scalar=B["tkmx"][:, 6:7],
                op=ALU.mult), waits=[sV.now(), sG.now()], incs=[(sV, 1)])
            rc.on("v", lambda v, i, slot=slot: v.scalar_tensor_tensor(
                out=B["comb"][:, slot, :], in0=B["tkt"][:], scalar=B["tkmx"][:, 7:8],
                in1=B["tke1"][:], op0=ALU.mult, op1=ALU.add),
                waits=[sV.now(), sG.now()], incs=[(sV, 1)])
            tb2 = banks.f32[4]
            rc.on("t", lambda t_, i, slot=slot, tb2=tb2: t_.transpose(
                out=tb2[0:16, 0:128], in_=B["comb"][:, slot, :],
                identity=B["ident_f32"][:]),
                waits=[sV.now()] + banks.bwait(tb2), incs=[(sT, 1)])
            rc.on("s", lambda s, i, tile=tile, tb2=tb2: s.copy(
                out=B["combT"][0:16, tile * 128:(tile + 1) * 128], in_=tb2[0:16, 0:128]),
                waits=[sT.now(), sS.now()], incs=[(sS, 1)])
            banks.bfree(tb2, [sS.now()])
    comb_done = sS.now()

    for tg in range(4):
        for e in range(2):
            for mc in range(4):
                gb = banks.f32[0]
                ub = banks.f32[1]
                for kc in range(8):
                    rc.on("t", lambda t_, i, e=e, mc=mc, kc=kc, tg=tg, gb=gb: t_.matmul(
                        out=gb[:], lhsT=B["sgslab"][:, e, kc, mc * 128:(mc + 1) * 128],
                        rhs=xnf[:, kc, tg * 512:(tg + 1) * 512],
                        start=(kc == 0), stop=(kc == 7)),
                        waits=([lnrdy[tg * 4 + 3], B["dSG"].now()] + banks.bwait(gb)
                               if kc == 0 else []),
                        incs=([(sT, 1)] if kc == 7 else []))
                for kc in range(8):
                    rc.on("t", lambda t_, i, e=e, mc=mc, kc=kc, tg=tg, ub=ub: t_.matmul(
                        out=ub[:], lhsT=B["suslab"][:, e, kc, mc * 128:(mc + 1) * 128],
                        rhs=xnf[:, kc, tg * 512:(tg + 1) * 512],
                        start=(kc == 0), stop=(kc == 7)),
                        waits=([B["dSU"].now()] + banks.bwait(ub) if kc == 0 else []),
                        incs=([(sT, 1)] if kc == 7 else []))
                rc.on("s", lambda s, i, gb=gb: s.activation(
                    out=B["sil"][:], in_=gb[:], func=AF.Silu),
                    waits=[sT.now(), sV.now()], incs=[(sS, 1)])
                banks.bfree(gb, [sS.now()])
                rc.on("v", lambda v, i, e=e, mc=mc, ub=ub: v.tensor_tensor(
                    out=B["hs"][:, e * 4 + mc, :], in0=B["sil"][:], in1=ub[:],
                    op=ALU.mult), waits=[sS.now()], incs=[(sV, 1)])
                banks.bfree(ub, [sV.now()])
        for e in range(NEXP):
            slot = e % 2
            rc.on("y", lambda y, i, e=e, slot=slot: y.dma_start(
                out=B["eslab"][:, slot, :, :],
                in_=scr["wp_l1rg"][e].rearrange("(c p) m -> p c m", p=128)),
                waits=B.pop(("esfree", slot), []), incs=[(B["dES0" if slot == 0 else "dES1"], 16)])
            rc.on("y", lambda y, i, e=e, slot=slot: y.dma_start(
                out=B["uslab2"][:, slot, :, :],
                in_=scr["wp_l1ru"][e].rearrange("(c p) m -> p c m", p=128)),
                waits=B.pop(("usfree", slot), []), incs=[(B["dUS20" if slot == 0 else "dUS21"], 16)])
            bc = banks.f32[2]
            rc.on("t", lambda t_, i, e=e, tg=tg, bc=bc: t_.matmul(
                out=bc[:], lhsT=B["sel16"][:, e * 128:(e + 1) * 128],
                rhs=B["combT"][0:16, tg * 512:(tg + 1) * 512], start=True, stop=True),
                waits=[comb_done, B["dCN"].now()] + banks.bwait(bc), incs=[(sT, 1)])
            bc_ready = sT.now()
            for mc in range(2):
                gb = banks.f32[3]
                ub = banks.f32[0]
                for kc in range(8):
                    rc.on("t", lambda t_, i, e=e, mc=mc, kc=kc, tg=tg, slot=slot,
                          gb=gb: t_.matmul(
                        out=gb[:], lhsT=B["eslab"][:, slot, kc, mc * 128:(mc + 1) * 128],
                        rhs=xnf[:, kc, tg * 512:(tg + 1) * 512],
                        start=(kc == 0), stop=(kc == 7)),
                        waits=([B["dES0" if slot == 0 else "dES1"].now()]
                               + banks.bwait(gb) if kc == 0 else []),
                        incs=([(sT, 1)] if kc == 7 else []))
                for kc in range(8):
                    rc.on("t", lambda t_, i, e=e, mc=mc, kc=kc, tg=tg, slot=slot,
                          ub=ub: t_.matmul(
                        out=ub[:], lhsT=B["uslab2"][:, slot, kc, mc * 128:(mc + 1) * 128],
                        rhs=xnf[:, kc, tg * 512:(tg + 1) * 512],
                        start=(kc == 0), stop=(kc == 7)),
                        waits=([B["dUS20" if slot == 0 else "dUS21"].now()]
                               + banks.bwait(ub) if kc == 0 else []),
                        incs=([(sT, 1)] if kc == 7 else []))
                rc.on("s", lambda s, i, gb=gb: s.activation(
                    out=B["sil"][:], in_=gb[:], func=AF.Silu),
                    waits=[sT.now(), sV.now()], incs=[(sS, 1)])
                banks.bfree(gb, [sS.now()])
                rc.on("v", lambda v, i, ub=ub: v.tensor_tensor(
                    out=B["sil"][:], in0=B["sil"][:], in1=ub[:], op=ALU.mult),
                    waits=[sS.now()], incs=[(sV, 1)])
                banks.bfree(ub, [sV.now()])
                rc.on("v", lambda v, i, e=e, mc=mc, bc=bc: v.tensor_tensor(
                    out=B["hr"][:, e * 2 + mc, :], in0=B["sil"][:], in1=bc[:],
                    op=ALU.mult), waits=[sV.now()], incs=[(sV, 1)])
                if mc == 1:
                    banks.bfree(bc, [sV.now()])
            B[("esfree", slot)] = [sT.now()]
            B[("usfree", slot)] = [sT.now()]
        h_done = sV.now()

        dwn = ([(banks.f32[j], banks.f32[j][:]) for j in range(6)]
               + [(bb, bb[:].bitcast(F32)) for bb in banks.bf])
        for bko, _ in dwn:
            for w in banks.bwait(bko):
                rc.on("t", None, waits=[w])
        nsrcmc = 2 * 4 + NEXP * 2
        si = 0
        for src_i in range(2 + NEXP):
            routed = src_i >= 2
            e = src_i - 2 if routed else src_i
            if routed:
                slot = e % 2
                rc.on("y", lambda y, i, e=e, slot=slot: y.dma_start(
                    out=B["dslab2"][:, slot, :, :],
                    in_=scr["wp_l1rd"][e].rearrange("(c p) m -> p c m", p=128)),
                    waits=B.pop(("dsfree", slot), []),
                    incs=[(B["dDS20" if slot == 0 else "dDS21"], 16)])
            nmc = 2 if routed else 4
            for mc in range(nmc):
                hsrc = B["hr"][:, e * 2 + mc, :] if routed else B["hs"][:, e * 4 + mc, :]
                for tt in range(4):
                    for og in range(2):
                        bk = dwn[tt * 2 + og][1]
                        if routed:
                            rhs_fn = (lambda i, slot=slot, mc=mc, og=og:
                                      B["dslab2"][:, slot, mc, og * 512:(og + 1) * 512])
                            w0 = ([B["dDS20" if slot == 0 else "dDS21"].now(), h_done]
                                  if (tt == 0 and og == 0 and mc == 0) else [])
                        else:
                            rhs_fn = (lambda i, e=e, mc=mc, og=og:
                                      B["sdslab"][:, e, mc, og * 512:(og + 1) * 512])
                            w0 = ([B["dSDW"].now(), h_done]
                                  if (tt == 0 and og == 0 and mc == 0) else [])
                        rc.on("t", lambda t_, i, hsrc=hsrc, rhs_fn=rhs_fn, tt=tt,
                              bk=bk, si=si, nsrcmc=nsrcmc: t_.matmul(
                            out=bk, lhsT=hsrc[:, tt * 128:(tt + 1) * 128],
                            rhs=rhs_fn(i), start=(si == 0), stop=(si == nsrcmc - 1)),
                            waits=w0, incs=([(sT, 1)] if (tt == 3 and og == 1 and
                                                          mc == nmc - 1) else []))
                si += 1
            if routed:
                B[("dsfree", slot)] = [sT.now()]
        down_done = sT.now()
        for tt in range(4):
            ttg = tg * 4 + tt
            xslot = tt % 2
            rc.on("y", lambda y, i, ttg=ttg, xslot=xslot: y.dma_start(
                out=B["xld2"][:, xslot, :], in_=xsrc_fn(i)[ttg * 128:(ttg + 1) * 128, :]),
                waits=B.pop(("xld2free", xslot), []), incs=[(B["dX2"][xslot], 16)])
            for og in range(2):
                bko, bk = dwn[tt * 2 + og]
                rc.on("v", lambda v, i, og=og, xslot=xslot, bk=bk: v.tensor_tensor(
                    out=B["yst"][:, xslot, og * 512:(og + 1) * 512], in0=bk,
                    in1=B["xld2"][:, xslot, og * 512:(og + 1) * 512], op=ALU.add),
                    waits=[down_done, B["dX2"][xslot].now()] + B.pop(("ystfree", xslot), []),
                    incs=[(sV, 1)])
                banks.bfree(bko, [sV.now()])
            rc.on("y", lambda y, i, ttg=ttg, xslot=xslot: y.dma_start(
                out=xdst_fn(i)[ttg * 128:(ttg + 1) * 128, :], in_=B["yst"][:, xslot, :]),
                waits=[sV.now()], incs=[(B["dST"][xslot], 16)])
            B[("xld2free", xslot)] = [sV.now()]
            B[("ystfree", xslot)] = [B["dST"][xslot].now()]


def _emit_local_pv2(rc, B, banks, ctxbank, vt6, qg, h, item, sT):
    kt, c0, c1, pw, ki = item
    for rcol in range(c0, c1, 128):
        tl_r = rcol // 128
        first_kt = max(qg * 4 + tl_r - 1, 0)
        rc.on("t", lambda t_, i, h=h, kt=kt, rcol=rcol, qg=qg,
              st=(kt == first_kt),
              sp=(kt == qg * 4 + tl_r), ctxbank=ctxbank: t_.matmul(
            out=ctxbank[0:65, rcol:rcol + 128], lhsT=vt6[:, kt % 5, h, 0:65],
            rhs=B["ptile"][:, kt % 2, rcol:rcol + 128],
            start=st, stop=sp),
            waits=[pw] + (banks.bwait(ctxbank) if ki == 0 and rcol == c0 else []),
            incs=[(sT, 1)])


def _emit_local_pv(rc, B, banks, pfE, vt6, qg, h, item, sT):
    kt, c0, c1, pw, ki = item
    for rcol in range(c0, c1, 128):
        tl_r = rcol // 128
        first_kt = max(qg * 4 + tl_r - 1, 0)
        rc.on("t", lambda t_, i, h=h, kt=kt, rcol=rcol, qg=qg,
              st=(kt == first_kt),
              sp=(kt == qg * 4 + tl_r): t_.matmul(
            out=pfE[0:65, rcol:rcol + 128], lhsT=vt6[:, kt % 5, h, 0:65],
            rhs=B["ptile"][:, kt % 2, rcol:rcol + 128],
            start=st, stop=sp),
            waits=[pw] + (banks.bwait(pfE) if ki == 0 and rcol == c0 else []),
            incs=[(sT, 1)])


def emit_phase_l0attn(rc, B, xsrc_fn, xdst_fn):
    sV, sS, sT, sG = B["sV"], B["sS"], B["sT"], B["sG"]
    banks = B["banks"]
    pfE, pfF = banks.f32[4], banks.f32[5]
    xnf, kbuf, qgb = B["xnf"], B["kbuf"], B["qgb"]
    vt6, rqb, rkf, rvt = B["vt6"], B["rqb"], B["rkf"], B["rvt"]
    localb, remoteb, latc = B["localb"], B["remoteb"], B["latc"]
    slab = B["slab"]
    scr = B["scr"]
    NRQ, NRK, NK, NQ = 0, 8, 10, 42

    rc.on("v", lambda v, i: v.memset(vt6[:, :, :, 64:65], 1.0), incs=[(sV, 1)])
    rc.on("v", lambda v, i: v.memset(rvt[:, :, 64:65], 1.0),
          waits=[sV.now()], incs=[(sV, 1)])
    rc.on("y", lambda y, i: y.dma_start(out=B["mask_lat"], in_=B["dr"]["c_mask_lat"]),
          waits=[sV.now()], incs=[(B["dCN"], 16)])

    rkvslab = slab[:, 0, :, 0:512]
    rqslab = slab[:, 0, :, 512:768]
    rc.on("g", lambda g, i: g.dma_start(
        out=rqslab, in_=B["dr"]["L0_attn_rq_w"].rearrange("(c p) m -> p c m", p=128)),
        waits=[sT.now()], incs=[(B["dGA"], 16)])
    rc.on("g", lambda g, i: g.dma_start(
        out=rkvslab, in_=B["dr"]["L0_attn_rkv_w"].rearrange("(c p) m -> p c m", p=128)),
        waits=[sT.now()], incs=[(B["dGB"], 16)])
    rc.on("g", lambda g, i: g.dma_start(
        out=B["routslab"][:], in_=B["dr"]["L0_attn_rout_w"]
        .rearrange("(c p) m -> p c m", p=128)),
        waits=[sT.now()], incs=[(B["dCB0"], 16)])
    rc.on("g", lambda g, i: g.dma_start(
        out=slab[:, 1, :, :], in_=B["dr"]["L0_attn_qkv_w"][:, 2048:3072]
        .rearrange("(c p) m -> p c m", p=128)),
        waits=[sT.now()], incs=[(B["dSL1"], 16)])

    for t in range(16):
        emit_ln_tile(rc, B, xsrc_fn, t, t % 2, B["xnt"][:, t % 2, :])
        emit_tile_to_feature(rc, B, B["xnt"][:, t % 2, :],
                             xnf[:, 0:8, t * 128:(t + 1) * 128], t % 2)
        B[("xldfree", t % 2)] = B[("xldfree", t % 2)] + [sS.now()]
    ln_done = sS.now()

    def _pool_reduce(v, i, c):
        with v.bass.allow_low_precision(reason="32-chunk mean pooled to bf16"):
            return v.tensor_reduce(
                out=B["pooledb"][:, c, :], in_=xnf[:, c, :]
                .rearrange("p (a b) -> p a b", b=32), axis=AX.X, op=ALU.add)

    for c in range(8):
        rc.on("v", lambda v, i, c=c: _pool_reduce(v, i, c),
              waits=([ln_done] if c == 0 else []), incs=[(sV, 1)])
        rc.on("v", lambda v, i, c=c: v.tensor_scalar(
            out=B["pooledb"][:, c, :], in0=B["pooledb"][:, c, :],
            scalar1=1.0 / 32.0, scalar2=None, op0=ALU.mult),
            waits=[sV.now()], incs=[(sV, 1)])
    pool_done = sV.now()

    for oc in range(2):
        bank = banks.f32[oc % 2]
        for kc in range(8):
            rc.on("t", lambda t_, i, oc=oc, kc=kc, bank=bank: t_.matmul(
                out=bank[:, 0:64], lhsT=rkvslab[:, kc, oc * 128:(oc + 1) * 128],
                rhs=B["pooledb"][:, kc, :], start=(kc == 0), stop=(kc == 7)),
                waits=([pool_done, B["dGB"].now()] + banks.bwait(bank)
                       if kc == 0 else []),
                incs=([(sT, 1)] if kc == 7 else []))
        rc.on("v", lambda v, i, oc=oc, bank=bank: v.tensor_scalar(
            out=rkf[:, oc, :], in0=bank[:, 0:64],
            scalar1=B["bias_pp"][:, BPP["l0_rk"] + oc:BPP["l0_rk"] + oc + 1],
            scalar2=None, op0=ALU.add), waits=[sT.now()], incs=[(sV, 1)])
        banks.bfree(bank, [sV.now()])
    bank = banks.f32[2]
    rc.on("t", lambda t_, i, bank=bank: t_.matmul(
        out=bank[0:64, 0:256], lhsT=B["ones_row_bf"][0:1, 0:64],
        rhs=B["bias_row"][0:1, BROW["l0_rv"]:BROW["l0_rv"] + 256],
        start=True, stop=False),
        waits=[pool_done, B["dGB"].now()] + banks.bwait(bank), incs=[])
    for kc in range(8):
        rc.on("t", lambda t_, i, kc=kc, bank=bank: t_.matmul(
            out=bank[0:64, 0:256], lhsT=B["pooledb"][:, kc, :],
            rhs=rkvslab[:, kc, 256:512], start=False, stop=(kc == 7)),
            incs=([(sT, 1)] if kc == 7 else []))
    rc.on("s", lambda s, i, bank=bank: s.copy(
        out=rvt[0:64, 0:4, 0:64],
        in_=bank[0:64, 0:256].rearrange("p (h d) -> p h d", d=64)),
        waits=[sT.now()], incs=[(sS, 1)])
    banks.bfree(bank, [sS.now()])

    for oc in range(2):
        for kc in range(8):
            for tg in range(4):
                bank = banks.f32[tg]
                rc.on("t", lambda t_, i, oc=oc, tg=tg, kc=kc, bank=bank: t_.matmul(
                    out=bank[:], lhsT=rqslab[:, kc, oc * 128:(oc + 1) * 128],
                    rhs=xnf[:, kc, tg * 512:(tg + 1) * 512],
                    start=(kc == 0), stop=(kc == 7)),
                    waits=([ln_done, B["dGA"].now()] + banks.bwait(bank)
                           if kc == 0 else []),
                    incs=([(sT, 1)] if kc == 7 and tg == 3 else []))
        for tg in range(4):
            bank = banks.f32[tg]
            rc.on("v", lambda v, i, oc=oc, tg=tg, bank=bank: v.tensor_scalar(
                out=rqb[:, oc, tg * 512:(tg + 1) * 512], in0=bank[:],
                scalar1=B["bias_pp"][:, BPP["l0_rq"] + oc:BPP["l0_rq"] + oc + 1],
                scalar2=None, op0=ALU.add),
                waits=([sT.now()] if tg == 0 else []), incs=[(sV, 1)])
            banks.bfree(bank, [sV.now()])
    rq_done = sV.now()

    def norm_cols(src_fn, n, base, width=512, wait0=()):
        for j in range(n):
            rc.on("s", lambda s, i, j=j: s.activation(
                out=B["sqscr"][:, 0:width], in_=src_fn(j), func=AF.Square),
                waits=list(wait0) + [sT.now(), sS.now()], incs=[(sS, 1)])
            bank = banks.f32[2 + j % 2]
            rc.on("t", lambda t_, i, bank=bank: t_.matmul(
                out=bank[0:2, 0:width], lhsT=B["headmask"][:],
                rhs=B["sqscr"][:, 0:width], start=True, stop=True),
                waits=[sS.now()] + banks.bwait(bank), incs=[(sT, 1)])
            rc.on("v", lambda v, i, j=j, bank=bank: v.tensor_reduce(
                out=B["nrm"][:, base + j:base + j + 1], in_=bank[0:2, 0:width],
                axis=AX.X, op=ALU.max), waits=[sT.now()], incs=[(sV, 1)])
            banks.bfree(bank, [sV.now()])

    def reduce_norm(base, n, dst):
        rc.on("v", lambda v, i: v.tensor_reduce(
            out=B["kn2a"][0:2, :], in_=B["nrm"][:, base:base + n], axis=AX.X,
            op=ALU.max), waits=[sV.now()], incs=[(sV, 1)])
        rc.on("t", lambda t_, i: t_.transpose(
            out=pfF[0:1, 0:2], in_=B["kn2a"][0:2, 0:1],
            identity=B["ident_f32"][0:2, 0:2]),
            waits=[sV.now()] + banks.bwait(pfF), incs=[(sT, 1)])
        rc.on("v", lambda v, i, dst=dst: v.tensor_reduce(
            out=dst, in_=pfF[0:1, 0:2], axis=AX.X, op=ALU.max),
            waits=[sT.now()], incs=[(sV, 1)])
        banks.bfree(pfF, [sV.now()])

    def make_neg_bound(qsrc, ksrc, dst):
        rc.on("v", lambda v, i, qsrc=qsrc, ksrc=ksrc: v.tensor_tensor(
            out=B["pp1"][0:1, :], in0=qsrc, in1=ksrc, op=ALU.mult),
            waits=[sV.now()], incs=[(sV, 1)])
        rc.on("s", lambda s, i: s.activation(
            out=B["pp1"][0:1, :], in_=B["pp1"][0:1, :], func=AF.Ln,
            bias=B["tiny"][0:1, :]), waits=[sV.now()], incs=[(sS, 1)])
        rc.on("s", lambda s, i: s.activation(
            out=B["pp2"][0:1, :], in_=B["pp1"][0:1, :], func=AF.Exp, scale=0.5),
            waits=[sS.now()], incs=[(sS, 1)])
        rc.on("v", lambda v, i: v.tensor_scalar(
            out=B["pp2f"][0:1, :], in0=B["pp2"][0:1, :], scalar1=-0.125,
            scalar2=None, op0=ALU.mult), waits=[sS.now()], incs=[(sV, 1)])
        rc.on("t", lambda t_, i: t_.matmul(
            out=pfF[:, 0:1], lhsT=B["ones_row_f32"][0:1, :], rhs=B["pp2f"][0:1, 0:1],
            start=True, stop=True),
            waits=[sV.now()] + banks.bwait(pfF), incs=[(sT, 1)])
        rc.on("s", lambda s, i, dst=dst: s.copy(out=dst, in_=pfF[:, 0:1]),
            waits=[sT.now(), sS.now()], incs=[(sS, 1)])
        banks.bfree(pfF, [sS.now()])

    norm_cols(lambda j: rqb[:, j // 4, (j % 4) * 512:(j % 4) * 512 + 512], 8, NRQ,
              wait0=[rq_done])
    reduce_norm(NRQ, 8, B["qn2"][0:1, :])
    norm_cols(lambda j: rkf[:, j, :], 2, NRK, width=64)
    reduce_norm(NRK, 2, B["kn2"][0:1, :])
    make_neg_bound(B["qn2"][0:1, :], B["kn2"][0:1, :], B["negM2"][:])

    rc.on("g", lambda g, i: g.dma_start(
        out=slab[:, 0, :, :], in_=B["dr"]["L0_attn_qkv_w"][:, 1024:2048]
        .rearrange("(c p) m -> p c m", p=128)),
        waits=[sT.now()], incs=[(B["dSL0"], 16)])
    for oc in range(8):
        for kc in range(8):
            for tg in range(4):
                bank = banks.f32[tg]
                rc.on("t", lambda t_, i, oc=oc, tg=tg, kc=kc, bank=bank: t_.matmul(
                    out=bank[:], lhsT=slab[:, 0, kc, oc * 128:(oc + 1) * 128],
                    rhs=xnf[:, kc, tg * 512:(tg + 1) * 512],
                    start=(kc == 0), stop=(kc == 7)),
                    waits=([ln_done, B["dSL0"].now()] + banks.bwait(bank)
                           if kc == 0 else []),
                    incs=([(sT, 1)] if kc == 7 and tg == 3 else []))
        for tg in range(4):
            bank = banks.f32[tg]
            rc.on("v", lambda v, i, oc=oc, tg=tg, bank=bank: v.tensor_scalar(
                out=kbuf[:, oc, tg * 512:(tg + 1) * 512], in0=bank[:],
                scalar1=B["bias_pp"][:, BPP["l0_k"] + oc:BPP["l0_k"] + oc + 1],
                scalar2=None, op0=ALU.add),
                waits=([sT.now()] if tg == 0 else []), incs=[(sV, 1)])
            banks.bfree(bank, [sV.now()])
    k_done = sV.now()
    norm_cols(lambda j: kbuf[:, j // 4, (j % 4) * 512:(j % 4) * 512 + 512], 32, NK,
              wait0=[k_done])
    reduce_norm(NK, 32, B["kn2"][0:1, :])

    for qg in range(4):
        rc.on("g", lambda g, i: g.dma_start(
            out=slab[:, 0, :, :], in_=B["dr"]["L0_attn_qkv_w"][:, 0:1024]
            .rearrange("(c p) m -> p c m", p=128)),
            waits=[sT.now()], incs=[(B["dSL0"], 16)])
        for oc in range(8):
            bank = banks.f32[oc % 2]
            for kc in range(8):
                rc.on("t", lambda t_, i, oc=oc, kc=kc, qg=qg, bank=bank: t_.matmul(
                    out=bank[:], lhsT=slab[:, 0, kc, oc * 128:(oc + 1) * 128],
                    rhs=xnf[:, kc, qg * 512:(qg + 1) * 512],
                    start=(kc == 0), stop=(kc == 7)),
                    waits=([B["dSL0"].now(), sV.now()] + banks.bwait(bank)
                           if kc == 0 else []),
                    incs=([(sT, 1)] if kc == 7 else []))
            rc.on("v", lambda v, i, oc=oc, bank=bank: v.tensor_scalar(
                out=qgb[:, oc, :], in0=bank[:],
                scalar1=B["bias_pp"][:, BPP["l0_q"] + oc:BPP["l0_q"] + oc + 1],
                scalar2=None, op0=ALU.add), waits=[sT.now()], incs=[(sV, 1)])
            banks.bfree(bank, [sV.now()])
        q_done = sV.now()
        norm_cols(lambda j: qgb[:, j, :], 8, NQ, wait0=[q_done])
        reduce_norm(NQ, 8, B["qn2"][0:1, :])
        make_neg_bound(B["qn2"][0:1, :], B["kn2"][0:1, :], B["negM"][:])
        negm_ready = sS.now()

        for kt in range(qg * 4, qg * 4 + 4):
            for og in range(2):
                bank = banks.f32[(kt * 2 + og) % 2]
                rc.on("t", lambda t_, i, og=og, bank=bank: t_.matmul(
                    out=bank[:], lhsT=B["ones_row_bf"][0:1, :],
                    rhs=B["bias_row"][0:1, BROW["l0_v"] + og * 512:BROW["l0_v"] + og * 512 + 512],
                    start=True, stop=False),
                    waits=[B["dSL1"].now(), sS.now()] + banks.bwait(bank), incs=[])
                for kc in range(8):
                    rc.on("t", lambda t_, i, kt=kt, og=og, kc=kc, bank=bank: t_.matmul(
                        out=bank[:], lhsT=xnf[:, kc, kt * 128:(kt + 1) * 128],
                        rhs=slab[:, 1, kc, og * 512:(og + 1) * 512],
                        start=False, stop=(kc == 7)),
                        incs=([(sT, 1)] if kc == 7 else []))
                rc.on("s", lambda s, i, kt=kt, og=og, bank=bank: s.copy(
                    out=vt6[:, kt % 5, og * 8:(og + 1) * 8, 0:64],
                    in_=bank[:].rearrange("p (h d) -> p h d", d=64)),
                    waits=[sT.now()], incs=[(sS, 1)])
                banks.bfree(bank, [sS.now()])
        v_ready = sS.now()

        lnorm_q = []

        def l0_norm_tail(item):
            h2, c2, r2, ctxb2 = item
            rc.on("t", lambda t_, i: t_.matmul(
                out=pfF[0:64, :], lhsT=B["ones_row_f32"][0:1, 0:64],
                rhs=B["linv"][0:1, :], start=True, stop=True),
                waits=[sV.now()] + banks.bwait(pfF), incs=[(sT, 1)])
            rc.on("s", lambda s, i: s.copy(out=B["bcs"][0:64, :], in_=pfF[0:64, :]),
                waits=[sT.now(), sS.now()], incs=[(sS, 1)])
            banks.bfree(pfF, [sS.now()])
            rc.on("v", lambda v, i, c=c2, r0=r2, ctxb2=ctxb2: v.tensor_tensor(
                out=localb[r0:r0 + 64, c, :], in0=ctxb2[0:64, 0:512],
                in1=B["bcs"][0:64, :], op=ALU.mult),
                waits=[sS.now()], incs=[(sV, 1)])
            banks.bfree(ctxb2, [sV.now()])

        for h in range(16):
            c = h // 2
            r0 = (h % 2) * 64
            ctxbank = banks.f32[2 + h % 2]
            kts = [kt for kt in range(qg * 4 - 1, qg * 4 + 4) if kt >= 0]
            pv_q = []
            for ki, kt in enumerate(kts):
                j = kt - qg * 4
                c0 = max(0, j * 128)
                c1 = min(512, (j + 2) * 128)
                sbank = banks.f32[kt % 2]
                rc.on("t", lambda t_, i, c=c, r0=r0, kt=kt, c0=c0, c1=c1,
                      sbank=sbank: t_.matmul(
                    out=sbank[:, c0:c1],
                    lhsT=kbuf[r0:r0 + 64, c, kt * 128:(kt + 1) * 128],
                    rhs=qgb[r0:r0 + 64, c, c0:c1], start=True, stop=True),
                    waits=[negm_ready, v_ready] + banks.bwait(sbank), incs=[(sT, 1)])
                rc.on("s", lambda s, i, kt=kt, c0=c0, c1=c1, sbank=sbank: s.activation(
                    out=B["ptile"][:, kt % 2, c0:c1], in_=sbank[:, c0:c1],
                    func=AF.Exp, bias=B["negM"][:], scale=0.125),
                    waits=[sT.now(), sS.now()], incs=[(sS, 1)])
                banks.bfree(sbank, [sS.now()])
                mspecs = []
                if j >= 0:
                    mspecs.append((j * 128, "mask_lt"))
                if (j + 2) * 128 <= 512 + 128 and (j + 1) * 128 < 512:
                    mspecs.append(((j + 1) * 128, "mask_ge"))
                for (mc0, mkey) in mspecs:
                    rc.on("v", lambda v, i, kt=kt, mc0=mc0, mkey=mkey: v.tensor_tensor(
                        out=B["ptile"][:, kt % 2, mc0:mc0 + 128],
                        in0=B["ptile"][:, kt % 2, mc0:mc0 + 128],
                        in1=B[mkey][:], op=ALU.mult),
                        waits=[sS.now()], incs=[(sV, 1)])
                pw = sV.now() if mspecs else sS.now()
                pv_q.append((kt, c0, c1, pw, ki))
                if ki == 1 and lnorm_q:
                    l0_norm_tail(lnorm_q.pop(0))
                if len(pv_q) > 1:
                    _emit_local_pv2(rc, B, banks, ctxbank, vt6, qg, h, pv_q.pop(0), sT)
            _emit_local_pv2(rc, B, banks, ctxbank, vt6, qg, h, pv_q.pop(0), sT)
            rc.on("v", lambda v, i, ctxbank=ctxbank: v.tensor_scalar(
                out=B["lrow"][0:1, :], in0=ctxbank[64:65, 0:512], scalar1=1e-30,
                scalar2=None, op0=ALU.max), waits=[sT.now(), sV.now()], incs=[(sV, 1)])
            rc.on("v", lambda v, i: v.reciprocal(
                out=B["linv"][0:1, :], in_=B["lrow"][0:1, :]),
                waits=[sV.now()], incs=[(sV, 1)])
            lnorm_q.append((h, c, r0, ctxbank))
        l0_norm_tail(lnorm_q.pop(0))

        for lh in range(4):
            lc = lh // 2
            lr = (lh % 2) * 64
            sbank = banks.f32[2 + lh % 2]
            rc.on("t", lambda t_, i, lc=lc, lr=lr, qg=qg, sbank=sbank: t_.matmul(
                out=sbank[0:64, :], lhsT=rkf[lr:lr + 64, lc, :],
                rhs=rqb[lr:lr + 64, lc, qg * 512:(qg + 1) * 512],
                start=True, stop=True),
                waits=[negm_ready] + banks.bwait(sbank), incs=[(sT, 1)])
            rc.on("s", lambda s, i, sbank=sbank: s.activation(
                out=B["lpt"][0:64, :], in_=sbank[0:64, :], func=AF.Exp,
                bias=B["negM2"][0:64, :], scale=0.125),
                waits=[sT.now(), sS.now(), sV.now()], incs=[(sS, 1)])
            banks.bfree(sbank, [sS.now()])
            rc.on("v", lambda v, i, qg=qg: v.tensor_tensor(
                out=B["lpt"][0:64, :], in0=B["lpt"][0:64, :],
                in1=B["mask_lat"][0:64, qg * 512:(qg + 1) * 512], op=ALU.mult),
                waits=[sS.now(), B["dCN"].now()], incs=[(sV, 1)])
            rc.on("t", lambda t_, i, lh=lh: t_.matmul(
                out=pfE[0:65, :], lhsT=rvt[0:64, lh, 0:65], rhs=B["lpt"][0:64, :],
                start=True, stop=True),
                waits=[sV.now()] + banks.bwait(pfE), incs=[(sT, 1)])
            rc.on("v", lambda v, i: v.tensor_scalar(
                out=B["lrow"][0:1, :], in0=pfE[64:65, 0:512], scalar1=1e-30,
                scalar2=None, op0=ALU.max), waits=[sT.now()], incs=[(sV, 1)])
            rc.on("v", lambda v, i: v.reciprocal(
                out=B["linv"][0:1, :], in_=B["lrow"][0:1, :]),
                waits=[sV.now()], incs=[(sV, 1)])
            rc.on("t", lambda t_, i: t_.matmul(
                out=pfF[0:64, :], lhsT=B["ones_row_f32"][0:1, 0:64],
                rhs=B["linv"][0:1, :], start=True, stop=True),
                waits=[sV.now()] + banks.bwait(pfF), incs=[(sT, 1)])
            rc.on("s", lambda s, i: s.copy(out=B["bcs"][0:64, :], in_=pfF[0:64, :]),
                waits=[sT.now(), sS.now()], incs=[(sS, 1)])
            banks.bfree(pfF, [sS.now()])
            rc.on("v", lambda v, i, lc=lc, lr=lr: v.tensor_tensor(
                out=latc[lr:lr + 64, lc, :], in0=pfE[0:64, 0:512],
                in1=B["bcs"][0:64, :], op=ALU.mult),
                waits=[sS.now()], incs=[(sV, 1)])
            banks.bfree(pfE, [sV.now()])
        lat_done = sV.now()

        for oc in range(8):
            bank = banks.f32[oc % 2]
            for kc in range(2):
                rc.on("t", lambda t_, i, oc=oc, kc=kc, bank=bank: t_.matmul(
                    out=bank[:], lhsT=B["routslab"][:, kc, oc * 128:(oc + 1) * 128],
                    rhs=latc[:, kc, :], start=(kc == 0), stop=(kc == 1)),
                    waits=([lat_done, B["dCB0"].now()] + banks.bwait(bank)
                           if kc == 0 else []),
                    incs=([(sT, 1)] if kc == 1 else []))
            rc.on("v", lambda v, i, oc=oc, bank=bank: v.tensor_scalar(
                out=remoteb[:, oc, :], in0=bank[:],
                scalar1=B["bias_pp"][:, BPP["l0_rout"] + oc:BPP["l0_rout"] + oc + 1],
                scalar2=None, op0=ALU.add), waits=[sT.now()], incs=[(sV, 1)])
            banks.bfree(bank, [sV.now()])
        rem_done = sV.now()

        for c in range(8):
            gb = banks.f32[5]
            rc.on("t", lambda t_, i, c=c, gb=gb: t_.matmul(
                out=gb[0:2, :], lhsT=B["gldiag"][:], rhs=localb[:, c, :],
                start=True, stop=False),
                waits=[rem_done, B["dCNG"].now()] + banks.bwait(gb), incs=[])
            rc.on("t", lambda t_, i, c=c, gb=gb: t_.matmul(
                out=gb[0:2, :], lhsT=B["grdiag"][:], rhs=remoteb[:, c, :],
                start=False, stop=True), incs=[(sT, 1)])
            rc.on("s", lambda s, i, gb=gb: s.activation(
                out=B["gsig"][0:2, :], in_=gb[0:2, :], func=AF.Exp, scale=-1.0),
                waits=[sT.now(), sV.now()], incs=[(sS, 1)])
            banks.bfree(gb, [sS.now()])
            rc.on("v", lambda v, i: v.tensor_scalar(
                out=B["gsig"][0:2, :], in0=B["gsig"][0:2, :], scalar1=1.0,
                scalar2=None, op0=ALU.add), waits=[sS.now()], incs=[(sV, 1)])
            rc.on("v", lambda v, i: v.reciprocal(
                out=B["gsig"][0:2, :], in_=B["gsig"][0:2, :]),
                waits=[sV.now()], incs=[(sV, 1)])
            gbc = banks.f32[5]
            rc.on("t", lambda t_, i, gbc=gbc: t_.matmul(
                out=gbc[:], lhsT=B["sel2"][:], rhs=B["gsig"][0:2, :],
                start=True, stop=True),
                waits=[sV.now()] + banks.bwait(gbc), incs=[(sT, 1)])
            rc.on("v", lambda v, i, c=c: v.tensor_tensor(
                out=B["scr1"][:], in0=localb[:, c, :], in1=remoteb[:, c, :],
                op=ALU.subtract), waits=[sT.now()], incs=[(sV, 1)])
            rc.on("v", lambda v, i, gbc=gbc: v.tensor_tensor(
                out=B["scr1"][:], in0=B["scr1"][:], in1=gbc[:], op=ALU.mult),
                waits=[sV.now()], incs=[(sV, 1)])
            banks.bfree(gbc, [sV.now()])
            rc.on("v", lambda v, i, c=c: v.tensor_tensor(
                out=localb[:, c, :], in0=B["scr1"][:], in1=remoteb[:, c, :],
                op=ALU.add), waits=[sV.now()], incs=[(sV, 1)])
        mix_done = sV.now()

        rc.on("g", lambda g, i: g.dma_start(
            out=slab[:, 0, :, :], in_=B["dr"]["L0_attn_out_w"]
            .rearrange("(c p) m -> p c m", p=128)),
            waits=[sT.now()], incs=[(B["dSL0"], 16)])
        for tl in range(4):
            tt = qg * 4 + tl
            xslot = tl % 2
            rc.on("y", lambda y, i, tt=tt, xslot=xslot: y.dma_start(
                out=B["xld2"][:, xslot, :], in_=xsrc_fn(i)[tt * 128:(tt + 1) * 128, :]),
                waits=B.pop(("xld2free", xslot), []), incs=[(B["dX2"][xslot], 16)])
            for og in range(2):
                bank = banks.f32[og % 2]
                rc.on("t", lambda t_, i, og=og, bank=bank: t_.matmul(
                    out=bank[:], lhsT=B["ones_row_bf"][0:1, :],
                    rhs=B["bias_row"][0:1, BROW["l0_out"] + og * 512:BROW["l0_out"] + og * 512 + 512],
                    start=True, stop=False),
                    waits=[mix_done, B["dSL0"].now()] + banks.bwait(bank), incs=[])
                for kc in range(8):
                    rc.on("t", lambda t_, i, tl=tl, og=og, kc=kc, bank=bank: t_.matmul(
                        out=bank[:], lhsT=localb[:, kc, tl * 128:(tl + 1) * 128],
                        rhs=slab[:, 0, kc, og * 512:(og + 1) * 512],
                        start=False, stop=(kc == 7)),
                        incs=([(sT, 1)] if kc == 7 else []))
                rc.on("v", lambda v, i, og=og, xslot=xslot, bank=bank: v.tensor_tensor(
                    out=B["yst"][:, xslot, og * 512:(og + 1) * 512], in0=bank[:],
                    in1=B["xld2"][:, xslot, og * 512:(og + 1) * 512], op=ALU.add),
                    waits=[sT.now(), B["dX2"][xslot].now()] + B.pop(("ystfree", xslot), []),
                    incs=[(sV, 1)])
                banks.bfree(bank, [sV.now()])
            rc.on("y", lambda y, i, tt=tt, xslot=xslot: y.dma_start(
                out=xdst_fn(i)[tt * 128:(tt + 1) * 128, :], in_=B["yst"][:, xslot, :]),
                waits=[sV.now()], incs=[(B["dST"][xslot], 16)])
            B[("xld2free", xslot)] = [sV.now()]
            B[("ystfree", xslot)] = [B["dST"][xslot].now()]


def _l0ffn_tq_flush(rc, B, banks, item, sT, sS, sV, sG):
    ig, tl, nic, hslot = item
    pb = banks.bf[tl % 2]
    for j in range(nic):
        rc.on("t", lambda t_, i, j=j, pb=pb, hslot=hslot: t_.transpose(
            out=pb[:, j * 128:(j + 1) * 128],
            in_=B["hq_t"][:, hslot, j * 128:(j + 1) * 128],
            identity=B["ident_bf"][:]),
            waits=([sG.now()] + banks.bwait(pb) if j == 0 else []),
            incs=([(sT, 1)] if j == nic - 1 else []))
    rc.on("s", lambda s, i, ig=ig, tl=tl, nic=nic, pb=pb: s.copy(
        out=B["hqf"][:, ig * 4:ig * 4 + nic, tl * 128:(tl + 1) * 128],
        in_=pb[:, 0:nic * 128].rearrange("p (c q) -> p c q", q=128)),
        waits=[sT.now()], incs=[(sS, 1)])
    banks.bfree(pb, [sS.now()])


def emit_phase_l0ffn(rc, B, xsrc_fn, xdst_fn):
    sV, sS, sT, sG = B["sV"], B["sS"], B["sT"], B["sG"]
    banks = B["banks"]
    xqf = B["xnf"]
    scr = B["scr"]

    lnrdy = []
    for t in range(16):
        emit_ln_tile(rc, B, xsrc_fn, t, t % 2, B["xnq"][:, t % 2, :])
        emit_quant_tile(rc, B, B["xnq"][:, t % 2, :], B["xqt"][:, t % 2, :], 1024,
                        src_wait=[sV.now()], dst_wait=[sT.now()], qsel=t % 2)
        emit_tile_to_feature(rc, B, B["xqt"][:, t % 2, :],
                             xqf[:, 0:8, t * 128:(t + 1) * 128], t % 2)
        B[("xldfree", t % 2)] = B[("xldfree", t % 2)] + [sS.now()]
        lnrdy.append(sS.now())
    ln_done = sS.now()

    for tg in range(4):
        units = [(ig, tl) for ig in range(6) for tl in range(4)]
        tq = []
        for ui, (ig, tl) in enumerate(units):
            w = 512 if ig < 5 else 256
            nic = w // 128
            if tl == 0:
                igs = ig % 2
                rc.on("y", lambda y, i, ig=ig, w=w, igs=igs: y.dma_start(
                    out=B["gslab"][:, igs, :, 0:w],
                    in_=scr["wp_l0g"][:, ig * 512:ig * 512 + w]
                    .rearrange("(c p) m -> p c m", p=128)),
                    waits=B.pop(("gsfree", igs), []),
                    incs=[(B["dGS" if igs == 0 else "dGS1"], 16)])
                rc.on("y", lambda y, i, ig=ig, w=w, igs=igs: y.dma_start(
                    out=B["uslab"][:, igs, :, 0:w],
                    in_=scr["wp_l0u"][:, ig * 512:ig * 512 + w]
                    .rearrange("(c p) m -> p c m", p=128)),
                    waits=B.pop(("usfree", igs), []),
                    incs=[(B["dUS" if igs == 0 else "dUS1"], 16)])
            igs = ig % 2
            tt = tg * 4 + tl
            gb = banks.f32[(ui % 2) * 2]
            ub = banks.f32[(ui % 2) * 2 + 1]
            for kc in range(8):
                rc.on("t", lambda t_, i, tt=tt, kc=kc, w=w, gb=gb, igs=igs: t_.matmul(
                    out=gb[:, 0:w], lhsT=xqf[:, kc, tt * 128:(tt + 1) * 128],
                    rhs=B["gslab"][:, igs, kc, 0:w], start=(kc == 0), stop=(kc == 7)),
                    waits=([B["dGS" if igs == 0 else "dGS1"].now(), lnrdy[tt]]
                           + banks.bwait(gb) if kc == 0 else []),
                    incs=([(sT, 1)] if kc == 7 else []))
            for kc in range(8):
                rc.on("t", lambda t_, i, tt=tt, kc=kc, w=w, ub=ub, igs=igs: t_.matmul(
                    out=ub[:, 0:w], lhsT=xqf[:, kc, tt * 128:(tt + 1) * 128],
                    rhs=B["uslab"][:, igs, kc, 0:w], start=(kc == 0), stop=(kc == 7)),
                    waits=([B["dUS" if igs == 0 else "dUS1"].now()]
                           + banks.bwait(ub) if kc == 0 else []),
                    incs=([(sT, 1)] if kc == 7 else []))
            if tl == 3:
                B[("gsfree", igs)] = [sT.now()]
                B[("usfree", igs)] = [sT.now()]
            if len(tq) > 2:
                _l0ffn_tq_flush(rc, B, banks, tq.pop(0), sT, sS, sV, sG)
            rc.on("s", lambda s, i, w=w, gb=gb, ui=ui: s.activation(
                out=B["hstage"][:, ui % 2, 0:w], in_=gb[:, 0:w], func=AF.Silu),
                waits=[sT.now(), sV.now()], incs=[(sS, 1)])
            banks.bfree(gb, [sS.now()])
            rc.on("v", lambda v, i, w=w, ub=ub, ui=ui: v.tensor_tensor(
                out=B["xnq"][:, ui % 2, 0:w], in0=B["hstage"][:, ui % 2, 0:w],
                in1=ub[:, 0:w], op=ALU.mult), waits=[sS.now()], incs=[(sV, 1)])
            banks.bfree(ub, [sV.now()])
            hslot = ui % 3
            emit_quant_tile(rc, B, B["xnq"][:, ui % 2, 0:w], B["hq_t"][:, hslot, 0:w],
                            w, src_wait=[sV.now()], dst_wait=[sT.now()], qsel=ui % 2)
            tq.append((ig, tl, nic, hslot))
        while tq:
            _l0ffn_tq_flush(rc, B, banks, tq.pop(0), sT, sS, sV, sG)
        hq_done = sS.now()

        dwn = ([(banks.f32[j], banks.f32[j][:]) for j in range(6)]
               + [(bb, bb[:].bitcast(F32)) for bb in banks.bf])
        for bko, _ in dwn:
            for wsp in banks.bwait(bko):
                rc.on("t", None, waits=[wsp])
        for og in range(2):
            rc.on("y", lambda y, i, og=og: y.dma_start(
                out=B["dslab"][:],
                in_=scr["wp_l0d"][:, og * 512:(og + 1) * 512]
                .rearrange("(c p) m -> p c m", p=128)),
                waits=B.pop(("dsfree2",), []), incs=[(B["dDS"], 16)])
            for tl in range(4):
                bk = dwn[tl * 2 + og][1]
                for kc in range(22):
                    rc.on("t", lambda t_, i, tl=tl, kc=kc, bk=bk: t_.matmul(
                        out=bk, lhsT=B["hqf"][:, kc, tl * 128:(tl + 1) * 128],
                        rhs=B["dslab"][:, kc, :], start=(kc == 0), stop=(kc == 21)),
                        waits=([hq_done, B["dDS"].now()] if kc == 0 else []),
                        incs=([(sT, 1)] if kc == 21 else []))
            B[("dsfree2",)] = [sT.now()]
        down_done = sT.now()
        for tl in range(4):
            ttg = tg * 4 + tl
            xslot = tl % 2
            rc.on("y", lambda y, i, ttg=ttg, xslot=xslot: y.dma_start(
                out=B["xld2"][:, xslot, :], in_=xsrc_fn(i)[ttg * 128:(ttg + 1) * 128, :]),
                waits=B.pop(("xld2free", xslot), []), incs=[(B["dX2"][xslot], 16)])
            for og in range(2):
                bko, bk = dwn[tl * 2 + og]
                rc.on("v", lambda v, i, og=og, xslot=xslot, bk=bk: v.tensor_tensor(
                    out=B["yst"][:, xslot, og * 512:(og + 1) * 512], in0=bk,
                    in1=B["xld2"][:, xslot, og * 512:(og + 1) * 512], op=ALU.add),
                    waits=[down_done, B["dX2"][xslot].now()] + B.pop(("ystfree", xslot), []),
                    incs=[(sV, 1)])
                banks.bfree(bko, [sV.now()])
            rc.on("y", lambda y, i, ttg=ttg, xslot=xslot: y.dma_start(
                out=xdst_fn(i)[ttg * 128:(ttg + 1) * 128, :], in_=B["yst"][:, xslot, :]),
                waits=[sV.now()], incs=[(B["dST"][xslot], 16)])
            B[("xld2free", xslot)] = [sV.now()]
            B[("ystfree", xslot)] = [B["dST"][xslot].now()]


def _in_maps(inputs, phases):
    consts = host_consts()
    bias_pp = _bias_pp_host(inputs)
    bias_row = _bias_row_host(inputs)
    x_full = np.ascontiguousarray(np.asarray(inputs["L0_attn_x"], dtype=np.float32))
    maps = []
    for c in range(8):
        m = {"x": x_full[c * NSEQ:(c + 1) * NSEQ]}
        for k, v in inputs.items():
            if k != "L0_attn_x":
                m[k] = np.ascontiguousarray(np.asarray(v, dtype=np.float32))
        for k, v in consts.items():
            m[k] = v
        m["c_bias_pp"] = bias_pp
        m["c_bias_row"] = bias_row
        maps.append(m)
    return maps


_PHASES_ALL = ["l0attn", "l0ffn", "l1attn", "l1ffn"]


def kernel(**inputs):
    nc = build_program(_PHASES_ALL)
    maps = _in_maps(inputs, _PHASES_ALL)
    res = run_bass_kernel_spmd(nc, maps, core_ids=list(range(8)))
    return np.concatenate([r["y"] for r in res.results], axis=0)
